# version 29
# baseline (speedup 1.0000x reference)
import math
import numpy as np
import ml_dtypes
from contextlib import ExitStack
import concourse.bass as bass
import concourse.mybir as mybir
from concourse.bass_utils import run_bass_kernel_spmd

F32 = mybir.dt.float32
BF16 = mybir.dt.bfloat16
AF = mybir.ActivationFunctionType
ALU = mybir.AluOpType
AX = mybir.AxisListType

MAGIC = float(np.float32(1.5 * 2 ** 23))
EPS = 1e-5
QMAX = 127.0
G8 = [list(range(8))]
G4 = [[0, 1, 2, 3], [4, 5, 6, 7]]

H = 1024
HP = 8
TME = 2048
IT = 22
DIN = 512
DP = 4
NST = 8
TB = 8192
CK = 1024
NCK = TB // CK


class B:
    DEBUG_TAPS = False
    TRUNCATE = False
    P0_QUANT = True
    lists = None

    def sem(self, name):
        h = self.outer.enter_context(self.nc.semaphore(name))
        self.sems[name] = h
        self.cnt[name] = 0

    def begin(self):
        if self.lists is None:
            self.lists = {k: [] for k in ("sync", "gpsimd", "vector", "scalar", "tensor")}

    def drive(self, *gens):
        gens = list(gens)
        while gens:
            for g in gens[:]:
                try:
                    next(g)
                except StopIteration:
                    gens.remove(g)

    def fence(self, sems):
        for eng in ("sync", "gpsimd", "vector", "scalar", "tensor"):
            self.op(eng, lambda e: None, waits=[(x, 1) for x in sems])

    def op(self, eng, emit, waits=(), inc=None, by=1):
        waits = tuple((s, self.cnt[s] if s in ("sLD", "sGD", "sSD", "sWQ") else v)
                      for (s, v) in waits if v)

        def closure(e, emit=emit, waits=waits, inc=inc, by=by):
            for s, v in waits:
                e.wait_ge(self.sems[s], v)
            inst = emit(e)
            if inc is not None and inst is not None:
                inst.then_inc(self.sems[inc], by)
        self.lists[eng].append(closure)
        if inc is not None:
            self.cnt[inc] += by
            return self.cnt[inc]
        return None

    def dma(self, eng, out, in_, waits=(), sem=None, **kw):
        if sem is None:
            sem = {"sync": "sLD", "gpsimd": "sGD", "scalar": "sSD"}[eng]
        return self.op(eng, lambda e: e.dma_start(out=out, in_=in_, **kw), waits, sem, 16)

    def flush(self):
        with self.nc.Block() as blk:
            for eng in ("sync", "gpsimd", "vector", "scalar", "tensor"):
                ops = self.lists[eng]
                if ops:
                    getattr(blk, eng)(lambda e, ops=tuple(ops): [f(e) for f in ops])
        self.lists = None
        self.nc.all_engine_barrier()

    def sb(self, stack, name, shape, dt):
        return stack.enter_context(self.nc.sbuf_tensor(name, shape, dt))

    def build(self):
        self.nc = bass.Bass("TRN2", target_bir_lowering=False, debug=False)
        nc = self.nc
        self.cnt = {}
        self.sems = {}
        P = {}

        def par(name, shape, dt=F32, out=False):
            P[name] = nc.declare_dram_parameter(name, shape, dt, isOutput=out)

        par("x0", [TME, H])
        par("wg", [2816, H]); par("wu", [2816, H]); par("wd", [H, 2816])
        par("clip2", [1, 2])
        par("ipwT", [H, 1024])
        par("cw4", [128, 16]); par("cbD", [128, 8]); par("dtb4", [128, 4])
        par("alog", [128, 32]); par("uT", [8, 512]); par("vprm", [8, 8])
        par("lrs1", [1, 1])
        par("xpwT", [512, 80]); par("dtwT", [64, 512]); par("opwT", [512, 1024])
        par("idf", [128, 128])
        par("out", [TME, H], out=True)

        D = {}

        def dram(name, shape, dt):
            D[name] = nc.dram_tensor(name, shape, dt)

        dram("wgt_q", [H, 2816], BF16)
        dram("wut_q", [H, 2816], BF16)
        dram("wdt_q", [2816, H], BF16)
        dram("hpre_sp", [2816, TME], F32)
        dram("hq_sp", [2816, TME], BF16)
        dram("x1_sp", [H, TME], F32)
        dram("z_sp", [DIN, TB], BF16)
        dram("u_sp", [DIN, TB], BF16)
        dram("mx_b", [1, 8], F32); dram("mx_o", [1, 8], F32)
        dram("xn2_b", [H, TME], BF16); dram("xn2_o", [4 * H, TME], BF16)
        dram("xdb_b", [80, TB], F32); dram("xdb_o", [80, TB], F32)
        dram("prt_b", [4 * H, TME], BF16); dram("prt_o", [H, TME], BF16)
        self.P, self.D = P, D

        self.outer = ExitStack()
        with self.outer:
            o = self.outer
            for s in ("sLD", "sGD", "sSD", "sCC", "sV", "sS", "sT", "sWQ"):
                self.sem(s)
            sb = self.sb
            self.A_sb = sb(o, "A_sb", [128, 32], F32)
            self.cbD_sb = sb(o, "cbD_sb", [128, 8], F32)
            self.dtb_sb = sb(o, "dtb_sb", [128, 4], F32)
            self.carry = sb(o, "carry", [128, 32], F32)
            self.onesf = sb(o, "onesf", [1, 128], F32)
            self.onesb = sb(o, "onesb", [1, 128], BF16)
            self.onescol = sb(o, "onescol", [128, 1], BF16)
            self.idf_sb = sb(o, "idf_sb", [128, 128], F32)
            self.idb_sb = sb(o, "idb_sb", [128, 128], BF16)
            self.DGT = sb(o, "DGT", [128, 16 * 128], BF16)
            self.sg_c = sb(o, "sg_c", [128, IT], F32)
            self.su_c = sb(o, "su_c", [128, IT], F32)
            self.sd_c = sb(o, "sd_c", [128, HP], F32)
            self.sgx = sb(o, "sgx", [128, IT], F32)
            self.sux = sb(o, "sux", [128, IT], F32)
            self.sdx = sb(o, "sdx", [128, HP], F32)
            self.sig2 = sb(o, "sig2", [1, 2], F32)
            self.gmax = sb(o, "gmax", [1, 1], F32)
            self.rowA = sb(o, "rowA", [1, TME], F32)
            self.rowB = sb(o, "rowB", [1, TME], F32)
            self.rowC = sb(o, "rowC", [1, TME], F32)
            self.col0 = sb(o, "col0", [128, 1], F32)
            self.col1 = sb(o, "col1", [128, 1], F32)
            self.rmax = sb(o, "rmax", [128, 1], F32)
            self.mcol = sb(o, "mcol", [128, 1], F32)
            self.sxc = sb(o, "sxc", [128, 1], F32)
            self.ivc = sb(o, "ivc", [128, 1], F32)
            self.mxw = sb(o, "mxw", [128, 96], F32)
            self.epsc = sb(o, "epsc", [1, 1], F32)
            self.xpw_sb = [sb(o, f"xpw{p}", [128, 80], BF16) for p in range(DP)]
            self.dtw_sb = sb(o, "dtw_sb", [64, 512], BF16)
            self.clip_sb = sb(o, "clip_sb", [1, 2], F32)
            self.p0 = o.enter_context(nc.psum_tensor("p0", [128, 512], F32))
            self.p1 = o.enter_context(nc.psum_tensor("p1", [128, 512], F32))
            self.p2 = o.enter_context(nc.psum_tensor("p2", [128, 512], F32))
            self.p3 = o.enter_context(nc.psum_tensor("p3", [128, 512], F32))
            self.pw = o.enter_context(nc.psum_tensor("pw", [128, 1024], F32))
            self.ptb0 = o.enter_context(nc.psum_tensor("ptb0", [128, 512], BF16))
            self.ptb1 = o.enter_context(nc.psum_tensor("ptb1", [128, 512], BF16))

            ph0 = ExitStack()
            self.begin()
            qg = self.phase0(ph0)
            with ExitStack() as ph:
                self.XR = [sb(ph, f"XR{p}", [128, TME], F32) for p in range(HP)]
                import itertools
                fstack0 = ExitStack()
                XQ0 = [self.sb(fstack0, f"XQ0_{p}", [128, TME], BF16)
                       for p in range(HP)]
                ag = itertools.chain(self.phase_load_x0(),
                                     self.ffn_stageA_gen(0, XQ0))
                self.drive(qg, ag)
                self.flush()
                ph0.close()
                self.ffn_rest(0, XQ0, fstack0)
                self.phase2a()
            if False:
                pass
                if self.DEBUG_TAPS:
                    P["tap_xr0"] = nc.declare_dram_parameter("tap_xr0", [H, TME], F32,
                                                             isOutput=True)
                    P["tap_idf"] = nc.declare_dram_parameter("tap_idf", [128, 128], F32,
                                                             isOutput=True)
                    P["tap_idb"] = nc.declare_dram_parameter("tap_idb", [128, 128], BF16,
                                                             isOutput=True)
                    self.begin()
                    fin = 0
                    for p in range(HP):
                        fin = self.dma("sync", P["tap_xr0"].ap()[128 * p:128 * (p + 1), :],
                                       self.XR[p][:])
                    fin = self.dma("sync", P["tap_idf"].ap(), self.idf_sb[:])
                    fin = self.dma("sync", P["tap_idb"].ap(), self.idb_sb[:])
                    self.op("sync", lambda e: None, waits=[("sLD", fin)])
                    self.flush()
                if self.TRUNCATE:
                    return nc
                self.ffn(app=0)
                self.phase2a()
            with ExitStack() as ph:
                self.yall = [sb(ph, f"yall{p}", [128, TB], BF16) for p in range(DP)]
                self.phase2b()
                self.phase2c()
                self.phase2d()
            with ExitStack() as ph:
                self.XR = [sb(ph, f"XS{p}", [128, TME], F32) for p in range(HP)]
                self.phase3_resid()
                self.ffn(app=1)
                self.phase4_out()
            if self.DEBUG_TAPS:
                taps = [("wgt_q", [H, 2816], BF16), ("wdt_q", [2816, H], BF16),
                        ("x1_sp", [H, TME], F32), ("xn2_o", [4 * H, TME], BF16),
                        ("xdb_o", [80, TB], F32), ("u_sp", [DIN, TB], BF16),
                        ("z_sp", [DIN, TB], BF16), ("prt_o", [H, TME], BF16),
                        ("hpre_sp", [2816, TME], F32), ("hq_sp", [2816, TME], BF16),
                        ("xdb_b", [80, TB], F32), ("prt_b", [4 * H, TME], BF16)]
                for nm, shape, dt in taps:
                    P["tap_" + nm] = nc.declare_dram_parameter(
                        "tap_" + nm, shape, dt, isOutput=True)
                self.begin()
                fin = 0
                for nm, shape, dt in taps:
                    fin = self.dma("sync", P["tap_" + nm].ap(), D[nm].ap())
                if "y_tap" in D:
                    fin = self.dma("sync", P["tap_y"].ap(), D["y_tap"].ap())
                self.op("sync", lambda e: None, waits=[("sLD", fin)])
                self.flush()
        return nc

    def phase0(self, ph):
        nc, P, D = self.nc, self.P, self.D
        op, dma = self.op, self.dma
        C = self.cnt
        if True:
            sb = self.sb
            wbuf = [sb(ph, f"w0b{i}", [128, 2816], F32) for i in range(2)]
            qbf = [sb(ph, f"q0b{i}", [128, 2816], BF16) for i in range(2)]
            qc0 = sb(ph, "qc0", [128, 1], F32)
            qc1 = sb(ph, "qc1", [128, 1], F32)
            tbuf = [sb(ph, f"t0b{i}", [128, 2816], BF16) for i in range(2)]
            alog_sb = sb(ph, "alog_sb", [128, 32], F32)
            expa = sb(ph, "expa", [128, 32], F32)
            uT_sb = sb(ph, "uT_sb", [8, 512], F32)
            v_sb = sb(ph, "v_sb", [8, 8], F32)
            lrs_sb = sb(ph, "lrs_sb", [1, 1], F32)
            ccol = sb(ph, "ccol", [128, 1], F32)
            uvev = sb(ph, "uvev", [128, 32], F32)
            iop = sb(ph, "iop", [128, 128], F32)
            iof = sb(ph, "iof", [128, 128], F32)
            mask = sb(ph, "mask", [128, 128], F32)
            cw_sb = sb(ph, "cw_sb", [128, 16], F32)

            for name, dst in [("alog", alog_sb), ("uT", uT_sb), ("vprm", v_sb),
                              ("lrs1", lrs_sb), ("cw4", cw_sb), ("cbD", self.cbD_sb),
                              ("dtb4", self.dtb_sb), ("idf", self.idf_sb),
                              ("clip2", self.clip_sb)]:
                dma("sync", dst[:], P[name].ap())
            ld0 = C["sLD"]
            dma("gpsimd", self.dtw_sb[:], P["dtwT"].ap())
            for p in range(DP):
                dma("gpsimd", self.xpw_sb[p][:], P["xpwT"].ap()[128 * p:128 * (p + 1), :])
            op("vector", lambda e: e.memset(self.onesf[:], 1.0), inc="sV")
            op("vector", lambda e: e.memset(self.onesb[:], 1.0), inc="sV")
            op("vector", lambda e: e.memset(self.onescol[:], 1.0), inc="sV")
            op("vector", lambda e: e.memset(self.carry[:], 0.0), inc="sV")
            op("vector", lambda e: e.memset(self.mcol[:], MAGIC), inc="sV")
            op("vector", lambda e: e.memset(self.epsc[:], EPS), inc="sV")
            vOnes = C["sV"]
            op("vector", lambda e: e.tensor_copy(self.idb_sb[:], self.idf_sb[:]),
               waits=[("sLD", ld0)], inc="sV")
            vIdb = C["sV"]
            op("gpsimd", lambda e: e.iota(iop[:], [[0, 128]], channel_multiplier=1,
                                          allow_small_or_imprecise_dtypes=True), inc="sGD", by=1)
            op("gpsimd", lambda e: e.iota(iof[:], [[1, 128]], channel_multiplier=0,
                                          allow_small_or_imprecise_dtypes=True), inc="sGD", by=1)
            gIota = C["sGD"]
            op("vector", lambda e: e.tensor_tensor(mask[:], iop[:], iof[:], ALU.is_equal),
               waits=[("sGD", gIota)], inc="sV")
            vMask = C["sV"]
            for j in range(16):
                op("vector", lambda e, j=j: e.tensor_scalar(
                    self.DGT[:, j * 128:(j + 1) * 128], mask[:], cw_sb[:, j:j + 1], None,
                    ALU.mult), waits=[("sV", vMask), ("sLD", ld0)], inc="sV")
            op("scalar", lambda e: e.activation(self.sig2[:], self.clip_sb[:], AF.Sigmoid),
               waits=[("sLD", ld0)], inc="sS")
            op("scalar", lambda e: e.activation(expa[:], alog_sb[:], AF.Exp),
               waits=[("sLD", ld0)], inc="sS")
            tC = op("tensor", lambda e: e.matmul(self.p2[0:128, 0:1], self.onesf[0:1, :],
                                                 lrs_sb[0:1, 0:1], start=True, stop=True),
                    waits=[("sLD", ld0), ("sV", vOnes)], inc="sT")
            sC = op("scalar", lambda e: e.activation(ccol[:], self.p2[0:128, 0:1], AF.Identity,
                                                     scale=1.0 / math.sqrt(8.0)),
                    waits=[("sT", tC)], inc="sS")
            for p in range(DP):
                tUV = op("tensor", lambda e, p=p: e.matmul(
                    self.p3[0:128, 0:8], uT_sb[0:8, 128 * p:128 * (p + 1)], v_sb[0:8, 0:8],
                    start=True, stop=True), waits=[("sLD", ld0), ("sS", sC)], inc="sT")
                op("scalar", lambda e, p=p: e.activation(
                    uvev[:, 8 * p:8 * p + 8], self.p3[0:128, 0:8], AF.Identity),
                    waits=[("sT", tUV)], inc="sS")
            sUV = C["sS"]
            op("vector", lambda e: e.scalar_tensor_tensor(
                self.A_sb[:], uvev[:], ccol[:, 0:1], expa[:], ALU.mult, ALU.subtract),
               waits=[("sS", sUV)], inc="sV")
            yield

            specs = [("wg", self.sg_c, D["wgt_q"], 22, H),
                     ("wu", self.su_c, D["wut_q"], 22, H),
                     ("wd", self.sd_c, D["wdt_q"], HP, 2816)]
            if not self.P0_QUANT:
                specs = []
            ptbs = [self.ptb0, self.ptb1]
            lastq = [0, 0]
            slot_ev = [0, 0]
            lastrd = [0, 0]
            prevrd = 0
            for wname, scol, qdst, ntile, CW in specs:
                for i in range(ntile):
                    b = i % 2
                    dL = dma("sync" if b == 0 else "scalar", wbuf[b][:, 0:CW],
                             P[wname].ap()[128 * i:128 * (i + 1), :],
                             waits=[("sV", lastrd[b])], sem="sWQ")
                    vr1 = op("vector", lambda e, b=b, CW=CW: e.tensor_reduce(
                        qc1[:], wbuf[b][:, 0:CW], AX.X, ALU.max,
                        apply_absolute_value=True),
                        waits=[("sWQ", dL), ("sS", prevrd)], inc="sV")
                    vr2 = op("vector", lambda e: e.reciprocal(qc0[:], qc1[:]),
                             waits=[("sV", vr1)], inc="sV")
                    vr3 = op("vector", lambda e: e.tensor_scalar(
                        qc0[:], qc0[:], QMAX, None, ALU.mult),
                        waits=[("sV", vr2)], inc="sV")
                    vI = op("vector", lambda e, scol=scol, i=i: e.tensor_scalar(
                        scol[:, i:i + 1], qc1[:], 1.0 / QMAX, None, ALU.mult),
                        waits=[("sV", vr3)], inc="sV")
                    sR = op("scalar", lambda e, b=b, CW=CW: e.activation(
                        wbuf[b][:, 0:CW], wbuf[b][:, 0:CW], AF.Identity,
                        scale=qc0[:, 0:1], bias=self.mcol[:, 0:1]), waits=[("sV", vI)], inc="sS")
                    prevrd = sR
                    vQ = op("vector", lambda e, b=b, CW=CW: e.tensor_scalar(
                        qbf[b][:, 0:CW], wbuf[b][:, 0:CW], MAGIC, None, ALU.subtract),
                        waits=[("sS", sR), ("sWQ", lastq[b])], inc="sV")
                    lastrd[b] = vQ
                    nblk = CW // 128
                    for cb in range(nblk):
                        sl = cb % 2
                        tT = op("tensor", lambda e, b=b, cb=cb, sl=sl: e.transpose(
                            ptbs[sl][:, 0:128], qbf[b][:, 128 * cb:128 * (cb + 1)],
                            self.idb_sb[:]),
                            waits=[("sV", vQ), ("sV", vIdb), ("sS", slot_ev[sl])], inc="sT")
                        slot_ev[sl] = op("scalar", lambda e, b=b, cb=cb, sl=sl: e.activation(
                            tbuf[b][:, 128 * cb:128 * (cb + 1)], ptbs[sl][:, 0:128],
                            AF.Identity),
                            waits=[("sT", tT)] + ([("sWQ", lastq[b])] if cb == 0 else []),
                            inc="sS")
                    CB = 2
                    for c0 in range(0, nblk, CB):
                        cn = min(CB, nblk - c0)
                        src = tbuf[b][:, 128 * c0:128 * (c0 + cn)].rearrange(
                            "p (c l) -> p c l", l=128)
                        dst = bass.AP(qdst, 128 * i + 128 * c0 * qdst.shape[1],
                                      [[qdst.shape[1], 128], [128 * qdst.shape[1], cn],
                                       [1, 128]])
                        lastq[b] = dma("sync", dst, src, sem="sWQ",
                                       waits=[("sS", slot_ev[0]), ("sS", slot_ev[1])])
                    yield

    def phase_load_x0(self):
        P = self.P
        op, dma = self.op, self.dma
        with ExitStack() as ph:
            xbuf = [self.sb(ph, f"x0b{i}", [128, H], F32) for i in range(2)]
            slots = [self.p0, self.p1]
            slot_ev = [0, 0]
            lastuse = [0, 0]
            for tt in range(16):
                yield
                b = tt % 2
                dL = dma("sync" if b == 0 else "scalar",
                         xbuf[b][:], P["x0"].ap()[128 * tt:128 * (tt + 1), :],
                         waits=[("sS", lastuse[b])], sem="sLD")
                for hp in range(HP):
                    sl = (tt * HP + hp) % 2
                    tT = op("tensor", lambda e, b=b, hp=hp, sl=sl: e.transpose(
                        slots[sl][:, 0:128], xbuf[b][:, 128 * hp:128 * (hp + 1)],
                        self.idf_sb[:]), waits=[("sLD", dL), ("sS", slot_ev[sl])], inc="sT")
                    slot_ev[sl] = op("scalar", lambda e, hp=hp, tt=tt, sl=sl: e.activation(
                        self.XR[hp][:, 128 * tt:128 * (tt + 1)], slots[sl][:, 0:128],
                        AF.Identity), waits=[("sT", tT)], inc="sS")
                lastuse[b] = slot_ev[(tt * HP + HP - 1) % 2]
            self._xr_ready = self.cnt["sS"]

    _stats_n = 0

    def stats_rows(self, X):
        op = self.op
        C = self.cnt
        B._stats_n += 1
        sn = B._stats_n
        with ExitStack() as ph:
            xb = [self.sb(ph, f"stb{sn}_{i}", [128, TME], BF16) for i in range(2)]
            lastmm = [0, 0]
            xrdy = getattr(self, "_xr_ready", 0)
            for hp in range(HP):
                b = hp % 2
                vCp = op("vector", lambda e, hp=hp, b=b: e.tensor_copy(xb[b][:], X[hp][:]),
                         waits=[("sT", lastmm[b]), ("sS", xrdy)], inc="sV")
                for ts in range(4):
                    pt = [self.p0, self.p1, self.p2, self.p3][ts]
                    lastmm[b] = op("tensor", lambda e, b=b, ts=ts, hp=hp, pt=pt: e.matmul(
                        pt[0:1, 0:512], self.onescol[:, 0:1],
                        xb[b][:, 512 * ts:512 * (ts + 1)],
                        start=(hp == 0), stop=(hp == HP - 1), skip_group_check=True),
                        waits=[("sV", vCp)], inc="sT")
            tSum = C["sT"]
            for ts in range(4):
                pt = [self.p0, self.p1, self.p2, self.p3][ts]
                op("scalar", lambda e, ts=ts, pt=pt: e.activation(
                    self.rowA[0:1, 512 * ts:512 * (ts + 1)], pt[0:1, 0:512],
                    AF.Identity, scale=1.0 / H), waits=[("sT", tSum)], inc="sS")
            sM = C["sS"]
            lastmm = [0, 0]
            for hp in range(HP):
                b = hp % 2
                sQp = op("scalar", lambda e, hp=hp, b=b: e.activation(
                    xb[b][:], X[hp][:], AF.Square),
                    waits=[("sM_dummy", 0), ("sS", sM if hp < 2 else 0), ("sT", lastmm[b])],
                    inc="sS")
                for ts in range(4):
                    pt = [self.p0, self.p1, self.p2, self.p3][ts]
                    lastmm[b] = op("tensor", lambda e, b=b, ts=ts, hp=hp, pt=pt: e.matmul(
                        pt[0:1, 0:512], self.onescol[:, 0:1],
                        xb[b][:, 512 * ts:512 * (ts + 1)],
                        start=(hp == 0), stop=(hp == HP - 1), skip_group_check=True),
                        waits=[("sS", sQp)], inc="sT")
            tSq = C["sT"]
            for ts in range(4):
                pt = [self.p0, self.p1, self.p2, self.p3][ts]
                op("scalar", lambda e, ts=ts, pt=pt: e.activation(
                    self.rowB[0:1, 512 * ts:512 * (ts + 1)], pt[0:1, 0:512],
                    AF.Identity, scale=1.0 / H), waits=[("sT", tSq)], inc="sS")
            sQ = C["sS"]
            v1 = op("vector", lambda e: e.tensor_tensor(
                self.rowC[0:1, :], self.rowA[0:1, :], self.rowA[0:1, :], ALU.mult),
                waits=[("sS", sQ)], inc="sV")
            v2 = op("vector", lambda e: e.tensor_tensor(
                self.rowB[0:1, :], self.rowB[0:1, :], self.rowC[0:1, :], ALU.subtract),
                inc="sV")
            op("scalar", lambda e: e.activation(self.rowB[0:1, :], self.rowB[0:1, :],
                                                AF.Ln, bias=self.epsc[0:1, 0:1]), waits=[("sV", v2)], inc="sS")
            sR = op("scalar", lambda e: e.activation(self.rowC[0:1, :], self.rowB[0:1, :],
                                                     AF.Exp, scale=-0.5), inc="sS")
            return sR

    def bcast_ts(self, ts, waits):
        op = self.op
        op("tensor", lambda e, ts=ts: e.matmul(
            self.p2[:, 0:512], self.onesf[0:1, :], self.rowA[0:1, 512 * ts:512 * (ts + 1)],
            start=True, stop=True), waits=waits, inc="sT")
        return op("tensor", lambda e, ts=ts: e.matmul(
            self.p3[:, 0:512], self.onesf[0:1, :], self.rowC[0:1, 512 * ts:512 * (ts + 1)],
            start=True, stop=True), inc="sT")

    def global_max_col(self, vReady):
        op, dma = self.op, self.dma
        tT = op("tensor", lambda e: e.transpose(
            self.p2[0:1, 0:128], self.rmax[0:128, 0:1], self.idf_sb[:]),
            waits=[("sV", vReady)], inc="sT")
        vG = op("vector", lambda e: e.tensor_reduce(
            self.gmax[0:1, 0:1], self.p2[0:1, 0:128], AX.X, ALU.max),
            waits=[("sT", tT)], inc="sV")
        dW = dma("sync", self.D["mx_b"].ap()[0:1, 0:1], self.gmax[0:1, 0:1],
                 waits=[("sV", vG)])
        cc = op("gpsimd", lambda e: e.collective_compute(
            "AllReduce", ALU.max, replica_groups=G8,
            ins=[self.D["mx_b"].ap()], outs=[self.D["mx_o"].ap()]),
            waits=[("sLD", dW)], inc="sCC", by=1)
        dR = dma("sync", self.gmax[0:1, 0:1], self.D["mx_o"].ap()[0:1, 0:1],
                 waits=[("sCC", cc)])
        return dR

    def ffn(self, app):
        fstack = ExitStack()
        XQ = [self.sb(fstack, f"XQ{app}_{p}", [128, TME], BF16) for p in range(HP)]
        self.begin()
        self.drive(self.ffn_stageA_gen(app, XQ))
        self.flush()
        self.ffn_rest(app, XQ, fstack)

    def ffn_stageA_gen(self, app, XQ):
        nc, P, D = self.nc, self.P, self.D
        op, dma = self.op, self.dma
        C = self.cnt
        if True:
            with ExitStack() as ph:
                xt = [self.sb(ph, f"fxt{app}_{i}", [128, 512], F32) for i in range(2)]
                qt = [self.sb(ph, f"fqt{app}_{i}", [128, 512], F32) for i in range(2)]
                yield
                sR = self.stats_rows(self.XR)
                yield
                vPrev = 0
                for ts in range(4):
                    yield
                    tB = self.bcast_ts(ts, [("sS", sR), ("sV", vPrev)])
                    for hp in range(HP):
                        b = hp % 2
                        op("vector", lambda e, ts=ts, hp=hp, b=b: e.tensor_tensor(
                            xt[b][:], self.XR[hp][:, 512 * ts:512 * (ts + 1)],
                            self.p2[:, 0:512], ALU.subtract),
                            waits=[("sT", tB)] if hp == 0 else [], inc="sV")
                        op("vector", lambda e, b=b: e.tensor_tensor(
                            xt[b][:], xt[b][:], self.p3[:, 0:512], ALU.mult), inc="sV")
                        op("vector", lambda e, b=b, ts=ts, hp=hp: e.tensor_reduce(
                            self.mxw[:, ts * 8 + hp:ts * 8 + hp + 1], xt[b][:], AX.X,
                            ALU.max, apply_absolute_value=True), inc="sV")
                    vPrev = C["sV"]
                vRm = op("vector", lambda e: e.tensor_reduce(
                    self.rmax[:], self.mxw[:, 0:32], AX.X, ALU.max),
                    waits=[("sV", vPrev)], inc="sV")
                dR = self.global_max_col(vRm)
                v1 = op("vector", lambda e: e.tensor_tensor(
                    self.gmax[0:1, 0:1], self.gmax[0:1, 0:1], self.sig2[0:1, 0:1], ALU.mult),
                    waits=[("sLD", dR)], inc="sV")
                tBc = op("tensor", lambda e: e.matmul(
                    self.p2[0:128, 0:1], self.onesf[0:1, :], self.gmax[0:1, 0:1],
                    start=True, stop=True), waits=[("sV", v1)], inc="sT")
                s1 = op("scalar", lambda e: e.activation(
                    self.sxc[:], self.p2[0:128, 0:1], AF.Identity, scale=1.0 / QMAX),
                    waits=[("sT", tBc)], inc="sS")
                v2 = op("vector", lambda e: e.reciprocal(self.ivc[:], self.sxc[:]),
                        waits=[("sS", s1)], inc="sV")
                lastq = [0, 0]
                for ts in range(4):
                    yield
                    tB = self.bcast_ts(ts, [("sV", C["sV"])])
                    for hp in range(HP):
                        b = hp % 2
                        op("vector", lambda e, ts=ts, hp=hp, b=b: e.tensor_tensor(
                            xt[b][:], self.XR[hp][:, 512 * ts:512 * (ts + 1)],
                            self.p2[:, 0:512], ALU.subtract),
                            waits=([("sT", tB)] if hp == 0 else []) +
                                  [("sS", lastq[b])], inc="sV")
                        vN = op("vector", lambda e, b=b: e.tensor_tensor(
                            xt[b][:], xt[b][:], self.p3[:, 0:512], ALU.mult), inc="sV")
                        sQ = op("scalar", lambda e, b=b: e.activation(
                            qt[b][:], xt[b][:], AF.Identity, scale=self.ivc[:, 0:1],
                            bias=self.mcol[:, 0:1]), waits=[("sV", vN), ("sV", v2)], inc="sS")
                        lastq[b] = sQ
                        op("vector", lambda e, b=b: e.tensor_scalar(
                            qt[b][:], qt[b][:], MAGIC, QMAX, ALU.subtract, ALU.min),
                            waits=[("sS", sQ)], inc="sV")
                        op("vector", lambda e, ts=ts, hp=hp, b=b: e.tensor_scalar(
                            XQ[hp][:, 512 * ts:512 * (ts + 1)], qt[b][:], -QMAX - 1.0, None,
                            ALU.max), inc="sV")
                if self.DEBUG_TAPS and app == 0:
                    P["tap_rows"] = nc.declare_dram_parameter("tap_rows", [2, TME], F32,
                                                              isOutput=True)
                    P["tap_mx"] = nc.declare_dram_parameter("tap_mx", [1, 8], F32,
                                                            isOutput=True)
                    P["tap_xq"] = nc.declare_dram_parameter("tap_xq", [H, TME], BF16,
                                                            isOutput=True)
                    fin = self.dma("sync", P["tap_rows"].ap()[0:1, :], self.rowA[0:1, :])
                    fin = self.dma("sync", P["tap_rows"].ap()[1:2, :], self.rowC[0:1, :])
                    fin = self.dma("sync", P["tap_mx"].ap(), D["mx_o"].ap())
                    for p in range(HP):
                        fin = self.dma("sync", P["tap_xq"].ap()[128 * p:128 * (p + 1), :],
                                       XQ[p][:], waits=[("sV", C["sV"])])
                    self.op("sync", lambda e: None, waits=[("sLD", fin)])

    def ffn_rest(self, app, XQ, fstack):
        nc, P, D = self.nc, self.P, self.D
        op, dma = self.op, self.dma
        C = self.cnt
        if True:
            with ExitStack() as ph:
                gst = [self.sb(ph, f"gst{app}_{i}", [128, 1024], BF16) for i in range(2)]
                ust = [self.sb(ph, f"ust{app}_{i}", [128, 1024], BF16) for i in range(2)]
                gt = [self.sb(ph, f"fgt{app}_{i}", [128, 512], F32) for i in range(2)]
                ut = [self.sb(ph, f"fut{app}_{i}", [128, 512], F32) for i in range(2)]
                hbuf = [self.sb(ph, f"fhb{app}_{i}", [128, TME], F32) for i in range(2)]
                self.begin()
                self.fence(("sWQ", "sLD"))
                op("vector", lambda e: e.tensor_scalar(
                    self.sgx[:, 0:IT], self.sg_c[:, 0:IT], self.sxc[:, 0:1], None,
                    ALU.mult), inc="sV")
                vScl = op("vector", lambda e: e.tensor_scalar(
                    self.sux[:, 0:IT], self.su_c[:, 0:IT], self.sxc[:, 0:1], None,
                    ALU.mult), inc="sV")
                lastmmG = [0, 0]
                lasthd = [0, 0]
                lastTT = [0, 0]
                prevG = 0
                prevU = 0
                for it in range(IT):
                    b = it % 2
                    for k0 in range(0, HP, 2):
                        srcg = bass.AP(D["wgt_q"], 128 * it + 128 * k0 * 2816,
                                       [[2816, 128], [128 * 2816, 2], [1, 128]])
                        dstg = gst[b][:, 128 * k0:128 * (k0 + 2)].rearrange(
                            "p (k l) -> p k l", l=128)
                        dG = dma("sync", dstg, srcg,
                                 waits=([("sT", lastmmG[b]), ("sWQ", 1)] if k0 == 0 else []))
                        srcu = bass.AP(D["wut_q"], 128 * it + 128 * k0 * 2816,
                                       [[2816, 128], [128 * 2816, 2], [1, 128]])
                        dstu = ust[b][:, 128 * k0:128 * (k0 + 2)].rearrange(
                            "p (k l) -> p k l", l=128)
                        dU = dma("sync", dstu, srcu)
                    for ts in range(4):
                        for k in range(HP):
                            op("tensor", lambda e, b=b, ts=ts, k=k: e.matmul(
                                self.p0[:, 0:512], gst[b][:, 128 * k:128 * (k + 1)],
                                XQ[k][:, 512 * ts:512 * (ts + 1)],
                                start=(k == 0), stop=(k == HP - 1), skip_group_check=True),
                                waits=([("sLD", dU), ("sS", prevG)] if k == 0 else []),
                                inc="sT")
                        tG = C["sT"]
                        for k in range(HP):
                            op("tensor", lambda e, b=b, ts=ts, k=k: e.matmul(
                                self.p1[:, 0:512], ust[b][:, 128 * k:128 * (k + 1)],
                                XQ[k][:, 512 * ts:512 * (ts + 1)],
                                start=(k == 0), stop=(k == HP - 1), skip_group_check=True),
                                waits=([("sS", prevU)] if k == 0 else []),
                                inc="sT")
                        tU = C["sT"]
                        lastmmG[b] = tU
                        bb = ts % 2
                        sG = op("scalar", lambda e, it=it, bb=bb: e.activation(
                            gt[bb][:], self.p0[:, 0:512], AF.Silu,
                            scale=self.sgx[:, it:it + 1]),
                            waits=[("sT", tG), ("sV", lastTT[bb]), ("sV", vScl)],
                            inc="sS")
                        prevG = sG
                        sU = op("scalar", lambda e, it=it, bb=bb: e.activation(
                            ut[bb][:], self.p1[:, 0:512], AF.Identity,
                            scale=self.sux[:, it:it + 1]), waits=[("sT", tU)], inc="sS")
                        prevU = sU
                        vH = op("vector", lambda e, b=b, ts=ts, bb=bb: e.tensor_tensor(
                            hbuf[b][:, 512 * ts:512 * (ts + 1)], gt[bb][:], ut[bb][:],
                            ALU.mult), waits=[("sS", sU), ("sLD", lasthd[b])], inc="sV")
                        lastTT[bb] = vH
                        op("vector", lambda e, b=b, ts=ts, it=it: e.tensor_reduce(
                            self.mxw[:, (it * 4 + ts) % 96:(it * 4 + ts) % 96 + 1],
                            hbuf[b][:, 512 * ts:512 * (ts + 1)], AX.X,
                            ALU.max, apply_absolute_value=True), inc="sV")
                    vDone = C["sV"]
                    lasthd[b] = dma("sync", D["hpre_sp"].ap()[128 * it:128 * (it + 1), :],
                                    hbuf[b][:], waits=[("sV", vDone)])
                vRm = op("vector", lambda e: e.tensor_reduce(
                    self.rmax[:], self.mxw[:, 0:88], AX.X, ALU.max),
                    waits=[("sV", C["sV"])], inc="sV")
                dR = self.global_max_col(vRm)
                v1 = op("vector", lambda e: e.tensor_tensor(
                    self.gmax[0:1, 0:1], self.gmax[0:1, 0:1], self.sig2[0:1, 1:2], ALU.mult),
                    waits=[("sLD", dR)], inc="sV")
                tBc = op("tensor", lambda e: e.matmul(
                    self.p2[0:128, 0:1], self.onesf[0:1, :], self.gmax[0:1, 0:1],
                    start=True, stop=True), waits=[("sV", v1)], inc="sT")
                s1 = op("scalar", lambda e: e.activation(
                    self.col1[:], self.p2[0:128, 0:1], AF.Identity, scale=1.0 / QMAX),
                    waits=[("sT", tBc)], inc="sS")
                op("vector", lambda e: e.reciprocal(self.col0[:], self.col1[:]),
                   waits=[("sS", s1)], inc="sV")
                op("vector", lambda e: e.tensor_scalar(
                    self.sdx[:, 0:HP], self.sd_c[:, 0:HP], self.col1[:, 0:1], 0.5,
                    ALU.mult, ALU.mult), inc="sV")
                if self.DEBUG_TAPS and app == 0:
                    P["tap_hpre0"] = nc.declare_dram_parameter(
                        "tap_hpre0", [2816, TME], F32, isOutput=True)
                    P["tap_mxh"] = nc.declare_dram_parameter(
                        "tap_mxh", [1, 8], F32, isOutput=True)
                    fin = dma("sync", P["tap_hpre0"].ap(), D["hpre_sp"].ap())
                    fin = dma("sync", P["tap_mxh"].ap(), D["mx_o"].ap())
                    op("sync", lambda e: None, waits=[("sLD", fin)])
                self.flush()

            fstack.close()

            with ExitStack() as ph:
                hbuf = [self.sb(ph, f"fh2{app}_{i}", [128, TME], F32) for i in range(2)]
                hqb = [self.sb(ph, f"fhq{app}_{i}", [128, TME], BF16) for i in range(2)]
                self.begin()
                self.fence(("sLD",))
                lastV = [0, 0]
                lastW = [0, 0]
                for it in range(IT):
                    b = it % 2
                    dL = dma("sync", hbuf[b][:], D["hpre_sp"].ap()[128 * it:128 * (it + 1), :],
                             waits=[("sV", lastV[b])])
                    sQ = op("scalar", lambda e, b=b: e.activation(
                        hbuf[b][:], hbuf[b][:], AF.Identity, scale=self.col0[:, 0:1],
                        bias=self.mcol[:, 0:1]), waits=[("sLD", dL)], inc="sS")
                    op("vector", lambda e, b=b: e.tensor_scalar(
                        hbuf[b][:], hbuf[b][:], MAGIC, QMAX, ALU.subtract, ALU.min),
                        waits=[("sS", sQ), ("sLD", lastW[b])], inc="sV")
                    vQ = op("vector", lambda e, b=b: e.tensor_scalar(
                        hqb[b][:], hbuf[b][:], -QMAX - 1.0, None, ALU.max), inc="sV")
                    lastV[b] = vQ
                    lastW[b] = dma("sync", D["hq_sp"].ap()[128 * it:128 * (it + 1), :],
                                   hqb[b][:], waits=[("sV", vQ)])
                if self.DEBUG_TAPS and app == 0:
                    P["tap_hq0"] = nc.declare_dram_parameter(
                        "tap_hq0", [2816, TME], BF16, isOutput=True)
                    fin = dma("sync", P["tap_hq0"].ap(), D["hq_sp"].ap())
                    op("sync", lambda e: None, waits=[("sLD", fin)])
                self.flush()

            with ExitStack() as ph:
                wdt = [self.sb(ph, f"wdt{app}_{k}", [128, 1024], BF16) for k in range(IT)]
                hqc = [self.sb(ph, f"hqc{app}_{i}", [128, 22 * 512], BF16) for i in range(2)]
                dt_ = [self.sb(ph, f"fdn{app}_{i}", [128, 512], F32) for i in range(2)]
                self.begin()
                self.fence(("sLD",))
                dW = 0
                for k in range(IT):
                    srcd = bass.AP(D["wdt_q"], 128 * k * H, [[H, 128], [1, H]])
                    dW = dma("sync", wdt[k][:, 0:H], srcd)
                lastmm = [0, 0]
                lastrs = [0, 0]
                for ts in range(4):
                    b = ts % 2
                    for k0 in range(0, IT, 2):
                        kn = min(2, IT - k0)
                        srch = bass.AP(D["hq_sp"], 512 * ts + 128 * k0 * TME,
                                       [[TME, 128], [128 * TME, kn], [1, 512]])
                        dsth = hqc[b][:, 512 * k0:512 * (k0 + kn)].rearrange(
                            "p (k l) -> p k l", l=512)
                        dH = dma("sync", dsth, srch,
                                 waits=[("sT", lastmm[b])] if k0 == 0 else [])
                    for hb in range(HP):
                        pk = [self.p0, self.p1, self.p2, self.p3][hb % 4]
                        for k in range(IT):
                            op("tensor", lambda e, b=b, hb=hb, k=k, pk=pk: e.matmul(
                                pk[:, 0:512], wdt[k][:, 128 * hb:128 * (hb + 1)],
                                hqc[b][:, 512 * k:512 * (k + 1)],
                                start=(k == 0), stop=(k == IT - 1), skip_group_check=True),
                                waits=([("sLD", max(dH, dW)), ("sV", lastrs[hb % 2])]
                                       if k == 0 else []), inc="sT")
                        lastmm[b] = C["sT"]
                        bb = hb % 2
                        sE = op("scalar", lambda e, hb=hb, bb=bb, pk=pk: e.activation(
                            dt_[bb][:], pk[:, 0:512], AF.Identity,
                            scale=self.sdx[:, hb:hb + 1]),
                            waits=[("sT", lastmm[b]), ("sV", lastrs[bb])], inc="sS")
                        lastrs[bb] = op("vector", lambda e, ts=ts, hb=hb, bb=bb: e.tensor_tensor(
                            self.XR[hb][:, 512 * ts:512 * (ts + 1)],
                            self.XR[hb][:, 512 * ts:512 * (ts + 1)], dt_[bb][:], ALU.add),
                            waits=[("sS", sE)], inc="sV")
                self.flush()

    def phase2a(self):
        P, D = self.P, self.D
        op, dma = self.op, self.dma
        C = self.cnt
        with ExitStack() as ph:
            xnb = [self.sb(ph, f"xnb{p}", [128, TME], BF16) for p in range(HP)]
            xt = [self.sb(ph, f"x2t{i}", [128, 512], F32) for i in range(2)]
            self.begin()
            sR = self.stats_rows(self.XR)
            vPrev = 0
            dcnt = []
            for ts in range(4):
                tB = self.bcast_ts(ts, [("sS", sR), ("sV", vPrev)])
                for hp in range(HP):
                    b = hp % 2
                    op("vector", lambda e, ts=ts, hp=hp, b=b: e.tensor_tensor(
                        xt[b][:], self.XR[hp][:, 512 * ts:512 * (ts + 1)],
                        self.p2[:, 0:512], ALU.subtract),
                        waits=[("sT", tB)] if hp == 0 else [], inc="sV")
                    op("vector", lambda e, ts=ts, hp=hp, b=b: e.tensor_tensor(
                        xnb[hp][:, 512 * ts:512 * (ts + 1)], xt[b][:], self.p3[:, 0:512],
                        ALU.mult), inc="sV")
                vPrev = C["sV"]
            for hp in range(HP):
                d1 = dma("sync", D["xn2_b"].ap()[128 * hp:128 * (hp + 1), :], xnb[hp][:],
                         waits=[("sV", vPrev)])
                d2 = dma("sync", D["x1_sp"].ap()[128 * hp:128 * (hp + 1), :], self.XR[hp][:])
                dcnt.append(d2)
            op("gpsimd", lambda e: e.collective_compute(
                "AllGather", ALU.bypass, replica_groups=G4,
                ins=[D["xn2_b"].ap()], outs=[D["xn2_o"].ap()]),
                waits=[("sLD", dcnt[-1])], inc="sCC", by=1)
            self.flush()

    def phase2b(self):
        nc, P, D = self.nc, self.P, self.D
        op, dma = self.op, self.dma
        C = self.cnt
        ccAG = C["sCC"]
        with ExitStack() as ph:
            sb = self.sb
            ipw = [sb(ph, f"ipw{k}", [128, 1024], BF16) for k in range(HP)]
            xin = [sb(ph, f"xin{k}", [128, CK], BF16) for k in range(HP)]
            xc = [sb(ph, f"xc{p}", [128, CK + 3], BF16) for p in range(DP)]
            zb = [sb(ph, f"zb{i}", [128, CK], BF16) for i in range(2)]
            uc = [sb(ph, f"uc{i}", [128, CK], BF16) for i in range(2)]
            xdr = sb(ph, "xdr", [80, CK], F32)
            self.begin()
            dI = 0
            for k in range(HP):
                dI = dma("gpsimd", ipw[k][:], P["ipwT"].ap()[128 * k:128 * (k + 1), :])
            for p in range(DP):
                op("vector", lambda e, p=p: e.memset(xc[p][:, 0:3], 0.0), inc="sV")
            vHead = C["sV"]
            lastx = 0
            lastxc = [0] * DP
            lastz = [0, 0]
            lastu = [0, 0]
            lastuT = [0, 0]
            lastxd = 0
            lastxdS = 0
            prevheads = vHead
            for ck in range(NCK):
                r = (CK * ck) // TME
                off = (CK * ck) % TME
                dX = 0
                for k in range(HP):
                    dX = dma("sync", xin[k][:],
                             D["xn2_o"].ap()[H * r + 128 * k: H * r + 128 * (k + 1),
                                             off:off + CK],
                             waits=[("sT", lastx), ("sCC", ccAG), ("sGD", dI)])
                if ck > 0:
                    for p in range(DP):
                        op("vector", lambda e, p=p: e.tensor_copy(
                            xc[p][:, 0:3], xc[p][:, CK:CK + 3]),
                            waits=[("sT", lastxc[p])], inc="sV")
                    prevheads = C["sV"]
                for po in range(8):
                    for ts in range(CK // 512):
                        pt = [self.p0, self.p1][ts]
                        for k in range(HP):
                            op("tensor", lambda e, po=po, ts=ts, k=k, pt=pt: e.matmul(
                                pt[:, 0:512], ipw[k][:, 128 * po:128 * (po + 1)],
                                xin[k][:, 512 * ts:512 * (ts + 1)],
                                start=(k == 0), stop=(k == HP - 1), skip_group_check=True),
                                waits=([("sLD", dX), ("sS", C["sS"])] if k == 0 else []),
                                inc="sT")
                        tIP = C["sT"]
                        if po < DP:
                            op("scalar", lambda e, po=po, ts=ts, pt=pt: e.activation(
                                xc[po][:, 3 + 512 * ts:3 + 512 * (ts + 1)], pt[:, 0:512],
                                AF.Identity),
                                waits=[("sT", tIP), ("sV", prevheads)], inc="sS")
                        else:
                            p = po - DP
                            b = ck % 2
                            op("scalar", lambda e, ts=ts, pt=pt, b=b: e.activation(
                                zb[b][:, 512 * ts:512 * (ts + 1)], pt[:, 0:512], AF.Silu),
                                waits=[("sT", tIP), ("sLD", lastz[b])], inc="sS")
                            if ts == CK // 512 - 1:
                                sZ = C["sS"]
                                lastz[b] = dma(
                                    "sync", D["z_sp"].ap()[128 * p:128 * (p + 1),
                                                           CK * ck:CK * (ck + 1)],
                                    zb[b][:], waits=[("sS", sZ)])
                    if po == HP - 1:
                        lastx = C["sT"]
                sIP = C["sS"]
                for p in range(DP):
                    b = (ck + p) % 2
                    for ts in range(CK // 512):
                        for kk in range(4):
                            op("tensor", lambda e, p=p, ts=ts, kk=kk: e.matmul(
                                self.p2[:, 0:512],
                                self.DGT[:, (p * 4 + kk) * 128:(p * 4 + kk + 1) * 128],
                                xc[p][:, 512 * ts + kk: 512 * ts + kk + 512],
                                start=(kk == 0), stop=(kk == 3), skip_group_check=True),
                                waits=([("sS", sIP), ("sS", C["sS"])] if kk == 0 else []),
                                inc="sT")
                        tCV = C["sT"]
                        op("scalar", lambda e, p=p, ts=ts, b=b: e.activation(
                            uc[b][:, 512 * ts:512 * (ts + 1)], self.p2[:, 0:512], AF.Silu,
                            bias=self.cbD_sb[:, p:p + 1]),
                            waits=[("sT", tCV), ("sLD", lastu[b]), ("sT", lastuT[b])],
                            inc="sS")
                    lastxc[p] = C["sT"]
                    sCV = C["sS"]
                    lastu[b] = dma("sync",
                                   D["u_sp"].ap()[128 * p:128 * (p + 1), CK * ck:CK * (ck + 1)],
                                   uc[b][:], waits=[("sS", sCV)])
                    for ts in range(CK // 512):
                        op("tensor", lambda e, p=p, ts=ts, b=b: e.matmul(
                            self.pw[0:80, 512 * ts:512 * (ts + 1)], self.xpw_sb[p][:, 0:80],
                            uc[b][:, 512 * ts:512 * (ts + 1)],
                            start=(p == 0), stop=(p == DP - 1), skip_group_check=True),
                            waits=([("sS", sCV), ("sS", lastxdS)] if p == 0 else
                                   [("sS", sCV)] if ts == 0 else []),
                            inc="sT")
                    lastuT[b] = C["sT"]
                tXD = C["sT"]
                sXD = op("scalar", lambda e: e.activation(
                    xdr[0:80, 0:512], self.pw[0:80, 0:512], AF.Identity),
                    waits=[("sT", tXD)], inc="sS")
                sXD2 = op("scalar", lambda e: e.activation(
                    xdr[0:80, 512:1024], self.pw[0:80, 512:1024], AF.Identity),
                    waits=[("sT", tXD)], inc="sS")
                lastxdS = sXD2
                lastxd = dma("sync", D["xdb_b"].ap()[0:80, CK * ck:CK * (ck + 1)],
                             xdr[0:80, :], waits=[("sS", sXD2), ("sLD", lastxd)])
            op("gpsimd", lambda e: e.collective_compute(
                "AllReduce", ALU.add, replica_groups=G4,
                ins=[D["xdb_b"].ap()], outs=[D["xdb_o"].ap()]),
                waits=[("sLD", lastxd)], inc="sCC", by=1)
            self.flush()

    def phase2c(self):
        nc, P, D = self.nc, self.P, self.D
        op, dma = self.op, self.dma
        C = self.cnt
        ccAR = C["sCC"]
        with ExitStack() as ph:
            sb = self.sb
            dtlr = sb(ph, "dtlr", [64, CK], BF16)
            bcn = [sb(ph, f"bcn{i}", [1, 2 * CK], BF16) for i in range(2)]
            uch = [sb(ph, f"uch{p}", [128, CK], BF16) for p in range(DP)]
            zch = [sb(ph, f"zch{i}", [128, CK], BF16) for i in range(2)]
            delta = [sb(ph, f"dlt{p}", [128, CK], BF16) for p in range(DP)]
            du = [sb(ph, f"du{p}", [128, CK], BF16) for p in range(DP)]
            sa = sb(ph, "sa", [128, CK], F32)
            sr = sb(ph, "sr", [128, CK], F32)
            Bb = [sb(ph, f"Bb{i}", [128, CK], BF16) for i in range(2)]
            Cb = [sb(ph, f"Cb{i}", [128, CK], F32) for i in range(2)]
            dab = [sb(ph, f"dab{i}", [128, CK], F32) for i in range(2)]
            Xb = [sb(ph, f"Xsc{i}", [128, CK], BF16) for i in range(2)]
            hb = [sb(ph, f"hsc{i}", [128, CK], F32) for i in range(2)]
            tmp = sb(ph, "ytmp", [128, CK], F32)
            yac = [sb(ph, f"yac{p}", [128, CK], F32) for p in range(DP)]
            self.begin()
            self.fence(("sLD", "sGD"))
            lastdt = 0
            lastBC = [0, 0]
            lastCC = [0, 0]
            lastDA = [0, 0]
            lastX = [0, 0]
            lastH = [0, 0]
            lastld = 0
            lastvch = 0
            lastz2 = [0, 0]
            lastbcn = [0, 0]
            carrycnt = {}
            for ck in range(NCK):
                dD = dma("gpsimd", dtlr[:], D["xdb_o"].ap()[0:64, CK * ck:CK * (ck + 1)],
                         waits=[("sCC", ccAR), ("sT", lastld)])

                dU = 0
                for p in range(DP):
                    dU = dma("sync", uch[p][:],
                             D["u_sp"].ap()[128 * p:128 * (p + 1), CK * ck:CK * (ck + 1)],
                             waits=[("sV", lastvch)])
                vDel = []
                prevVD = 0
                for p in range(DP):
                    for ts in range(CK // 512):
                        op("tensor", lambda e, p=p, ts=ts: e.matmul(
                            self.pw[:, 512 * ts:512 * (ts + 1)], self.dtw_sb[0:64,
                            128 * p:128 * (p + 1)], dtlr[0:64, 512 * ts:512 * (ts + 1)],
                            start=True, stop=True),
                            waits=[("sGD", dD), ("sS", lastdt)] if ts == 0 else [], inc="sT")
                    tDT = C["sT"]
                    s1 = op("scalar", lambda e, p=p: e.activation(
                        sa[:], self.pw[:, 0:CK], AF.Abs, bias=self.dtb_sb[:, p:p + 1]),
                        waits=[("sT", tDT), ("sV", prevVD)], inc="sS")
                    s2 = op("scalar", lambda e, p=p: e.activation(
                        sr[:], self.pw[:, 0:CK], AF.Relu, bias=self.dtb_sb[:, p:p + 1]),
                        inc="sS")
                    lastdt = s2
                    s3 = op("scalar", lambda e: e.activation(sa[:], sa[:], AF.Exp, scale=-1.0),
                            inc="sS")
                    s4 = op("scalar", lambda e: e.activation(sa[:], sa[:], AF.Ln, bias=1.0),
                            inc="sS")
                    vD = op("vector", lambda e, p=p: e.tensor_tensor(
                        delta[p][:], sr[:], sa[:], ALU.add), waits=[("sS", s4)], inc="sV")
                    vDu = op("vector", lambda e, p=p: e.tensor_tensor(
                        du[p][:], delta[p][:], uch[p][:], ALU.mult),
                        waits=[("sLD", dU)], inc="sV")
                    prevVD = vDu
                    vDel.append((vD, vDu))
                prevBCevac = 0
                for n in range(NST):
                    nb = n % 2
                    dB = dma("gpsimd", bcn[nb][0:1, :].rearrange("p (r l) -> p r l", l=CK),
                             bass.AP(D["xdb_o"].ap().tensor, (64 + n) * TB + CK * ck,
                                     [[0, 1], [8 * TB, 2], [1, CK]]),
                             waits=[("sT", lastbcn[nb]), ("sCC", ccAR)])
                    op("tensor", lambda e, nb=nb: e.matmul(
                        self.p2[:, 0:512], self.onesb[0:1, :],
                        bcn[nb][0:1, 0:512], start=True, stop=True),
                        waits=[("sGD", dB), ("sV", lastBC[nb]), ("sV", lastCC[nb]),
                               ("sS", prevBCevac)], inc="sT")
                    op("tensor", lambda e, nb=nb: e.matmul(
                        self.p3[:, 0:512], self.onesb[0:1, :],
                        bcn[nb][0:1, 512:CK], start=True, stop=True),
                        inc="sT")
                    tBC = C["sT"]
                    op("scalar", lambda e, nb=nb: e.activation(
                        Bb[nb][:, 0:512], self.p2[:, 0:512], AF.Identity),
                        waits=[("sT", tBC)], inc="sS")
                    sB = op("scalar", lambda e, nb=nb: e.activation(
                        Bb[nb][:, 512:CK], self.p3[:, 0:512], AF.Identity), inc="sS")
                    op("tensor", lambda e, nb=nb: e.matmul(
                        self.p2[:, 0:512], self.onesb[0:1, :],
                        bcn[nb][0:1, CK:CK + 512], start=True, stop=True),
                        waits=[("sS", sB)], inc="sT")
                    lastbcn[nb] = op("tensor", lambda e, nb=nb: e.matmul(
                        self.p3[:, 0:512], self.onesb[0:1, :],
                        bcn[nb][0:1, CK + 512:2 * CK], start=True, stop=True),
                        inc="sT")
                    tCC = C["sT"]
                    op("scalar", lambda e, nb=nb: e.activation(
                        Cb[nb][:, 0:512], self.p2[:, 0:512], AF.Identity),
                        waits=[("sT", tCC)], inc="sS")
                    sCr2 = op("scalar", lambda e, nb=nb: e.activation(
                        Cb[nb][:, 512:CK], self.p3[:, 0:512], AF.Identity), inc="sS")
                    prevBCevac = sCr2
                    for p in range(DP):
                        pb = (n * DP + p) % 2
                        vD, vDu = vDel[p]
                        sDA = op("scalar", lambda e, n=n, p=p, pb=pb: e.activation(
                            dab[pb][:], delta[p][:], AF.Exp,
                            scale=self.A_sb[:, (p * 8 + n):(p * 8 + n) + 1]),
                            waits=[("sV", vD), ("sV", lastDA[pb])], inc="sS")
                        vX = op("vector", lambda e, p=p, pb=pb, nb=nb: e.tensor_tensor(
                            Xb[pb][:], du[p][:], Bb[nb][:], ALU.mult),
                            waits=[("sV", vDu), ("sS", sB), ("sV", lastX[pb])], inc="sV")
                        lastBC[nb] = vX
                        idx = n * DP + p
                        wv = [("sS", sDA), ("sV", lastH[pb])]
                        if idx in carrycnt:
                            wv.append(("sV", carrycnt[idx]))
                        vSc = op("vector", lambda e, pb=pb, idx=idx: e.tensor_tensor_scan(
                            hb[pb][:], dab[pb][:], Xb[pb][:],
                            self.carry[:, idx:idx + 1], ALU.mult, ALU.add),
                            waits=wv, inc="sV")
                        lastDA[pb] = vSc
                        lastX[pb] = vSc
                        carrycnt[idx] = op("vector", lambda e, pb=pb, idx=idx: e.tensor_copy(
                            self.carry[:, idx:idx + 1], hb[pb][:, CK - 1:CK]),
                            waits=[("sV", vSc)], inc="sV")
                        if n == 0:
                            vY = op("vector", lambda e, p=p, pb=pb, nb=nb:
                                    e.tensor_tensor(
                                        yac[p][:], hb[pb][:], Cb[nb][:], ALU.mult),
                                    waits=[("sS", sCr2)], inc="sV")
                        else:
                            op("vector", lambda e, pb=pb, nb=nb: e.tensor_tensor(
                                tmp[:], hb[pb][:], Cb[nb][:], ALU.mult),
                                waits=[("sS", sCr2)], inc="sV")
                            vY = op("vector", lambda e, p=p: e.tensor_tensor(
                                yac[p][:], yac[p][:], tmp[:], ALU.add),
                                inc="sV")
                        lastH[pb] = vY
                        lastCC[nb] = vY
                for p in range(DP):
                    dZ = dma("sync", zch[p % 2][:],
                             D["z_sp"].ap()[128 * p:128 * (p + 1), CK * ck:CK * (ck + 1)],
                             waits=[("sV", lastz2[p % 2])])
                    op("vector", lambda e, p=p: e.tensor_scalar(
                        tmp[:], uch[p][:], self.cbD_sb[:, 4 + p:5 + p], None, ALU.mult),
                        inc="sV")
                    op("vector", lambda e, p=p: e.tensor_tensor(
                        yac[p][:], yac[p][:], tmp[:], ALU.add), inc="sV")
                    lastz2[p % 2] = op("vector", lambda e, p=p, ck=ck: e.tensor_tensor(
                        self.yall[p][:, CK * ck:CK * (ck + 1)],
                        yac[p][:], zch[p % 2][:], ALU.mult),
                        waits=[("sLD", dZ)], inc="sV")
                lastvch = C["sV"]
                lastld = C["sT"]
            self.flush()

    def phase2d(self):
        nc, P, D = self.nc, self.P, self.D
        op, dma = self.op, self.dma
        C = self.cnt
        with ExitStack() as ph:
            opw = [self.sb(ph, f"opw{p}", [128, 1024], BF16) for p in range(DP)]
            ob = [self.sb(ph, f"opb{i}", [128, TME], BF16) for i in range(2)]
            self.begin()
            self.fence(("sLD", "sGD"))
            if self.DEBUG_TAPS:
                if "y_tap" not in D:
                    D["y_tap"] = nc.dram_tensor("y_tap", [DIN, TB], BF16)
                    P["tap_y"] = nc.declare_dram_parameter("tap_y", [DIN, TB], BF16,
                                                           isOutput=True)
                for p in range(DP):
                    dma("sync", D["y_tap"].ap()[128 * p:128 * (p + 1), :], self.yall[p][:])
            dO = 0
            for p in range(DP):
                dO = dma("gpsimd", opw[p][:], P["opwT"].ap()[128 * p:128 * (p + 1), :])
            lastst = [0, 0]
            lastev = 0
            for tb in range(4):
                for hb in range(HP):
                    b = (tb * HP + hb) % 2
                    for ts in range(4):
                        pk = [self.p0, self.p1, self.p2, self.p3][ts]
                        for p in range(DP):
                            op("tensor", lambda e, tb=tb, hb=hb, ts=ts, p=p, pk=pk: e.matmul(
                                pk[:, 0:512], opw[p][:, 128 * hb:128 * (hb + 1)],
                                self.yall[p][:, TME * tb + 512 * ts:TME * tb + 512 * (ts + 1)],
                                start=(p == 0), stop=(p == DP - 1), skip_group_check=True),
                                waits=([("sGD", dO), ("sS", lastev)] if (ts == 0 and p == 0)
                                       else []), inc="sT")
                    tMM = C["sT"]
                    for ts in range(4):
                        pk = [self.p0, self.p1, self.p2, self.p3][ts]
                        op("scalar", lambda e, b=b, ts=ts, pk=pk: e.activation(
                            ob[b][:, 512 * ts:512 * (ts + 1)], pk[:, 0:512], AF.Identity),
                            waits=[("sT", tMM), ("sLD", lastst[b])], inc="sS")
                    sEv = C["sS"]
                    lastev = sEv
                    lastst[b] = dma("sync",
                                    D["prt_b"].ap()[H * tb + 128 * hb: H * tb + 128 * (hb + 1), :],
                                    ob[b][:], waits=[("sS", sEv)])
            op("gpsimd", lambda e: e.collective_compute(
                "ReduceScatter", ALU.add, replica_groups=G4,
                ins=[D["prt_b"].ap()], outs=[D["prt_o"].ap()]),
                waits=[("sLD", max(lastst))], inc="sCC", by=1)
            self.flush()

    def phase3_resid(self):
        P, D = self.P, self.D
        op, dma = self.op, self.dma
        C = self.cnt
        ccRS = C["sCC"]
        with ExitStack() as ph:
            rb = [self.sb(ph, f"rb{i}", [128, TME], BF16) for i in range(2)]
            self.begin()
            self.fence(("sLD", "sGD"))
            lastadd = [0, 0]
            for hp in range(HP):
                b = hp % 2
                dX = dma("sync", self.XR[hp][:], D["x1_sp"].ap()[128 * hp:128 * (hp + 1), :])
                dR = dma("sync", rb[b][:], D["prt_o"].ap()[128 * hp:128 * (hp + 1), :],
                         waits=[("sCC", ccRS), ("sV", lastadd[b])])
                lastadd[b] = op("vector", lambda e, hp=hp, b=b: e.tensor_tensor(
                    self.XR[hp][:], self.XR[hp][:], rb[b][:], ALU.add),
                    waits=[("sLD", dR)], inc="sV")
            self.flush()

    def phase4_out(self):
        P = self.P
        op, dma = self.op, self.dma
        with ExitStack() as ph:
            obuf = [self.sb(ph, f"ob4{i}", [128, H], F32) for i in range(2)]
            self.begin()
            slots = [self.p0, self.p1]
            slot_ev = [0, 0]
            lastdma = [0, 0]
            for tt in range(16):
                b = tt % 2
                for hp in range(HP):
                    sl = (tt * HP + hp) % 2
                    tT = op("tensor", lambda e, tt=tt, hp=hp, sl=sl: e.transpose(
                        slots[sl][:, 0:128], self.XR[hp][:, 128 * tt:128 * (tt + 1)],
                        self.idf_sb[:]), waits=[("sS", slot_ev[sl])], inc="sT")
                    slot_ev[sl] = op("scalar", lambda e, hp=hp, b=b, sl=sl: e.activation(
                        obuf[b][:, 128 * hp:128 * (hp + 1)], slots[sl][:, 0:128],
                        AF.Identity), waits=[("sT", tT), ("sLD", lastdma[b])], inc="sS")
                lastdma[b] = dma("sync", P["out"].ap()[128 * tt:128 * (tt + 1), :], obuf[b][:],
                                 waits=[("sS", slot_ev[(tt * HP + HP - 1) % 2])])
            fin = self.cnt["sLD"]
            op("sync", lambda e: None, waits=[("sLD", fin)])
            self.flush()


_NC_CACHE = None


def _get_nc():
    global _NC_CACHE
    if _NC_CACHE is None:
        _NC_CACHE = B().build()
    return _NC_CACHE


def kernel(**inp):
    nc = _get_nc()
    xs = np.ascontiguousarray(inp["L0_attn_hidden_states"].reshape(16384, 1024))
    ipw = inp["L0_attn_in_proj_w"]
    cw = inp["L0_attn_conv_w"]
    cb = inp["L0_attn_conv_b"]
    xpw = inp["L0_attn_x_proj_w"]
    dtw = inp["L0_attn_dt_proj_w"]
    dtb = inp["L0_attn_dt_proj_b"]
    alog = inp["L0_attn_A_log"]
    dD = inp["L0_attn_D"]
    opw = inp["L0_attn_out_proj_w"]
    U = inp["L0_attn_U_param"]
    V = inp["L0_attn_V_param"]
    lrs = inp["L0_attn_low_rank_scale"]
    f32 = np.float32
    in_maps = []
    for c in range(8):
        q = c % 4
        dsl = slice(512 * q, 512 * (q + 1))
        zsl = slice(2048 + 512 * q, 2048 + 512 * (q + 1))
        m = {
            "x0": np.ascontiguousarray(xs[2048 * c:2048 * (c + 1)], f32),
            "wg": np.ascontiguousarray(inp["L0_ffn_w_gate"], f32),
            "wu": np.ascontiguousarray(inp["L0_ffn_w_up"], f32),
            "wd": np.ascontiguousarray(inp["L0_ffn_w_down"], f32),
            "clip2": np.ascontiguousarray(
                np.array([[inp["L0_ffn_clip_x"][0], inp["L0_ffn_clip_down"][0]]], f32)),
            "ipwT": np.ascontiguousarray(
                np.concatenate([ipw[dsl], ipw[zsl]], axis=0).T, f32),
            "cw4": np.ascontiguousarray(
                cw[dsl].reshape(4, 128, 4).transpose(1, 0, 2).reshape(128, 16), f32),
            "cbD": np.ascontiguousarray(
                np.concatenate([cb[dsl].reshape(4, 128).T, dD[dsl].reshape(4, 128).T],
                               axis=1), f32),
            "dtb4": np.ascontiguousarray(dtb[dsl].reshape(4, 128).T, f32),
            "alog": np.ascontiguousarray(
                alog[dsl].reshape(4, 128, 8).transpose(1, 0, 2).reshape(128, 32), f32),
            "uT": np.ascontiguousarray(U[dsl].T, f32),
            "vprm": np.ascontiguousarray(V, f32),
            "lrs1": np.ascontiguousarray(np.asarray(lrs, f32).reshape(1, 1)),
            "xpwT": np.ascontiguousarray(xpw[:, dsl].T, f32),
            "dtwT": np.ascontiguousarray(dtw[dsl].T, f32),
            "opwT": np.ascontiguousarray(opw[:, dsl].T, f32),
            "idf": np.eye(128, dtype=f32),
        }
        in_maps.append(m)
    res = run_bass_kernel_spmd(nc, in_maps, core_ids=list(range(8)))
    out = np.concatenate([np.asarray(res.results[c]["out"]) for c in range(8)], axis=0)
    return out.reshape(2, 8192, 1024).astype(np.float32)


# version 35
# speedup vs baseline: 1.0003x; 1.0003x over previous
import math
import numpy as np
import ml_dtypes
from contextlib import ExitStack
import concourse.bass as bass
import concourse.mybir as mybir
from concourse.bass_utils import run_bass_kernel_spmd

F32 = mybir.dt.float32
BF16 = mybir.dt.bfloat16
AF = mybir.ActivationFunctionType
ALU = mybir.AluOpType
AX = mybir.AxisListType

MAGIC = float(np.float32(1.5 * 2 ** 23))
EPS = 1e-5
QMAX = 127.0
G8 = [list(range(8))]
G4 = [[0, 1, 2, 3], [4, 5, 6, 7]]

H = 1024
HP = 8
TME = 2048
IT = 22
DIN = 512
DP = 4
NST = 8
TB = 8192
CK = 1024
NCK = TB // CK


class B:
    DEBUG_TAPS = False
    TRUNCATE = False
    P0_QUANT = True
    lists = None

    def sem(self, name):
        h = self.outer.enter_context(self.nc.semaphore(name))
        self.sems[name] = h
        self.cnt[name] = 0

    def begin(self):
        if self.lists is None:
            self.lists = {k: [] for k in ("sync", "gpsimd", "vector", "scalar", "tensor")}

    def drive(self, *gens):
        gens = list(gens)
        while gens:
            for g in gens[:]:
                try:
                    next(g)
                except StopIteration:
                    gens.remove(g)

    def fence(self, sems):
        for eng in ("sync", "gpsimd", "vector", "scalar", "tensor"):
            self.op(eng, lambda e: None, waits=[(x, 1) for x in sems])

    def op(self, eng, emit, waits=(), inc=None, by=1):
        waits = tuple((s, self.cnt[s] if s in ("sLD", "sGD", "sSD", "sWQ") else v)
                      for (s, v) in waits if v)

        def closure(e, emit=emit, waits=waits, inc=inc, by=by):
            for s, v in waits:
                e.wait_ge(self.sems[s], v)
            inst = emit(e)
            if inc is not None and inst is not None:
                inst.then_inc(self.sems[inc], by)
        self.lists[eng].append(closure)
        if inc is not None:
            self.cnt[inc] += by
            return self.cnt[inc]
        return None

    def dma(self, eng, out, in_, waits=(), sem=None, **kw):
        if sem is None:
            sem = {"sync": "sLD", "gpsimd": "sGD", "scalar": "sSD"}[eng]
        return self.op(eng, lambda e: e.dma_start(out=out, in_=in_, **kw), waits, sem, 16)

    def flush(self):
        with self.nc.Block() as blk:
            for eng in ("sync", "gpsimd", "vector", "scalar", "tensor"):
                ops = self.lists[eng]
                if ops:
                    getattr(blk, eng)(lambda e, ops=tuple(ops): [f(e) for f in ops])
        self.lists = None
        self.nc.all_engine_barrier()

    def sb(self, stack, name, shape, dt):
        return stack.enter_context(self.nc.sbuf_tensor(name, shape, dt))

    def build(self):
        self.nc = bass.Bass("TRN2", target_bir_lowering=False, debug=False)
        nc = self.nc
        self.cnt = {}
        self.sems = {}
        P = {}

        def par(name, shape, dt=F32, out=False):
            P[name] = nc.declare_dram_parameter(name, shape, dt, isOutput=out)

        par("x0", [TME, H])
        par("wg", [2816, H]); par("wu", [2816, H]); par("wd", [H, 2816])
        par("clip2", [1, 2])
        par("ipwT", [H, 1024])
        par("cw4", [128, 16]); par("cbD", [128, 8]); par("dtb4", [128, 4])
        par("alog", [128, 32]); par("uT", [8, 512]); par("vprm", [8, 8])
        par("lrs1", [1, 1])
        par("xpwT", [512, 80]); par("dtwT", [64, 512]); par("opwT", [512, 1024])
        par("idf", [128, 128])
        par("out", [TME, H], out=True)

        D = {}

        def dram(name, shape, dt):
            D[name] = nc.dram_tensor(name, shape, dt)

        dram("wgt_q", [H, 2816], BF16)
        dram("wut_q", [H, 2816], BF16)
        dram("wdt_q", [2816, H], BF16)
        dram("hpre_sp", [2816, TME], F32)
        dram("hq_sp", [2816, TME], BF16)
        dram("x1_sp", [H, TME], F32)
        dram("z_sp", [DIN, TB], BF16)
        dram("u_sp", [DIN, TB], BF16)
        dram("mx_b", [1, 8], F32); dram("mx_o", [1, 8], F32)
        dram("xn2_bA", [H, TME // 2], BF16); dram("xn2_oA", [4 * H, TME // 2], BF16)
        dram("xn2_bB", [H, TME // 2], BF16); dram("xn2_oB", [4 * H, TME // 2], BF16)
        dram("xdb_b", [80, TB], F32); dram("xdb_o", [80, TB], F32)
        dram("prt_b", [4 * H, TME], BF16); dram("prt_o", [H, TME], BF16)
        self.P, self.D = P, D

        self.outer = ExitStack()
        with self.outer:
            o = self.outer
            for s in ("sLD", "sGD", "sSD", "sCC", "sV", "sS", "sT", "sWQ"):
                self.sem(s)
            sb = self.sb
            self.A_sb = sb(o, "A_sb", [128, 32], F32)
            self.cbD_sb = sb(o, "cbD_sb", [128, 8], F32)
            self.dtb_sb = sb(o, "dtb_sb", [128, 4], F32)
            self.carry = sb(o, "carry", [128, 32], F32)
            self.onesf = sb(o, "onesf", [1, 128], F32)
            self.onesb = sb(o, "onesb", [1, 128], BF16)
            self.onescol = sb(o, "onescol", [128, 1], BF16)
            self.idf_sb = sb(o, "idf_sb", [128, 128], F32)
            self.idb_sb = sb(o, "idb_sb", [128, 128], BF16)
            self.DGT = sb(o, "DGT", [128, 16 * 128], BF16)
            self.sg_c = sb(o, "sg_c", [128, IT], F32)
            self.su_c = sb(o, "su_c", [128, IT], F32)
            self.sd_c = sb(o, "sd_c", [128, HP], F32)
            self.sgx = sb(o, "sgx", [128, IT], F32)
            self.sux = sb(o, "sux", [128, IT], F32)
            self.sdx = sb(o, "sdx", [128, HP], F32)
            self.sig2 = sb(o, "sig2", [1, 2], F32)
            self.gmax = sb(o, "gmax", [1, 1], F32)
            self.rowA = sb(o, "rowA", [1, TME], F32)
            self.rowB = sb(o, "rowB", [1, TME], F32)
            self.rowC = sb(o, "rowC", [1, TME], F32)
            self.col0 = sb(o, "col0", [128, 1], F32)
            self.col1 = sb(o, "col1", [128, 1], F32)
            self.rmax = sb(o, "rmax", [128, 1], F32)
            self.mcol = sb(o, "mcol", [128, 1], F32)
            self.sxc = sb(o, "sxc", [128, 1], F32)
            self.ivc = sb(o, "ivc", [128, 1], F32)
            self.mxw = sb(o, "mxw", [128, 96], F32)
            self.epsc = sb(o, "epsc", [1, 1], F32)
            self.xpw_sb = [sb(o, f"xpw{p}", [128, 80], BF16) for p in range(DP)]
            self.dtw_sb = sb(o, "dtw_sb", [64, 512], BF16)
            self.clip_sb = sb(o, "clip_sb", [1, 2], F32)
            self.p0 = o.enter_context(nc.psum_tensor("p0", [128, 512], F32))
            self.p1 = o.enter_context(nc.psum_tensor("p1", [128, 512], F32))
            self.p2 = o.enter_context(nc.psum_tensor("p2", [128, 512], F32))
            self.p3 = o.enter_context(nc.psum_tensor("p3", [128, 512], F32))
            self.pw = o.enter_context(nc.psum_tensor("pw", [128, 1024], F32))
            self.ptb0 = o.enter_context(nc.psum_tensor("ptb0", [128, 512], BF16))
            self.ptb1 = o.enter_context(nc.psum_tensor("ptb1", [128, 512], BF16))

            ph0 = ExitStack()
            self.begin()
            qg = self.phase0(ph0)
            with ExitStack() as ph:
                self.XR = [sb(ph, f"XR{p}", [128, TME], F32) for p in range(HP)]
                import itertools
                fstack0 = ExitStack()
                XQ0 = [self.sb(fstack0, f"XQ0_{p}", [128, TME], BF16)
                       for p in range(HP)]
                ag = itertools.chain(self.phase_load_x0(),
                                     self.ffn_stageA_gen(0, XQ0))
                self.drive(qg, ag)
                self.flush()
                ph0.close()
                self.ffn_rest(0, XQ0, fstack0)
                self.phase2a()
            if False:
                pass
                if self.DEBUG_TAPS:
                    P["tap_xr0"] = nc.declare_dram_parameter("tap_xr0", [H, TME], F32,
                                                             isOutput=True)
                    P["tap_idf"] = nc.declare_dram_parameter("tap_idf", [128, 128], F32,
                                                             isOutput=True)
                    P["tap_idb"] = nc.declare_dram_parameter("tap_idb", [128, 128], BF16,
                                                             isOutput=True)
                    self.begin()
                    fin = 0
                    for p in range(HP):
                        fin = self.dma("sync", P["tap_xr0"].ap()[128 * p:128 * (p + 1), :],
                                       self.XR[p][:])
                    fin = self.dma("sync", P["tap_idf"].ap(), self.idf_sb[:])
                    fin = self.dma("sync", P["tap_idb"].ap(), self.idb_sb[:])
                    self.op("sync", lambda e: None, waits=[("sLD", fin)])
                    self.flush()
                if self.TRUNCATE:
                    return nc
                self.ffn(app=0)
                self.phase2a()
            with ExitStack() as ph:
                self.yall = [sb(ph, f"yall{p}", [128, TB], BF16) for p in range(DP)]
                self.phase2b()
                self.phase2c()
                self.phase2d()
            with ExitStack() as ph:
                self.XR = [sb(ph, f"XS{p}", [128, TME], F32) for p in range(HP)]
                self.phase3_resid()
                self.ffn(app=1)
                self.phase4_out()
            if self.DEBUG_TAPS:
                taps = [("wgt_q", [H, 2816], BF16), ("wdt_q", [2816, H], BF16),
                        ("x1_sp", [H, TME], F32), ("xn2_o", [4 * H, TME], BF16),
                        ("xdb_o", [80, TB], F32), ("u_sp", [DIN, TB], BF16),
                        ("z_sp", [DIN, TB], BF16), ("prt_o", [H, TME], BF16),
                        ("hpre_sp", [2816, TME], F32), ("hq_sp", [2816, TME], BF16),
                        ("xdb_b", [80, TB], F32), ("prt_b", [4 * H, TME], BF16)]
                for nm, shape, dt in taps:
                    P["tap_" + nm] = nc.declare_dram_parameter(
                        "tap_" + nm, shape, dt, isOutput=True)
                self.begin()
                fin = 0
                for nm, shape, dt in taps:
                    fin = self.dma("sync", P["tap_" + nm].ap(), D[nm].ap())
                if "y_tap" in D:
                    fin = self.dma("sync", P["tap_y"].ap(), D["y_tap"].ap())
                self.op("sync", lambda e: None, waits=[("sLD", fin)])
                self.flush()
        return nc

    def phase0(self, ph):
        nc, P, D = self.nc, self.P, self.D
        op, dma = self.op, self.dma
        C = self.cnt
        if True:
            sb = self.sb
            wbuf = [sb(ph, f"w0b{i}", [128, 2816], F32) for i in range(2)]
            qbf = [sb(ph, f"q0b{i}", [128, 2816], BF16) for i in range(2)]
            qc0p = [sb(ph, f"qc0{i}", [128, 1], F32) for i in range(2)]
            qc1p = [sb(ph, f"qc1{i}", [128, 1], F32) for i in range(2)]
            tbuf = [sb(ph, f"t0b{i}", [128, 2816], BF16) for i in range(2)]
            alog_sb = sb(ph, "alog_sb", [128, 32], F32)
            expa = sb(ph, "expa", [128, 32], F32)
            uT_sb = sb(ph, "uT_sb", [8, 512], F32)
            v_sb = sb(ph, "v_sb", [8, 8], F32)
            lrs_sb = sb(ph, "lrs_sb", [1, 1], F32)
            ccol = sb(ph, "ccol", [128, 1], F32)
            uvev = sb(ph, "uvev", [128, 32], F32)
            iop = sb(ph, "iop", [128, 128], F32)
            iof = sb(ph, "iof", [128, 128], F32)
            mask = sb(ph, "mask", [128, 128], F32)
            cw_sb = sb(ph, "cw_sb", [128, 16], F32)

            for name, dst in [("alog", alog_sb), ("uT", uT_sb), ("vprm", v_sb),
                              ("lrs1", lrs_sb), ("cw4", cw_sb), ("cbD", self.cbD_sb),
                              ("dtb4", self.dtb_sb), ("idf", self.idf_sb),
                              ("clip2", self.clip_sb)]:
                dma("sync", dst[:], P[name].ap())
            ld0 = C["sLD"]
            dma("gpsimd", self.dtw_sb[:], P["dtwT"].ap())
            for p in range(DP):
                dma("gpsimd", self.xpw_sb[p][:], P["xpwT"].ap()[128 * p:128 * (p + 1), :])
            op("vector", lambda e: e.memset(self.onesf[:], 1.0), inc="sV")
            op("vector", lambda e: e.memset(self.onesb[:], 1.0), inc="sV")
            op("vector", lambda e: e.memset(self.onescol[:], 1.0), inc="sV")
            op("vector", lambda e: e.memset(self.carry[:], 0.0), inc="sV")
            op("vector", lambda e: e.memset(self.mcol[:], MAGIC), inc="sV")
            op("vector", lambda e: e.memset(self.epsc[:], EPS), inc="sV")
            vOnes = C["sV"]
            op("vector", lambda e: e.tensor_copy(self.idb_sb[:], self.idf_sb[:]),
               waits=[("sLD", ld0)], inc="sV")
            vIdb = C["sV"]
            op("gpsimd", lambda e: e.iota(iop[:], [[0, 128]], channel_multiplier=1,
                                          allow_small_or_imprecise_dtypes=True), inc="sGD", by=1)
            op("gpsimd", lambda e: e.iota(iof[:], [[1, 128]], channel_multiplier=0,
                                          allow_small_or_imprecise_dtypes=True), inc="sGD", by=1)
            gIota = C["sGD"]
            op("vector", lambda e: e.tensor_tensor(mask[:], iop[:], iof[:], ALU.is_equal),
               waits=[("sGD", gIota)], inc="sV")
            vMask = C["sV"]
            for j in range(16):
                op("vector", lambda e, j=j: e.tensor_scalar(
                    self.DGT[:, j * 128:(j + 1) * 128], mask[:], cw_sb[:, j:j + 1], None,
                    ALU.mult), waits=[("sV", vMask), ("sLD", ld0)], inc="sV")
            op("scalar", lambda e: e.activation(self.sig2[:], self.clip_sb[:], AF.Sigmoid),
               waits=[("sLD", ld0)], inc="sS")
            op("scalar", lambda e: e.activation(expa[:], alog_sb[:], AF.Exp),
               waits=[("sLD", ld0)], inc="sS")
            tC = op("tensor", lambda e: e.matmul(self.p2[0:128, 0:1], self.onesf[0:1, :],
                                                 lrs_sb[0:1, 0:1], start=True, stop=True),
                    waits=[("sLD", ld0), ("sV", vOnes)], inc="sT")
            sC = op("scalar", lambda e: e.activation(ccol[:], self.p2[0:128, 0:1], AF.Identity,
                                                     scale=1.0 / math.sqrt(8.0)),
                    waits=[("sT", tC)], inc="sS")
            for p in range(DP):
                tUV = op("tensor", lambda e, p=p: e.matmul(
                    self.p3[0:128, 0:8], uT_sb[0:8, 128 * p:128 * (p + 1)], v_sb[0:8, 0:8],
                    start=True, stop=True), waits=[("sLD", ld0), ("sS", sC)], inc="sT")
                op("scalar", lambda e, p=p: e.activation(
                    uvev[:, 8 * p:8 * p + 8], self.p3[0:128, 0:8], AF.Identity),
                    waits=[("sT", tUV)], inc="sS")
            sUV = C["sS"]
            op("vector", lambda e: e.scalar_tensor_tensor(
                self.A_sb[:], uvev[:], ccol[:, 0:1], expa[:], ALU.mult, ALU.subtract),
               waits=[("sS", sUV)], inc="sV")
            yield

            specs = [("wg", self.sg_c, D["wgt_q"], 22, H),
                     ("wu", self.su_c, D["wut_q"], 22, H),
                     ("wd", self.sd_c, D["wdt_q"], HP, 2816)]
            if not self.P0_QUANT:
                specs = []
            ptbs = [self.ptb0, self.ptb1]
            lastq = [0, 0]
            slot_ev = [0, 0]
            lastrd = [0, 0]
            prevrdP = [0, 0]
            for wname, scol, qdst, ntile, CW in specs:
                for i in range(ntile):
                    b = i % 2
                    qc0, qc1 = qc0p[b], qc1p[b]
                    dma("sync", wbuf[b][0:64, 0:CW],
                        P[wname].ap()[128 * i:128 * i + 64, :],
                        waits=[("sV", lastrd[b])], sem="sWQ")
                    dL = dma("scalar", wbuf[b][64:128, 0:CW],
                             P[wname].ap()[128 * i + 64:128 * (i + 1), :],
                             waits=[("sV", lastrd[b])], sem="sWQ")
                    vr1 = op("vector", lambda e, b=b, CW=CW, qc1=qc1: e.tensor_reduce(
                        qc1[:], wbuf[b][:, 0:CW], AX.X, ALU.max,
                        apply_absolute_value=True),
                        waits=[("sWQ", dL), ("sS", prevrdP[b])], inc="sV")
                    vr2 = op("vector", lambda e, qc0=qc0, qc1=qc1: e.reciprocal(
                        qc0[:], qc1[:]), waits=[("sV", vr1)], inc="sV")
                    vr3 = op("vector", lambda e, qc0=qc0: e.tensor_scalar(
                        qc0[:], qc0[:], QMAX, None, ALU.mult),
                        waits=[("sV", vr2)], inc="sV")
                    vI = op("vector", lambda e, scol=scol, i=i, qc1=qc1: e.tensor_scalar(
                        scol[:, i:i + 1], qc1[:], 1.0 / QMAX, None, ALU.mult),
                        waits=[("sV", vr3)], inc="sV")
                    sR = op("scalar", lambda e, b=b, CW=CW, qc0=qc0: e.activation(
                        wbuf[b][:, 0:CW], wbuf[b][:, 0:CW], AF.Identity,
                        scale=qc0[:, 0:1], bias=self.mcol[:, 0:1]), waits=[("sV", vI)], inc="sS")
                    prevrdP[b] = sR
                    vQ = op("vector", lambda e, b=b, CW=CW: e.tensor_scalar(
                        qbf[b][:, 0:CW], wbuf[b][:, 0:CW], MAGIC, None, ALU.subtract),
                        waits=[("sS", sR), ("sWQ", lastq[b])], inc="sV")
                    lastrd[b] = vQ
                    nblk = CW // 128
                    for cb in range(nblk):
                        sl = cb % 2
                        tT = op("tensor", lambda e, b=b, cb=cb, sl=sl: e.transpose(
                            ptbs[sl][:, 0:128], qbf[b][:, 128 * cb:128 * (cb + 1)],
                            self.idb_sb[:]),
                            waits=[("sV", vQ), ("sV", vIdb), ("sS", slot_ev[sl])], inc="sT")
                        slot_ev[sl] = op("scalar", lambda e, b=b, cb=cb, sl=sl: e.activation(
                            tbuf[b][:, 128 * cb:128 * (cb + 1)], ptbs[sl][:, 0:128],
                            AF.Identity),
                            waits=[("sT", tT)] + ([("sWQ", lastq[b])] if cb == 0 else []),
                            inc="sS")
                    CB = 2
                    for c0 in range(0, nblk, CB):
                        cn = min(CB, nblk - c0)
                        src = tbuf[b][:, 128 * c0:128 * (c0 + cn)].rearrange(
                            "p (c l) -> p c l", l=128)
                        dst = bass.AP(qdst, 128 * i + 128 * c0 * qdst.shape[1],
                                      [[qdst.shape[1], 128], [128 * qdst.shape[1], cn],
                                       [1, 128]])
                        lastq[b] = dma("sync", dst, src, sem="sWQ",
                                       waits=[("sS", slot_ev[0]), ("sS", slot_ev[1])])
                    yield

    def phase_load_x0(self):
        P = self.P
        op, dma = self.op, self.dma
        with ExitStack() as ph:
            xbuf = [self.sb(ph, f"x0b{i}", [128, H], F32) for i in range(2)]
            slots = [self.p0, self.p1]
            slot_ev = [0, 0]
            lastuse = [0, 0]
            for tt in range(16):
                yield
                b = tt % 2
                dL = dma("sync" if b == 0 else "scalar",
                         xbuf[b][:], P["x0"].ap()[128 * tt:128 * (tt + 1), :],
                         waits=[("sS", lastuse[b])], sem="sLD")
                for hp in range(HP):
                    sl = (tt * HP + hp) % 2
                    tT = op("tensor", lambda e, b=b, hp=hp, sl=sl: e.transpose(
                        slots[sl][:, 0:128], xbuf[b][:, 128 * hp:128 * (hp + 1)],
                        self.idf_sb[:]), waits=[("sLD", dL), ("sS", slot_ev[sl])], inc="sT")
                    slot_ev[sl] = op("scalar", lambda e, hp=hp, tt=tt, sl=sl: e.activation(
                        self.XR[hp][:, 128 * tt:128 * (tt + 1)], slots[sl][:, 0:128],
                        AF.Identity), waits=[("sT", tT)], inc="sS")
                lastuse[b] = slot_ev[(tt * HP + HP - 1) % 2]
            self._xr_ready = self.cnt["sS"]

    _stats_n = 0

    def stats_rows(self, X):
        op = self.op
        C = self.cnt
        B._stats_n += 1
        sn = B._stats_n
        with ExitStack() as ph:
            xb = [self.sb(ph, f"stb{sn}_{i}", [128, TME], BF16) for i in range(2)]
            lastmm = [0, 0]
            xrdy = getattr(self, "_xr_ready", 0)
            for hp in range(HP):
                b = hp % 2
                vCp = op("vector", lambda e, hp=hp, b=b: e.tensor_copy(xb[b][:], X[hp][:]),
                         waits=[("sT", lastmm[b]), ("sS", xrdy)], inc="sV")
                for ts in range(4):
                    pt = [self.p0, self.p1, self.p2, self.p3][ts]
                    lastmm[b] = op("tensor", lambda e, b=b, ts=ts, hp=hp, pt=pt: e.matmul(
                        pt[0:1, 0:512], self.onescol[:, 0:1],
                        xb[b][:, 512 * ts:512 * (ts + 1)],
                        start=(hp == 0), stop=(hp == HP - 1), skip_group_check=True),
                        waits=[("sV", vCp)], inc="sT")
            tSum = C["sT"]
            for ts in range(4):
                pt = [self.p0, self.p1, self.p2, self.p3][ts]
                op("scalar", lambda e, ts=ts, pt=pt: e.activation(
                    self.rowA[0:1, 512 * ts:512 * (ts + 1)], pt[0:1, 0:512],
                    AF.Identity, scale=1.0 / H), waits=[("sT", tSum)], inc="sS")
            sM = C["sS"]
            lastmm = [0, 0]
            for hp in range(HP):
                b = hp % 2
                sQp = op("scalar", lambda e, hp=hp, b=b: e.activation(
                    xb[b][:], X[hp][:], AF.Square),
                    waits=[("sM_dummy", 0), ("sS", sM if hp < 2 else 0), ("sT", lastmm[b])],
                    inc="sS")
                for ts in range(4):
                    pt = [self.p0, self.p1, self.p2, self.p3][ts]
                    lastmm[b] = op("tensor", lambda e, b=b, ts=ts, hp=hp, pt=pt: e.matmul(
                        pt[0:1, 0:512], self.onescol[:, 0:1],
                        xb[b][:, 512 * ts:512 * (ts + 1)],
                        start=(hp == 0), stop=(hp == HP - 1), skip_group_check=True),
                        waits=[("sS", sQp)], inc="sT")
            tSq = C["sT"]
            for ts in range(4):
                pt = [self.p0, self.p1, self.p2, self.p3][ts]
                op("scalar", lambda e, ts=ts, pt=pt: e.activation(
                    self.rowB[0:1, 512 * ts:512 * (ts + 1)], pt[0:1, 0:512],
                    AF.Identity, scale=1.0 / H), waits=[("sT", tSq)], inc="sS")
            sQ = C["sS"]
            v1 = op("vector", lambda e: e.tensor_tensor(
                self.rowC[0:1, :], self.rowA[0:1, :], self.rowA[0:1, :], ALU.mult),
                waits=[("sS", sQ)], inc="sV")
            v2 = op("vector", lambda e: e.tensor_tensor(
                self.rowB[0:1, :], self.rowB[0:1, :], self.rowC[0:1, :], ALU.subtract),
                inc="sV")
            op("scalar", lambda e: e.activation(self.rowB[0:1, :], self.rowB[0:1, :],
                                                AF.Ln, bias=self.epsc[0:1, 0:1]), waits=[("sV", v2)], inc="sS")
            sR = op("scalar", lambda e: e.activation(self.rowC[0:1, :], self.rowB[0:1, :],
                                                     AF.Exp, scale=-0.5), inc="sS")
            return sR

    def bcast_ts(self, ts, waits):
        op = self.op
        op("tensor", lambda e, ts=ts: e.matmul(
            self.p2[:, 0:512], self.onesf[0:1, :], self.rowA[0:1, 512 * ts:512 * (ts + 1)],
            start=True, stop=True), waits=waits, inc="sT")
        return op("tensor", lambda e, ts=ts: e.matmul(
            self.p3[:, 0:512], self.onesf[0:1, :], self.rowC[0:1, 512 * ts:512 * (ts + 1)],
            start=True, stop=True), inc="sT")

    def global_max_col(self, vReady):
        op, dma = self.op, self.dma
        tT = op("tensor", lambda e: e.transpose(
            self.p2[0:1, 0:128], self.rmax[0:128, 0:1], self.idf_sb[:]),
            waits=[("sV", vReady)], inc="sT")
        vG = op("vector", lambda e: e.tensor_reduce(
            self.gmax[0:1, 0:1], self.p2[0:1, 0:128], AX.X, ALU.max),
            waits=[("sT", tT)], inc="sV")
        dW = dma("sync", self.D["mx_b"].ap()[0:1, 0:1], self.gmax[0:1, 0:1],
                 waits=[("sV", vG)])
        cc = op("gpsimd", lambda e: e.collective_compute(
            "AllReduce", ALU.max, replica_groups=G8,
            ins=[self.D["mx_b"].ap()], outs=[self.D["mx_o"].ap()]),
            waits=[("sLD", dW)], inc="sCC", by=1)
        dR = dma("sync", self.gmax[0:1, 0:1], self.D["mx_o"].ap()[0:1, 0:1],
                 waits=[("sCC", cc)])
        return dR

    def ffn(self, app):
        fstack = ExitStack()
        XQ = [self.sb(fstack, f"XQ{app}_{p}", [128, TME], BF16) for p in range(HP)]
        self.begin()
        self.drive(self.ffn_stageA_gen(app, XQ))
        self.flush()
        self.ffn_rest(app, XQ, fstack)

    def ffn_stageA_gen(self, app, XQ):
        nc, P, D = self.nc, self.P, self.D
        op, dma = self.op, self.dma
        C = self.cnt
        if True:
            with ExitStack() as ph:
                xt = [self.sb(ph, f"fxt{app}_{i}", [128, 512], F32) for i in range(2)]
                qt = [self.sb(ph, f"fqt{app}_{i}", [128, 512], F32) for i in range(2)]
                yield
                sR = self.stats_rows(self.XR)
                yield
                vPrev = 0
                for ts in range(4):
                    yield
                    tB = self.bcast_ts(ts, [("sS", sR), ("sV", vPrev)])
                    for hp in range(HP):
                        b = hp % 2
                        op("vector", lambda e, ts=ts, hp=hp, b=b: e.tensor_tensor(
                            xt[b][:], self.XR[hp][:, 512 * ts:512 * (ts + 1)],
                            self.p2[:, 0:512], ALU.subtract),
                            waits=[("sT", tB)] if hp == 0 else [], inc="sV")
                        op("vector", lambda e, b=b: e.tensor_tensor(
                            xt[b][:], xt[b][:], self.p3[:, 0:512], ALU.mult), inc="sV")
                        op("vector", lambda e, b=b, ts=ts, hp=hp: e.tensor_reduce(
                            self.mxw[:, ts * 8 + hp:ts * 8 + hp + 1], xt[b][:], AX.X,
                            ALU.max, apply_absolute_value=True), inc="sV")
                    vPrev = C["sV"]
                vRm = op("vector", lambda e: e.tensor_reduce(
                    self.rmax[:], self.mxw[:, 0:32], AX.X, ALU.max),
                    waits=[("sV", vPrev)], inc="sV")
                dR = self.global_max_col(vRm)
                v1 = op("vector", lambda e: e.tensor_tensor(
                    self.gmax[0:1, 0:1], self.gmax[0:1, 0:1], self.sig2[0:1, 0:1], ALU.mult),
                    waits=[("sLD", dR)], inc="sV")
                tBc = op("tensor", lambda e: e.matmul(
                    self.p2[0:128, 0:1], self.onesf[0:1, :], self.gmax[0:1, 0:1],
                    start=True, stop=True), waits=[("sV", v1)], inc="sT")
                s1 = op("scalar", lambda e: e.activation(
                    self.sxc[:], self.p2[0:128, 0:1], AF.Identity, scale=1.0 / QMAX),
                    waits=[("sT", tBc)], inc="sS")
                v2 = op("vector", lambda e: e.reciprocal(self.ivc[:], self.sxc[:]),
                        waits=[("sS", s1)], inc="sV")
                lastq = [0, 0]
                for ts in range(4):
                    yield
                    tB = self.bcast_ts(ts, [("sV", C["sV"])])
                    for hp in range(HP):
                        b = hp % 2
                        op("vector", lambda e, ts=ts, hp=hp, b=b: e.tensor_tensor(
                            xt[b][:], self.XR[hp][:, 512 * ts:512 * (ts + 1)],
                            self.p2[:, 0:512], ALU.subtract),
                            waits=([("sT", tB)] if hp == 0 else []) +
                                  [("sS", lastq[b])], inc="sV")
                        vN = op("vector", lambda e, b=b: e.tensor_tensor(
                            xt[b][:], xt[b][:], self.p3[:, 0:512], ALU.mult), inc="sV")
                        sQ = op("scalar", lambda e, b=b: e.activation(
                            qt[b][:], xt[b][:], AF.Identity, scale=self.ivc[:, 0:1],
                            bias=self.mcol[:, 0:1]), waits=[("sV", vN), ("sV", v2)], inc="sS")
                        lastq[b] = sQ
                        op("vector", lambda e, b=b: e.tensor_scalar(
                            qt[b][:], qt[b][:], MAGIC, QMAX, ALU.subtract, ALU.min),
                            waits=[("sS", sQ)], inc="sV")
                        op("vector", lambda e, ts=ts, hp=hp, b=b: e.tensor_scalar(
                            XQ[hp][:, 512 * ts:512 * (ts + 1)], qt[b][:], -QMAX - 1.0, None,
                            ALU.max), inc="sV")
                if self.DEBUG_TAPS and app == 0:
                    P["tap_rows"] = nc.declare_dram_parameter("tap_rows", [2, TME], F32,
                                                              isOutput=True)
                    P["tap_mx"] = nc.declare_dram_parameter("tap_mx", [1, 8], F32,
                                                            isOutput=True)
                    P["tap_xq"] = nc.declare_dram_parameter("tap_xq", [H, TME], BF16,
                                                            isOutput=True)
                    fin = self.dma("sync", P["tap_rows"].ap()[0:1, :], self.rowA[0:1, :])
                    fin = self.dma("sync", P["tap_rows"].ap()[1:2, :], self.rowC[0:1, :])
                    fin = self.dma("sync", P["tap_mx"].ap(), D["mx_o"].ap())
                    for p in range(HP):
                        fin = self.dma("sync", P["tap_xq"].ap()[128 * p:128 * (p + 1), :],
                                       XQ[p][:], waits=[("sV", C["sV"])])
                    self.op("sync", lambda e: None, waits=[("sLD", fin)])

    def ffn_rest(self, app, XQ, fstack):
        nc, P, D = self.nc, self.P, self.D
        op, dma = self.op, self.dma
        C = self.cnt
        if True:
            with ExitStack() as ph:
                gst = [self.sb(ph, f"gst{app}_{i}", [128, 1024], BF16) for i in range(2)]
                ust = [self.sb(ph, f"ust{app}_{i}", [128, 1024], BF16) for i in range(2)]
                gt = [self.sb(ph, f"fgt{app}_{i}", [128, 512], F32) for i in range(2)]
                ut = [self.sb(ph, f"fut{app}_{i}", [128, 512], F32) for i in range(2)]
                hbuf = [self.sb(ph, f"fhb{app}_{i}", [128, TME], F32) for i in range(2)]
                self.begin()
                self.fence(("sWQ", "sLD"))
                op("vector", lambda e: e.tensor_scalar(
                    self.sgx[:, 0:IT], self.sg_c[:, 0:IT], self.sxc[:, 0:1], None,
                    ALU.mult), inc="sV")
                vScl = op("vector", lambda e: e.tensor_scalar(
                    self.sux[:, 0:IT], self.su_c[:, 0:IT], self.sxc[:, 0:1], None,
                    ALU.mult), inc="sV")
                lastmmG = [0, 0]
                lasthd = [0, 0]
                lastTT = [0, 0]
                prevG = 0
                prevU = 0
                for it in range(IT):
                    b = it % 2
                    for k0 in range(0, HP, 2):
                        srcg = bass.AP(D["wgt_q"], 128 * it + 128 * k0 * 2816,
                                       [[2816, 128], [128 * 2816, 2], [1, 128]])
                        dstg = gst[b][:, 128 * k0:128 * (k0 + 2)].rearrange(
                            "p (k l) -> p k l", l=128)
                        dG = dma("sync", dstg, srcg,
                                 waits=([("sT", lastmmG[b]), ("sWQ", 1)] if k0 == 0 else []))
                        srcu = bass.AP(D["wut_q"], 128 * it + 128 * k0 * 2816,
                                       [[2816, 128], [128 * 2816, 2], [1, 128]])
                        dstu = ust[b][:, 128 * k0:128 * (k0 + 2)].rearrange(
                            "p (k l) -> p k l", l=128)
                        dU = dma("sync", dstu, srcu)
                    for ts in range(4):
                        for k in range(HP):
                            op("tensor", lambda e, b=b, ts=ts, k=k: e.matmul(
                                self.p0[:, 0:512], gst[b][:, 128 * k:128 * (k + 1)],
                                XQ[k][:, 512 * ts:512 * (ts + 1)],
                                start=(k == 0), stop=(k == HP - 1), skip_group_check=True),
                                waits=([("sLD", dU), ("sS", prevG)] if k == 0 else []),
                                inc="sT")
                        tG = C["sT"]
                        for k in range(HP):
                            op("tensor", lambda e, b=b, ts=ts, k=k: e.matmul(
                                self.p1[:, 0:512], ust[b][:, 128 * k:128 * (k + 1)],
                                XQ[k][:, 512 * ts:512 * (ts + 1)],
                                start=(k == 0), stop=(k == HP - 1), skip_group_check=True),
                                waits=([("sS", prevU)] if k == 0 else []),
                                inc="sT")
                        tU = C["sT"]
                        lastmmG[b] = tU
                        bb = ts % 2
                        sG = op("scalar", lambda e, it=it, bb=bb: e.activation(
                            gt[bb][:], self.p0[:, 0:512], AF.Silu,
                            scale=self.sgx[:, it:it + 1]),
                            waits=[("sT", tG), ("sV", lastTT[bb]), ("sV", vScl)],
                            inc="sS")
                        prevG = sG
                        sU = op("scalar", lambda e, it=it, bb=bb: e.activation(
                            ut[bb][:], self.p1[:, 0:512], AF.Identity,
                            scale=self.sux[:, it:it + 1]), waits=[("sT", tU)], inc="sS")
                        prevU = sU
                        vH = op("vector", lambda e, b=b, ts=ts, bb=bb: e.tensor_tensor(
                            hbuf[b][:, 512 * ts:512 * (ts + 1)], gt[bb][:], ut[bb][:],
                            ALU.mult), waits=[("sS", sU), ("sLD", lasthd[b])], inc="sV")
                        lastTT[bb] = vH
                        op("vector", lambda e, b=b, ts=ts, it=it: e.tensor_reduce(
                            self.mxw[:, (it * 4 + ts) % 96:(it * 4 + ts) % 96 + 1],
                            hbuf[b][:, 512 * ts:512 * (ts + 1)], AX.X,
                            ALU.max, apply_absolute_value=True), inc="sV")
                    vDone = C["sV"]
                    lasthd[b] = dma("sync", D["hpre_sp"].ap()[128 * it:128 * (it + 1), :],
                                    hbuf[b][:], waits=[("sV", vDone)])
                vRm = op("vector", lambda e: e.tensor_reduce(
                    self.rmax[:], self.mxw[:, 0:88], AX.X, ALU.max),
                    waits=[("sV", C["sV"])], inc="sV")
                dR = self.global_max_col(vRm)
                v1 = op("vector", lambda e: e.tensor_tensor(
                    self.gmax[0:1, 0:1], self.gmax[0:1, 0:1], self.sig2[0:1, 1:2], ALU.mult),
                    waits=[("sLD", dR)], inc="sV")
                tBc = op("tensor", lambda e: e.matmul(
                    self.p2[0:128, 0:1], self.onesf[0:1, :], self.gmax[0:1, 0:1],
                    start=True, stop=True), waits=[("sV", v1)], inc="sT")
                s1 = op("scalar", lambda e: e.activation(
                    self.col1[:], self.p2[0:128, 0:1], AF.Identity, scale=1.0 / QMAX),
                    waits=[("sT", tBc)], inc="sS")
                op("vector", lambda e: e.reciprocal(self.col0[:], self.col1[:]),
                   waits=[("sS", s1)], inc="sV")
                op("vector", lambda e: e.tensor_scalar(
                    self.sdx[:, 0:HP], self.sd_c[:, 0:HP], self.col1[:, 0:1], 0.5,
                    ALU.mult, ALU.mult), inc="sV")
                if self.DEBUG_TAPS and app == 0:
                    P["tap_hpre0"] = nc.declare_dram_parameter(
                        "tap_hpre0", [2816, TME], F32, isOutput=True)
                    P["tap_mxh"] = nc.declare_dram_parameter(
                        "tap_mxh", [1, 8], F32, isOutput=True)
                    fin = dma("sync", P["tap_hpre0"].ap(), D["hpre_sp"].ap())
                    fin = dma("sync", P["tap_mxh"].ap(), D["mx_o"].ap())
                    op("sync", lambda e: None, waits=[("sLD", fin)])
                self.flush()

            fstack.close()

            with ExitStack() as ph:
                hbuf = [self.sb(ph, f"fh2{app}_{i}", [128, TME], F32) for i in range(2)]
                hqb = [self.sb(ph, f"fhq{app}_{i}", [128, TME], BF16) for i in range(2)]
                self.begin()
                self.fence(("sLD",))
                lastV = [0, 0]
                lastW = [0, 0]
                for it in range(IT):
                    b = it % 2
                    dL = dma("sync", hbuf[b][:], D["hpre_sp"].ap()[128 * it:128 * (it + 1), :],
                             waits=[("sV", lastV[b])])
                    sQ = op("scalar", lambda e, b=b: e.activation(
                        hbuf[b][:], hbuf[b][:], AF.Identity, scale=self.col0[:, 0:1],
                        bias=self.mcol[:, 0:1]), waits=[("sLD", dL)], inc="sS")
                    op("vector", lambda e, b=b: e.tensor_scalar(
                        hbuf[b][:], hbuf[b][:], MAGIC, QMAX, ALU.subtract, ALU.min),
                        waits=[("sS", sQ), ("sLD", lastW[b])], inc="sV")
                    vQ = op("vector", lambda e, b=b: e.tensor_scalar(
                        hqb[b][:], hbuf[b][:], -QMAX - 1.0, None, ALU.max), inc="sV")
                    lastV[b] = vQ
                    lastW[b] = dma("sync", D["hq_sp"].ap()[128 * it:128 * (it + 1), :],
                                   hqb[b][:], waits=[("sV", vQ)])
                if self.DEBUG_TAPS and app == 0:
                    P["tap_hq0"] = nc.declare_dram_parameter(
                        "tap_hq0", [2816, TME], BF16, isOutput=True)
                    fin = dma("sync", P["tap_hq0"].ap(), D["hq_sp"].ap())
                    op("sync", lambda e: None, waits=[("sLD", fin)])
                self.flush()

            with ExitStack() as ph:
                wdt = [self.sb(ph, f"wdt{app}_{k}", [128, 1024], BF16) for k in range(IT)]
                hqc = [self.sb(ph, f"hqc{app}_{i}", [128, 22 * 512], BF16) for i in range(2)]
                dt_ = [self.sb(ph, f"fdn{app}_{i}", [128, 512], F32) for i in range(2)]
                self.begin()
                self.fence(("sLD",))
                dW = 0
                for k in range(IT):
                    srcd = bass.AP(D["wdt_q"], 128 * k * H, [[H, 128], [1, H]])
                    dW = dma("sync", wdt[k][:, 0:H], srcd)
                lastmm = [0, 0]
                lastrs = [0, 0]
                for ts in range(4):
                    b = ts % 2
                    for k0 in range(0, IT, 2):
                        kn = min(2, IT - k0)
                        srch = bass.AP(D["hq_sp"], 512 * ts + 128 * k0 * TME,
                                       [[TME, 128], [128 * TME, kn], [1, 512]])
                        dsth = hqc[b][:, 512 * k0:512 * (k0 + kn)].rearrange(
                            "p (k l) -> p k l", l=512)
                        dH = dma("sync", dsth, srch,
                                 waits=[("sT", lastmm[b])] if k0 == 0 else [])
                    for hb in range(HP):
                        pk = [self.p0, self.p1, self.p2, self.p3][hb % 4]
                        for k in range(IT):
                            op("tensor", lambda e, b=b, hb=hb, k=k, pk=pk: e.matmul(
                                pk[:, 0:512], wdt[k][:, 128 * hb:128 * (hb + 1)],
                                hqc[b][:, 512 * k:512 * (k + 1)],
                                start=(k == 0), stop=(k == IT - 1), skip_group_check=True),
                                waits=([("sLD", max(dH, dW)), ("sV", lastrs[hb % 2])]
                                       if k == 0 else []), inc="sT")
                        lastmm[b] = C["sT"]
                        bb = hb % 2
                        sE = op("scalar", lambda e, hb=hb, bb=bb, pk=pk: e.activation(
                            dt_[bb][:], pk[:, 0:512], AF.Identity,
                            scale=self.sdx[:, hb:hb + 1]),
                            waits=[("sT", lastmm[b]), ("sV", lastrs[bb])], inc="sS")
                        lastrs[bb] = op("vector", lambda e, ts=ts, hb=hb, bb=bb: e.tensor_tensor(
                            self.XR[hb][:, 512 * ts:512 * (ts + 1)],
                            self.XR[hb][:, 512 * ts:512 * (ts + 1)], dt_[bb][:], ALU.add),
                            waits=[("sS", sE)], inc="sV")
                self.flush()

    def phase2a(self):
        P, D = self.P, self.D
        op, dma = self.op, self.dma
        C = self.cnt
        with ExitStack() as ph:
            xnb = [self.sb(ph, f"xnb{p}", [128, TME], BF16) for p in range(HP)]
            xt = [self.sb(ph, f"x2t{i}", [128, 512], F32) for i in range(2)]
            self.begin()
            sR = self.stats_rows(self.XR)
            vPrev = 0
            for ts in range(4):
                tB = self.bcast_ts(ts, [("sS", sR), ("sV", vPrev)])
                for hp in range(HP):
                    b = hp % 2
                    op("vector", lambda e, ts=ts, hp=hp, b=b: e.tensor_tensor(
                        xt[b][:], self.XR[hp][:, 512 * ts:512 * (ts + 1)],
                        self.p2[:, 0:512], ALU.subtract),
                        waits=[("sT", tB)] if hp == 0 else [], inc="sV")
                    op("vector", lambda e, ts=ts, hp=hp, b=b: e.tensor_tensor(
                        xnb[hp][:, 512 * ts:512 * (ts + 1)], xt[b][:], self.p3[:, 0:512],
                        ALU.mult), inc="sV")
                vPrev = C["sV"]
                if ts in (1, 3):
                    half = "A" if ts == 1 else "B"
                    col = 0 if ts == 1 else TME // 2
                    dh = 0
                    for hp in range(HP):
                        dh = dma("sync", D[f"xn2_b{half}"].ap()[128 * hp:128 * (hp + 1), :],
                                 xnb[hp][:, col:col + TME // 2], waits=[("sV", vPrev)])
                    op("gpsimd", lambda e, half=half: e.collective_compute(
                        "AllGather", ALU.bypass, replica_groups=G4,
                        ins=[D[f"xn2_b{half}"].ap()], outs=[D[f"xn2_o{half}"].ap()]),
                        waits=[("sLD", dh)], inc="sCC", by=1)
            for hp in range(HP):
                dma("sync", D["x1_sp"].ap()[128 * hp:128 * (hp + 1), :], self.XR[hp][:])
            self.flush()

    def phase2b(self):
        nc, P, D = self.nc, self.P, self.D
        op, dma = self.op, self.dma
        C = self.cnt
        ccAGB = C["sCC"]
        ccAGA = ccAGB - 1
        with ExitStack() as ph:
            sb = self.sb
            ipw = [sb(ph, f"ipw{k}", [128, 1024], BF16) for k in range(HP)]
            xin = [sb(ph, f"xin{k}", [128, CK], BF16) for k in range(HP)]
            xc = [sb(ph, f"xc{p}", [128, CK + 3], BF16) for p in range(DP)]
            zb = [sb(ph, f"zb{i}", [128, CK], BF16) for i in range(2)]
            uc = [sb(ph, f"uc{i}", [128, CK], BF16) for i in range(2)]
            xdr = sb(ph, "xdr", [80, CK], F32)
            self.begin()
            dI = 0
            for k in range(HP):
                dI = dma("gpsimd", ipw[k][:], P["ipwT"].ap()[128 * k:128 * (k + 1), :])
            for p in range(DP):
                op("vector", lambda e, p=p: e.memset(xc[p][:, 0:3], 0.0), inc="sV")
            vHead = C["sV"]
            lastx = 0
            lastxc = [0] * DP
            lastz = [0, 0]
            lastu = [0, 0]
            lastuT = [0, 0]
            lastxd = 0
            lastxdS = 0
            prevheads = vHead
            for ck in range(NCK):
                r = (CK * ck) // TME
                off = (CK * ck) % TME
                dX = 0
                srcT = D["xn2_oA"] if ck % 2 == 0 else D["xn2_oB"]
                ccW = ccAGA if ck % 2 == 0 else ccAGB
                for k in range(HP):
                    dX = dma("sync", xin[k][:],
                             srcT.ap()[H * r + 128 * k: H * r + 128 * (k + 1), :],
                             waits=[("sT", lastx), ("sCC", ccW), ("sGD", dI)])
                if ck > 0:
                    for p in range(DP):
                        op("vector", lambda e, p=p: e.tensor_copy(
                            xc[p][:, 0:3], xc[p][:, CK:CK + 3]),
                            waits=[("sT", lastxc[p])], inc="sV")
                    prevheads = C["sV"]
                for po in range(8):
                    for ts in range(CK // 512):
                        pt = [self.p0, self.p1][ts]
                        for k in range(HP):
                            op("tensor", lambda e, po=po, ts=ts, k=k, pt=pt: e.matmul(
                                pt[:, 0:512], ipw[k][:, 128 * po:128 * (po + 1)],
                                xin[k][:, 512 * ts:512 * (ts + 1)],
                                start=(k == 0), stop=(k == HP - 1), skip_group_check=True),
                                waits=([("sLD", dX), ("sS", C["sS"])] if k == 0 else []),
                                inc="sT")
                        tIP = C["sT"]
                        if po < DP:
                            op("scalar", lambda e, po=po, ts=ts, pt=pt: e.activation(
                                xc[po][:, 3 + 512 * ts:3 + 512 * (ts + 1)], pt[:, 0:512],
                                AF.Identity),
                                waits=[("sT", tIP), ("sV", prevheads)], inc="sS")
                        else:
                            p = po - DP
                            b = ck % 2
                            op("scalar", lambda e, ts=ts, pt=pt, b=b: e.activation(
                                zb[b][:, 512 * ts:512 * (ts + 1)], pt[:, 0:512], AF.Silu),
                                waits=[("sT", tIP), ("sLD", lastz[b])], inc="sS")
                            if ts == CK // 512 - 1:
                                sZ = C["sS"]
                                lastz[b] = dma(
                                    "sync", D["z_sp"].ap()[128 * p:128 * (p + 1),
                                                           CK * ck:CK * (ck + 1)],
                                    zb[b][:], waits=[("sS", sZ)])
                    if po == HP - 1:
                        lastx = C["sT"]
                sIP = C["sS"]
                for p in range(DP):
                    b = (ck + p) % 2
                    for ts in range(CK // 512):
                        for kk in range(4):
                            op("tensor", lambda e, p=p, ts=ts, kk=kk: e.matmul(
                                self.p2[:, 0:512],
                                self.DGT[:, (p * 4 + kk) * 128:(p * 4 + kk + 1) * 128],
                                xc[p][:, 512 * ts + kk: 512 * ts + kk + 512],
                                start=(kk == 0), stop=(kk == 3), skip_group_check=True),
                                waits=([("sS", sIP), ("sS", C["sS"])] if kk == 0 else []),
                                inc="sT")
                        tCV = C["sT"]
                        op("scalar", lambda e, p=p, ts=ts, b=b: e.activation(
                            uc[b][:, 512 * ts:512 * (ts + 1)], self.p2[:, 0:512], AF.Silu,
                            bias=self.cbD_sb[:, p:p + 1]),
                            waits=[("sT", tCV), ("sLD", lastu[b]), ("sT", lastuT[b])],
                            inc="sS")
                    lastxc[p] = C["sT"]
                    sCV = C["sS"]
                    lastu[b] = dma("sync",
                                   D["u_sp"].ap()[128 * p:128 * (p + 1), CK * ck:CK * (ck + 1)],
                                   uc[b][:], waits=[("sS", sCV)])
                    for ts in range(CK // 512):
                        op("tensor", lambda e, p=p, ts=ts, b=b: e.matmul(
                            self.pw[0:80, 512 * ts:512 * (ts + 1)], self.xpw_sb[p][:, 0:80],
                            uc[b][:, 512 * ts:512 * (ts + 1)],
                            start=(p == 0), stop=(p == DP - 1), skip_group_check=True),
                            waits=([("sS", sCV), ("sS", lastxdS)] if p == 0 else
                                   [("sS", sCV)] if ts == 0 else []),
                            inc="sT")
                    lastuT[b] = C["sT"]
                tXD = C["sT"]
                sXD = op("scalar", lambda e: e.activation(
                    xdr[0:80, 0:512], self.pw[0:80, 0:512], AF.Identity),
                    waits=[("sT", tXD)], inc="sS")
                sXD2 = op("scalar", lambda e: e.activation(
                    xdr[0:80, 512:1024], self.pw[0:80, 512:1024], AF.Identity),
                    waits=[("sT", tXD)], inc="sS")
                lastxdS = sXD2
                lastxd = dma("sync", D["xdb_b"].ap()[0:80, CK * ck:CK * (ck + 1)],
                             xdr[0:80, :], waits=[("sS", sXD2), ("sLD", lastxd)])
            op("gpsimd", lambda e: e.collective_compute(
                "AllReduce", ALU.add, replica_groups=G4,
                ins=[D["xdb_b"].ap()], outs=[D["xdb_o"].ap()]),
                waits=[("sLD", lastxd)], inc="sCC", by=1)
            self.flush()

    def phase2c(self):
        nc, P, D = self.nc, self.P, self.D
        op, dma = self.op, self.dma
        C = self.cnt
        ccAR = C["sCC"]
        with ExitStack() as ph:
            sb = self.sb
            dtlr = sb(ph, "dtlr", [64, CK], BF16)
            bcn = [sb(ph, f"bcn{i}", [1, 2 * CK], BF16) for i in range(2)]
            uch = [sb(ph, f"uch{p}", [128, CK], BF16) for p in range(DP)]
            zch = [sb(ph, f"zch{i}", [128, CK], BF16) for i in range(2)]
            delta = [sb(ph, f"dlt{p}", [128, CK], BF16) for p in range(DP)]
            du = [sb(ph, f"du{p}", [128, CK], BF16) for p in range(DP)]
            sa = sb(ph, "sa", [128, CK], F32)
            sr = sb(ph, "sr", [128, CK], F32)
            Bb = [sb(ph, f"Bb{i}", [128, CK], BF16) for i in range(2)]
            Cb = [sb(ph, f"Cb{i}", [128, CK], F32) for i in range(2)]
            dab = [sb(ph, f"dab{i}", [128, CK], F32) for i in range(2)]
            Xb = [sb(ph, f"Xsc{i}", [128, CK], BF16) for i in range(2)]
            hb = [sb(ph, f"hsc{i}", [128, CK], F32) for i in range(2)]
            tmp = sb(ph, "ytmp", [128, CK], F32)
            yac = [sb(ph, f"yac{p}", [128, CK], F32) for p in range(DP)]
            self.begin()
            self.fence(("sLD", "sGD"))
            lastdt = 0
            lastBC = [0, 0]
            lastCC = [0, 0]
            lastDA = [0, 0]
            lastX = [0, 0]
            lastH = [0, 0]
            lastld = 0
            lastvch = 0
            lastz2 = [0, 0]
            lastbcn = [0, 0]
            carrycnt = {}
            for ck in range(NCK):
                dD = dma("gpsimd", dtlr[:], D["xdb_o"].ap()[0:64, CK * ck:CK * (ck + 1)],
                         waits=[("sCC", ccAR), ("sT", lastld)])

                dU = 0
                for p in range(DP):
                    dU = dma("sync", uch[p][:],
                             D["u_sp"].ap()[128 * p:128 * (p + 1), CK * ck:CK * (ck + 1)],
                             waits=[("sV", lastvch)])
                vDel = []
                prevVD = 0
                for p in range(DP):
                    for ts in range(CK // 512):
                        op("tensor", lambda e, p=p, ts=ts: e.matmul(
                            self.pw[:, 512 * ts:512 * (ts + 1)], self.dtw_sb[0:64,
                            128 * p:128 * (p + 1)], dtlr[0:64, 512 * ts:512 * (ts + 1)],
                            start=True, stop=True),
                            waits=[("sGD", dD), ("sS", lastdt)] if ts == 0 else [], inc="sT")
                    tDT = C["sT"]
                    s1 = op("scalar", lambda e, p=p: e.activation(
                        sa[:], self.pw[:, 0:CK], AF.Abs, bias=self.dtb_sb[:, p:p + 1]),
                        waits=[("sT", tDT), ("sV", prevVD)], inc="sS")
                    s2 = op("scalar", lambda e, p=p: e.activation(
                        sr[:], self.pw[:, 0:CK], AF.Relu, bias=self.dtb_sb[:, p:p + 1]),
                        inc="sS")
                    lastdt = s2
                    s3 = op("scalar", lambda e: e.activation(sa[:], sa[:], AF.Exp, scale=-1.0),
                            inc="sS")
                    s4 = op("scalar", lambda e: e.activation(sa[:], sa[:], AF.Ln, bias=1.0),
                            inc="sS")
                    vD = op("vector", lambda e, p=p: e.tensor_tensor(
                        delta[p][:], sr[:], sa[:], ALU.add), waits=[("sS", s4)], inc="sV")
                    vDu = op("vector", lambda e, p=p: e.tensor_tensor(
                        du[p][:], delta[p][:], uch[p][:], ALU.mult),
                        waits=[("sLD", dU)], inc="sV")
                    prevVD = vDu
                    vDel.append((vD, vDu))
                prevBCevac = 0
                for n in range(NST):
                    nb = n % 2
                    dB = dma("gpsimd", bcn[nb][0:1, :].rearrange("p (r l) -> p r l", l=CK),
                             bass.AP(D["xdb_o"].ap().tensor, (64 + n) * TB + CK * ck,
                                     [[0, 1], [8 * TB, 2], [1, CK]]),
                             waits=[("sT", lastbcn[nb]), ("sCC", ccAR)])
                    op("tensor", lambda e, nb=nb: e.matmul(
                        self.p2[:, 0:512], self.onesb[0:1, :],
                        bcn[nb][0:1, 0:512], start=True, stop=True),
                        waits=[("sGD", dB), ("sV", lastBC[nb]), ("sV", lastCC[nb]),
                               ("sS", prevBCevac)], inc="sT")
                    op("tensor", lambda e, nb=nb: e.matmul(
                        self.p3[:, 0:512], self.onesb[0:1, :],
                        bcn[nb][0:1, 512:CK], start=True, stop=True),
                        inc="sT")
                    tBC = C["sT"]
                    op("scalar", lambda e, nb=nb: e.activation(
                        Bb[nb][:, 0:512], self.p2[:, 0:512], AF.Identity),
                        waits=[("sT", tBC)], inc="sS")
                    sB = op("scalar", lambda e, nb=nb: e.activation(
                        Bb[nb][:, 512:CK], self.p3[:, 0:512], AF.Identity), inc="sS")
                    op("tensor", lambda e, nb=nb: e.matmul(
                        self.p2[:, 0:512], self.onesb[0:1, :],
                        bcn[nb][0:1, CK:CK + 512], start=True, stop=True),
                        waits=[("sS", sB)], inc="sT")
                    lastbcn[nb] = op("tensor", lambda e, nb=nb: e.matmul(
                        self.p3[:, 0:512], self.onesb[0:1, :],
                        bcn[nb][0:1, CK + 512:2 * CK], start=True, stop=True),
                        inc="sT")
                    tCC = C["sT"]
                    op("scalar", lambda e, nb=nb: e.activation(
                        Cb[nb][:, 0:512], self.p2[:, 0:512], AF.Identity),
                        waits=[("sT", tCC)], inc="sS")
                    sCr2 = op("scalar", lambda e, nb=nb: e.activation(
                        Cb[nb][:, 512:CK], self.p3[:, 0:512], AF.Identity), inc="sS")
                    prevBCevac = sCr2
                    for p in range(DP):
                        pb = (n * DP + p) % 2
                        vD, vDu = vDel[p]
                        sDA = op("scalar", lambda e, n=n, p=p, pb=pb: e.activation(
                            dab[pb][:], delta[p][:], AF.Exp,
                            scale=self.A_sb[:, (p * 8 + n):(p * 8 + n) + 1]),
                            waits=[("sV", vD), ("sV", lastDA[pb])], inc="sS")
                        vX = op("vector", lambda e, p=p, pb=pb, nb=nb: e.tensor_tensor(
                            Xb[pb][:], du[p][:], Bb[nb][:], ALU.mult),
                            waits=[("sV", vDu), ("sS", sB), ("sV", lastX[pb])], inc="sV")
                        lastBC[nb] = vX
                        idx = n * DP + p
                        wv = [("sS", sDA), ("sV", lastH[pb])]
                        if idx in carrycnt:
                            wv.append(("sV", carrycnt[idx]))
                        vSc = op("vector", lambda e, pb=pb, idx=idx: e.tensor_tensor_scan(
                            hb[pb][:], dab[pb][:], Xb[pb][:],
                            self.carry[:, idx:idx + 1], ALU.mult, ALU.add),
                            waits=wv, inc="sV")
                        lastDA[pb] = vSc
                        lastX[pb] = vSc
                        carrycnt[idx] = op("vector", lambda e, pb=pb, idx=idx: e.tensor_copy(
                            self.carry[:, idx:idx + 1], hb[pb][:, CK - 1:CK]),
                            waits=[("sV", vSc)], inc="sV")
                        if n == 0:
                            vY = op("vector", lambda e, p=p, pb=pb, nb=nb:
                                    e.tensor_tensor(
                                        yac[p][:], hb[pb][:], Cb[nb][:], ALU.mult),
                                    waits=[("sS", sCr2)], inc="sV")
                        else:
                            op("vector", lambda e, pb=pb, nb=nb: e.tensor_tensor(
                                tmp[:], hb[pb][:], Cb[nb][:], ALU.mult),
                                waits=[("sS", sCr2)], inc="sV")
                            vY = op("vector", lambda e, p=p: e.tensor_tensor(
                                yac[p][:], yac[p][:], tmp[:], ALU.add),
                                inc="sV")
                        lastH[pb] = vY
                        lastCC[nb] = vY
                for p in range(DP):
                    dZ = dma("sync", zch[p % 2][:],
                             D["z_sp"].ap()[128 * p:128 * (p + 1), CK * ck:CK * (ck + 1)],
                             waits=[("sV", lastz2[p % 2])])
                    op("vector", lambda e, p=p: e.tensor_scalar(
                        tmp[:], uch[p][:], self.cbD_sb[:, 4 + p:5 + p], None, ALU.mult),
                        inc="sV")
                    op("vector", lambda e, p=p: e.tensor_tensor(
                        yac[p][:], yac[p][:], tmp[:], ALU.add), inc="sV")
                    lastz2[p % 2] = op("vector", lambda e, p=p, ck=ck: e.tensor_tensor(
                        self.yall[p][:, CK * ck:CK * (ck + 1)],
                        yac[p][:], zch[p % 2][:], ALU.mult),
                        waits=[("sLD", dZ)], inc="sV")
                lastvch = C["sV"]
                lastld = C["sT"]
            self.flush()

    def phase2d(self):
        nc, P, D = self.nc, self.P, self.D
        op, dma = self.op, self.dma
        C = self.cnt
        with ExitStack() as ph:
            opw = [self.sb(ph, f"opw{p}", [128, 1024], BF16) for p in range(DP)]
            ob = [self.sb(ph, f"opb{i}", [128, TME], BF16) for i in range(2)]
            self.begin()
            self.fence(("sLD", "sGD"))
            if self.DEBUG_TAPS:
                if "y_tap" not in D:
                    D["y_tap"] = nc.dram_tensor("y_tap", [DIN, TB], BF16)
                    P["tap_y"] = nc.declare_dram_parameter("tap_y", [DIN, TB], BF16,
                                                           isOutput=True)
                for p in range(DP):
                    dma("sync", D["y_tap"].ap()[128 * p:128 * (p + 1), :], self.yall[p][:])
            dO = 0
            for p in range(DP):
                dO = dma("gpsimd", opw[p][:], P["opwT"].ap()[128 * p:128 * (p + 1), :])
            lastst = [0, 0]
            lastev = 0
            for tb in range(4):
                for hb in range(HP):
                    b = (tb * HP + hb) % 2
                    for ts in range(4):
                        pk = [self.p0, self.p1, self.p2, self.p3][ts]
                        for p in range(DP):
                            op("tensor", lambda e, tb=tb, hb=hb, ts=ts, p=p, pk=pk: e.matmul(
                                pk[:, 0:512], opw[p][:, 128 * hb:128 * (hb + 1)],
                                self.yall[p][:, TME * tb + 512 * ts:TME * tb + 512 * (ts + 1)],
                                start=(p == 0), stop=(p == DP - 1), skip_group_check=True),
                                waits=([("sGD", dO), ("sS", lastev)] if (ts == 0 and p == 0)
                                       else []), inc="sT")
                    tMM = C["sT"]
                    for ts in range(4):
                        pk = [self.p0, self.p1, self.p2, self.p3][ts]
                        op("scalar", lambda e, b=b, ts=ts, pk=pk: e.activation(
                            ob[b][:, 512 * ts:512 * (ts + 1)], pk[:, 0:512], AF.Identity),
                            waits=[("sT", tMM), ("sLD", lastst[b])], inc="sS")
                    sEv = C["sS"]
                    lastev = sEv
                    lastst[b] = dma("sync",
                                    D["prt_b"].ap()[H * tb + 128 * hb: H * tb + 128 * (hb + 1), :],
                                    ob[b][:], waits=[("sS", sEv)])
            op("gpsimd", lambda e: e.collective_compute(
                "ReduceScatter", ALU.add, replica_groups=G4,
                ins=[D["prt_b"].ap()], outs=[D["prt_o"].ap()]),
                waits=[("sLD", max(lastst))], inc="sCC", by=1)
            self.flush()

    def phase3_resid(self):
        P, D = self.P, self.D
        op, dma = self.op, self.dma
        C = self.cnt
        ccRS = C["sCC"]
        with ExitStack() as ph:
            rb = [self.sb(ph, f"rb{i}", [128, TME], BF16) for i in range(2)]
            self.begin()
            self.fence(("sLD", "sGD"))
            lastadd = [0, 0]
            for hp in range(HP):
                b = hp % 2
                dX = dma("sync", self.XR[hp][:], D["x1_sp"].ap()[128 * hp:128 * (hp + 1), :])
                dR = dma("sync", rb[b][:], D["prt_o"].ap()[128 * hp:128 * (hp + 1), :],
                         waits=[("sCC", ccRS), ("sV", lastadd[b])])
                lastadd[b] = op("vector", lambda e, hp=hp, b=b: e.tensor_tensor(
                    self.XR[hp][:], self.XR[hp][:], rb[b][:], ALU.add),
                    waits=[("sLD", dR)], inc="sV")
            self.flush()

    def phase4_out(self):
        P = self.P
        op, dma = self.op, self.dma
        with ExitStack() as ph:
            obuf = [self.sb(ph, f"ob4{i}", [128, H], F32) for i in range(2)]
            self.begin()
            slots = [self.p0, self.p1]
            slot_ev = [0, 0]
            lastdma = [0, 0]
            for tt in range(16):
                b = tt % 2
                for hp in range(HP):
                    sl = (tt * HP + hp) % 2
                    tT = op("tensor", lambda e, tt=tt, hp=hp, sl=sl: e.transpose(
                        slots[sl][:, 0:128], self.XR[hp][:, 128 * tt:128 * (tt + 1)],
                        self.idf_sb[:]), waits=[("sS", slot_ev[sl])], inc="sT")
                    slot_ev[sl] = op("scalar", lambda e, hp=hp, b=b, sl=sl: e.activation(
                        obuf[b][:, 128 * hp:128 * (hp + 1)], slots[sl][:, 0:128],
                        AF.Identity), waits=[("sT", tT), ("sLD", lastdma[b])], inc="sS")
                lastdma[b] = dma("sync", P["out"].ap()[128 * tt:128 * (tt + 1), :], obuf[b][:],
                                 waits=[("sS", slot_ev[(tt * HP + HP - 1) % 2])])
            fin = self.cnt["sLD"]
            op("sync", lambda e: None, waits=[("sLD", fin)])
            self.flush()


_NC_CACHE = None


def _get_nc():
    global _NC_CACHE
    if _NC_CACHE is None:
        _NC_CACHE = B().build()
    return _NC_CACHE


def kernel(**inp):
    nc = _get_nc()
    xs = np.ascontiguousarray(inp["L0_attn_hidden_states"].reshape(16384, 1024))
    ipw = inp["L0_attn_in_proj_w"]
    cw = inp["L0_attn_conv_w"]
    cb = inp["L0_attn_conv_b"]
    xpw = inp["L0_attn_x_proj_w"]
    dtw = inp["L0_attn_dt_proj_w"]
    dtb = inp["L0_attn_dt_proj_b"]
    alog = inp["L0_attn_A_log"]
    dD = inp["L0_attn_D"]
    opw = inp["L0_attn_out_proj_w"]
    U = inp["L0_attn_U_param"]
    V = inp["L0_attn_V_param"]
    lrs = inp["L0_attn_low_rank_scale"]
    f32 = np.float32
    in_maps = []
    for c in range(8):
        q = c % 4
        dsl = slice(512 * q, 512 * (q + 1))
        zsl = slice(2048 + 512 * q, 2048 + 512 * (q + 1))
        m = {
            "x0": np.ascontiguousarray(xs[2048 * c:2048 * (c + 1)], f32),
            "wg": np.ascontiguousarray(inp["L0_ffn_w_gate"], f32),
            "wu": np.ascontiguousarray(inp["L0_ffn_w_up"], f32),
            "wd": np.ascontiguousarray(inp["L0_ffn_w_down"], f32),
            "clip2": np.ascontiguousarray(
                np.array([[inp["L0_ffn_clip_x"][0], inp["L0_ffn_clip_down"][0]]], f32)),
            "ipwT": np.ascontiguousarray(
                np.concatenate([ipw[dsl], ipw[zsl]], axis=0).T, f32),
            "cw4": np.ascontiguousarray(
                cw[dsl].reshape(4, 128, 4).transpose(1, 0, 2).reshape(128, 16), f32),
            "cbD": np.ascontiguousarray(
                np.concatenate([cb[dsl].reshape(4, 128).T, dD[dsl].reshape(4, 128).T],
                               axis=1), f32),
            "dtb4": np.ascontiguousarray(dtb[dsl].reshape(4, 128).T, f32),
            "alog": np.ascontiguousarray(
                alog[dsl].reshape(4, 128, 8).transpose(1, 0, 2).reshape(128, 32), f32),
            "uT": np.ascontiguousarray(U[dsl].T, f32),
            "vprm": np.ascontiguousarray(V, f32),
            "lrs1": np.ascontiguousarray(np.asarray(lrs, f32).reshape(1, 1)),
            "xpwT": np.ascontiguousarray(xpw[:, dsl].T, f32),
            "dtwT": np.ascontiguousarray(dtw[dsl].T, f32),
            "opwT": np.ascontiguousarray(opw[:, dsl].T, f32),
            "idf": np.eye(128, dtype=f32),
        }
        in_maps.append(m)
    res = run_bass_kernel_spmd(nc, in_maps, core_ids=list(range(8)))
    out = np.concatenate([np.asarray(res.results[c]["out"]) for c in range(8)], axis=0)
    return out.reshape(2, 8192, 1024).astype(np.float32)


# version 38
# speedup vs baseline: 1.0010x; 1.0007x over previous
import math
import numpy as np
import ml_dtypes
from contextlib import ExitStack
import concourse.bass as bass
import concourse.mybir as mybir
from concourse.bass_utils import run_bass_kernel_spmd

F32 = mybir.dt.float32
BF16 = mybir.dt.bfloat16
AF = mybir.ActivationFunctionType
ALU = mybir.AluOpType
AX = mybir.AxisListType

MAGIC = float(np.float32(1.5 * 2 ** 23))
EPS = 1e-5
QMAX = 127.0
G8 = [list(range(8))]
G4 = [[0, 1, 2, 3], [4, 5, 6, 7]]

H = 1024
HP = 8
TME = 2048
IT = 22
DIN = 512
DP = 4
NST = 8
TB = 8192
CK = 1024
NCK = TB // CK


class B:
    DEBUG_TAPS = False
    TRUNCATE = False
    P0_QUANT = True
    lists = None

    def sem(self, name):
        h = self.outer.enter_context(self.nc.semaphore(name))
        self.sems[name] = h
        self.cnt[name] = 0

    def begin(self):
        if self.lists is None:
            self.lists = {k: [] for k in ("sync", "gpsimd", "vector", "scalar", "tensor")}

    def drive(self, *gens):
        gens = list(gens)
        while gens:
            for g in gens[:]:
                try:
                    next(g)
                except StopIteration:
                    gens.remove(g)

    def fence(self, sems):
        for eng in ("sync", "gpsimd", "vector", "scalar", "tensor"):
            self.op(eng, lambda e: None, waits=[(x, 1) for x in sems])

    def op(self, eng, emit, waits=(), inc=None, by=1):
        waits = tuple((s, self.cnt[s] if s in ("sLD", "sGD", "sSD", "sWQ") else v)
                      for (s, v) in waits if v)

        def closure(e, emit=emit, waits=waits, inc=inc, by=by):
            for s, v in waits:
                e.wait_ge(self.sems[s], v)
            inst = emit(e)
            if inc is not None and inst is not None:
                inst.then_inc(self.sems[inc], by)
        self.lists[eng].append(closure)
        if inc is not None:
            self.cnt[inc] += by
            return self.cnt[inc]
        return None

    def dma(self, eng, out, in_, waits=(), sem=None, **kw):
        if sem is None:
            sem = {"sync": "sLD", "gpsimd": "sGD", "scalar": "sSD"}[eng]
        return self.op(eng, lambda e: e.dma_start(out=out, in_=in_, **kw), waits, sem, 16)

    def flush(self):
        with self.nc.Block() as blk:
            for eng in ("sync", "gpsimd", "vector", "scalar", "tensor"):
                ops = self.lists[eng]
                if ops:
                    getattr(blk, eng)(lambda e, ops=tuple(ops): [f(e) for f in ops])
        self.lists = None
        self.nc.all_engine_barrier()

    def sb(self, stack, name, shape, dt):
        return stack.enter_context(self.nc.sbuf_tensor(name, shape, dt))

    def build(self):
        self.nc = bass.Bass("TRN2", target_bir_lowering=False, debug=False)
        nc = self.nc
        self.cnt = {}
        self.sems = {}
        P = {}

        def par(name, shape, dt=F32, out=False):
            P[name] = nc.declare_dram_parameter(name, shape, dt, isOutput=out)

        par("x0", [TME, H])
        par("wg", [2816, H]); par("wu", [2816, H]); par("wd", [H, 2816])
        par("clip2", [1, 2])
        par("ipwT", [H, 1024])
        par("cw4", [128, 16]); par("cbD", [128, 8]); par("dtb4", [128, 4])
        par("alog", [128, 32]); par("uT", [8, 512]); par("vprm", [8, 8])
        par("lrs1", [1, 1])
        par("xpwT", [512, 80]); par("dtwT", [64, 512]); par("opwT", [512, 1024])
        par("idf", [128, 128])
        par("out", [TME, H], out=True)

        D = {}

        def dram(name, shape, dt):
            D[name] = nc.dram_tensor(name, shape, dt)

        dram("wgt_q", [H, 2816], BF16)
        dram("wut_q", [H, 2816], BF16)
        dram("wdt_q", [2816, H], BF16)
        dram("hpre_sp", [2816, TME], F32)
        dram("hq_sp", [2816, TME], BF16)
        dram("x1_sp", [H, TME], F32)
        dram("z_sp", [DIN, TB], BF16)
        dram("u_sp", [DIN, TB], BF16)
        dram("mx_b", [1, 8], F32); dram("mx_o", [1, 8], F32)
        dram("xn2_bA", [H, TME // 2], BF16); dram("xn2_oA", [4 * H, TME // 2], BF16)
        dram("xn2_bB", [H, TME // 2], BF16); dram("xn2_oB", [4 * H, TME // 2], BF16)
        dram("xdb_b", [80, TB], F32); dram("xdb_o", [80, TB], F32)
        dram("prtH0", [2 * H, TME], BF16); dram("prtH1", [2 * H, TME], BF16)
        dram("prt_o0", [H // 2, TME], BF16); dram("prt_o1", [H // 2, TME], BF16)
        self.P, self.D = P, D

        self.outer = ExitStack()
        with self.outer:
            o = self.outer
            for s in ("sLD", "sGD", "sSD", "sCC", "sV", "sS", "sT", "sWQ"):
                self.sem(s)
            sb = self.sb
            self.A_sb = sb(o, "A_sb", [128, 32], F32)
            self.cbD_sb = sb(o, "cbD_sb", [128, 8], F32)
            self.dtb_sb = sb(o, "dtb_sb", [128, 4], F32)
            self.carry = sb(o, "carry", [128, 32], F32)
            self.onesf = sb(o, "onesf", [1, 128], F32)
            self.onesb = sb(o, "onesb", [1, 128], BF16)
            self.onescol = sb(o, "onescol", [128, 1], BF16)
            self.idf_sb = sb(o, "idf_sb", [128, 128], F32)
            self.idb_sb = sb(o, "idb_sb", [128, 128], BF16)
            self.DGT = sb(o, "DGT", [128, 16 * 128], BF16)
            self.sg_c = sb(o, "sg_c", [128, IT], F32)
            self.su_c = sb(o, "su_c", [128, IT], F32)
            self.sd_c = sb(o, "sd_c", [128, HP], F32)
            self.sgx = sb(o, "sgx", [128, IT], F32)
            self.sux = sb(o, "sux", [128, IT], F32)
            self.sdx = sb(o, "sdx", [128, HP], F32)
            self.sig2 = sb(o, "sig2", [1, 2], F32)
            self.gmax = sb(o, "gmax", [1, 1], F32)
            self.rowA = sb(o, "rowA", [1, TME], F32)
            self.rowB = sb(o, "rowB", [1, TME], F32)
            self.rowC = sb(o, "rowC", [1, TME], F32)
            self.col0 = sb(o, "col0", [128, 1], F32)
            self.col1 = sb(o, "col1", [128, 1], F32)
            self.rmax = sb(o, "rmax", [128, 1], F32)
            self.mcol = sb(o, "mcol", [128, 1], F32)
            self.sxc = sb(o, "sxc", [128, 1], F32)
            self.ivc = sb(o, "ivc", [128, 1], F32)
            self.mxw = sb(o, "mxw", [128, 96], F32)
            self.epsc = sb(o, "epsc", [1, 1], F32)
            self.xpw_sb = [sb(o, f"xpw{p}", [128, 80], BF16) for p in range(DP)]
            self.dtw_sb = sb(o, "dtw_sb", [64, 512], BF16)
            self.clip_sb = sb(o, "clip_sb", [1, 2], F32)
            self.p0 = o.enter_context(nc.psum_tensor("p0", [128, 512], F32))
            self.p1 = o.enter_context(nc.psum_tensor("p1", [128, 512], F32))
            self.p2 = o.enter_context(nc.psum_tensor("p2", [128, 512], F32))
            self.p3 = o.enter_context(nc.psum_tensor("p3", [128, 512], F32))
            self.pw = o.enter_context(nc.psum_tensor("pw", [128, 1024], F32))
            self.ptb0 = o.enter_context(nc.psum_tensor("ptb0", [128, 512], BF16))
            self.ptb1 = o.enter_context(nc.psum_tensor("ptb1", [128, 512], BF16))

            ph0 = ExitStack()
            self.begin()
            qg = self.phase0(ph0)
            with ExitStack() as ph:
                self.XR = [sb(ph, f"XR{p}", [128, TME], F32) for p in range(HP)]
                import itertools
                fstack0 = ExitStack()
                XQ0 = [self.sb(fstack0, f"XQ0_{p}", [128, TME], BF16)
                       for p in range(HP)]
                ag = itertools.chain(self.phase_load_x0(),
                                     self.ffn_stageA_gen(0, XQ0))
                self.drive(qg, ag)
                self.flush()
                ph0.close()
                self.ffn_rest(0, XQ0, fstack0)
                self.phase2a()
            if False:
                pass
                if self.DEBUG_TAPS:
                    P["tap_xr0"] = nc.declare_dram_parameter("tap_xr0", [H, TME], F32,
                                                             isOutput=True)
                    P["tap_idf"] = nc.declare_dram_parameter("tap_idf", [128, 128], F32,
                                                             isOutput=True)
                    P["tap_idb"] = nc.declare_dram_parameter("tap_idb", [128, 128], BF16,
                                                             isOutput=True)
                    self.begin()
                    fin = 0
                    for p in range(HP):
                        fin = self.dma("sync", P["tap_xr0"].ap()[128 * p:128 * (p + 1), :],
                                       self.XR[p][:])
                    fin = self.dma("sync", P["tap_idf"].ap(), self.idf_sb[:])
                    fin = self.dma("sync", P["tap_idb"].ap(), self.idb_sb[:])
                    self.op("sync", lambda e: None, waits=[("sLD", fin)])
                    self.flush()
                if self.TRUNCATE:
                    return nc
                self.ffn(app=0)
                self.phase2a()
            with ExitStack() as ph:
                self.yall = [sb(ph, f"yall{p}", [128, TB], BF16) for p in range(DP)]
                self.phase2b()
                self.phase2c()
                self.phase2d()
            with ExitStack() as ph:
                self.XR = [sb(ph, f"XS{p}", [128, TME], F32) for p in range(HP)]
                self.phase3_resid()
                self.ffn(app=1)
                self.phase4_out()
            if self.DEBUG_TAPS:
                taps = [("wgt_q", [H, 2816], BF16), ("wdt_q", [2816, H], BF16),
                        ("x1_sp", [H, TME], F32), ("xn2_o", [4 * H, TME], BF16),
                        ("xdb_o", [80, TB], F32), ("u_sp", [DIN, TB], BF16),
                        ("z_sp", [DIN, TB], BF16), ("prt_o", [H, TME], BF16),
                        ("hpre_sp", [2816, TME], F32), ("hq_sp", [2816, TME], BF16),
                        ("xdb_b", [80, TB], F32), ("prt_b", [4 * H, TME], BF16)]
                for nm, shape, dt in taps:
                    P["tap_" + nm] = nc.declare_dram_parameter(
                        "tap_" + nm, shape, dt, isOutput=True)
                self.begin()
                fin = 0
                for nm, shape, dt in taps:
                    fin = self.dma("sync", P["tap_" + nm].ap(), D[nm].ap())
                if "y_tap" in D:
                    fin = self.dma("sync", P["tap_y"].ap(), D["y_tap"].ap())
                self.op("sync", lambda e: None, waits=[("sLD", fin)])
                self.flush()
        return nc

    def phase0(self, ph):
        nc, P, D = self.nc, self.P, self.D
        op, dma = self.op, self.dma
        C = self.cnt
        if True:
            sb = self.sb
            wbuf = [sb(ph, f"w0b{i}", [128, 2816], F32) for i in range(2)]
            qbf = [sb(ph, f"q0b{i}", [128, 2816], BF16) for i in range(2)]
            qc0p = [sb(ph, f"qc0{i}", [128, 1], F32) for i in range(2)]
            qc1p = [sb(ph, f"qc1{i}", [128, 1], F32) for i in range(2)]
            tbuf = [sb(ph, f"t0b{i}", [128, 2816], BF16) for i in range(2)]
            alog_sb = sb(ph, "alog_sb", [128, 32], F32)
            expa = sb(ph, "expa", [128, 32], F32)
            uT_sb = sb(ph, "uT_sb", [8, 512], F32)
            v_sb = sb(ph, "v_sb", [8, 8], F32)
            lrs_sb = sb(ph, "lrs_sb", [1, 1], F32)
            ccol = sb(ph, "ccol", [128, 1], F32)
            uvev = sb(ph, "uvev", [128, 32], F32)
            iop = sb(ph, "iop", [128, 128], F32)
            iof = sb(ph, "iof", [128, 128], F32)
            mask = sb(ph, "mask", [128, 128], F32)
            cw_sb = sb(ph, "cw_sb", [128, 16], F32)

            for name, dst in [("alog", alog_sb), ("uT", uT_sb), ("vprm", v_sb),
                              ("lrs1", lrs_sb), ("cw4", cw_sb), ("cbD", self.cbD_sb),
                              ("dtb4", self.dtb_sb), ("idf", self.idf_sb),
                              ("clip2", self.clip_sb)]:
                dma("sync", dst[:], P[name].ap())
            ld0 = C["sLD"]
            dma("gpsimd", self.dtw_sb[:], P["dtwT"].ap())
            for p in range(DP):
                dma("gpsimd", self.xpw_sb[p][:], P["xpwT"].ap()[128 * p:128 * (p + 1), :])
            op("vector", lambda e: e.memset(self.onesf[:], 1.0), inc="sV")
            op("vector", lambda e: e.memset(self.onesb[:], 1.0), inc="sV")
            op("vector", lambda e: e.memset(self.onescol[:], 1.0), inc="sV")
            op("vector", lambda e: e.memset(self.carry[:], 0.0), inc="sV")
            op("vector", lambda e: e.memset(self.mcol[:], MAGIC), inc="sV")
            op("vector", lambda e: e.memset(self.epsc[:], EPS), inc="sV")
            vOnes = C["sV"]
            op("vector", lambda e: e.tensor_copy(self.idb_sb[:], self.idf_sb[:]),
               waits=[("sLD", ld0)], inc="sV")
            vIdb = C["sV"]
            op("gpsimd", lambda e: e.iota(iop[:], [[0, 128]], channel_multiplier=1,
                                          allow_small_or_imprecise_dtypes=True), inc="sGD", by=1)
            op("gpsimd", lambda e: e.iota(iof[:], [[1, 128]], channel_multiplier=0,
                                          allow_small_or_imprecise_dtypes=True), inc="sGD", by=1)
            gIota = C["sGD"]
            op("vector", lambda e: e.tensor_tensor(mask[:], iop[:], iof[:], ALU.is_equal),
               waits=[("sGD", gIota)], inc="sV")
            vMask = C["sV"]
            for j in range(16):
                op("vector", lambda e, j=j: e.tensor_scalar(
                    self.DGT[:, j * 128:(j + 1) * 128], mask[:], cw_sb[:, j:j + 1], None,
                    ALU.mult), waits=[("sV", vMask), ("sLD", ld0)], inc="sV")
            op("scalar", lambda e: e.activation(self.sig2[:], self.clip_sb[:], AF.Sigmoid),
               waits=[("sLD", ld0)], inc="sS")
            op("scalar", lambda e: e.activation(expa[:], alog_sb[:], AF.Exp),
               waits=[("sLD", ld0)], inc="sS")
            tC = op("tensor", lambda e: e.matmul(self.p2[0:128, 0:1], self.onesf[0:1, :],
                                                 lrs_sb[0:1, 0:1], start=True, stop=True),
                    waits=[("sLD", ld0), ("sV", vOnes)], inc="sT")
            sC = op("scalar", lambda e: e.activation(ccol[:], self.p2[0:128, 0:1], AF.Identity,
                                                     scale=1.0 / math.sqrt(8.0)),
                    waits=[("sT", tC)], inc="sS")
            for p in range(DP):
                tUV = op("tensor", lambda e, p=p: e.matmul(
                    self.p3[0:128, 0:8], uT_sb[0:8, 128 * p:128 * (p + 1)], v_sb[0:8, 0:8],
                    start=True, stop=True), waits=[("sLD", ld0), ("sS", sC)], inc="sT")
                op("scalar", lambda e, p=p: e.activation(
                    uvev[:, 8 * p:8 * p + 8], self.p3[0:128, 0:8], AF.Identity),
                    waits=[("sT", tUV)], inc="sS")
            sUV = C["sS"]
            op("vector", lambda e: e.scalar_tensor_tensor(
                self.A_sb[:], uvev[:], ccol[:, 0:1], expa[:], ALU.mult, ALU.subtract),
               waits=[("sS", sUV)], inc="sV")
            yield

            specs = [("wg", self.sg_c, D["wgt_q"], 22, H),
                     ("wu", self.su_c, D["wut_q"], 22, H),
                     ("wd", self.sd_c, D["wdt_q"], HP, 2816)]
            if not self.P0_QUANT:
                specs = []
            ptbs = [self.ptb0, self.ptb1]
            lastq = [0, 0]
            slot_ev = [0, 0]
            lastrd = [0, 0]
            prevrdP = [0, 0]
            for wname, scol, qdst, ntile, CW in specs:
                for i in range(ntile):
                    b = i % 2
                    qc0, qc1 = qc0p[b], qc1p[b]
                    dL = dma("sync" if b == 0 else "scalar", wbuf[b][:, 0:CW],
                             P[wname].ap()[128 * i:128 * (i + 1), :],
                             waits=[("sV", lastrd[b])], sem="sWQ")
                    vr1 = op("vector", lambda e, b=b, CW=CW, qc1=qc1: e.tensor_reduce(
                        qc1[:], wbuf[b][:, 0:CW], AX.X, ALU.max,
                        apply_absolute_value=True),
                        waits=[("sWQ", dL), ("sS", prevrdP[b])], inc="sV")
                    vr2 = op("vector", lambda e, qc0=qc0, qc1=qc1: e.reciprocal(
                        qc0[:], qc1[:]), waits=[("sV", vr1)], inc="sV")
                    vr3 = op("vector", lambda e, qc0=qc0: e.tensor_scalar(
                        qc0[:], qc0[:], QMAX, None, ALU.mult),
                        waits=[("sV", vr2)], inc="sV")
                    vI = op("vector", lambda e, scol=scol, i=i, qc1=qc1: e.tensor_scalar(
                        scol[:, i:i + 1], qc1[:], 1.0 / QMAX, None, ALU.mult),
                        waits=[("sV", vr3)], inc="sV")
                    sR = op("scalar", lambda e, b=b, CW=CW, qc0=qc0: e.activation(
                        wbuf[b][:, 0:CW], wbuf[b][:, 0:CW], AF.Identity,
                        scale=qc0[:, 0:1], bias=self.mcol[:, 0:1]), waits=[("sV", vI)], inc="sS")
                    prevrdP[b] = sR
                    vQ = op("vector", lambda e, b=b, CW=CW: e.tensor_scalar(
                        qbf[b][:, 0:CW], wbuf[b][:, 0:CW], MAGIC, None, ALU.subtract),
                        waits=[("sS", sR), ("sWQ", lastq[b])], inc="sV")
                    lastrd[b] = vQ
                    nblk = CW // 128
                    for cb in range(nblk):
                        sl = cb % 2
                        tT = op("tensor", lambda e, b=b, cb=cb, sl=sl: e.transpose(
                            ptbs[sl][:, 0:128], qbf[b][:, 128 * cb:128 * (cb + 1)],
                            self.idb_sb[:]),
                            waits=[("sV", vQ), ("sV", vIdb), ("sS", slot_ev[sl])], inc="sT")
                        slot_ev[sl] = op("scalar", lambda e, b=b, cb=cb, sl=sl: e.activation(
                            tbuf[b][:, 128 * cb:128 * (cb + 1)], ptbs[sl][:, 0:128],
                            AF.Identity),
                            waits=[("sT", tT)] + ([("sWQ", lastq[b])] if cb == 0 else []),
                            inc="sS")
                    CB = 2
                    for c0 in range(0, nblk, CB):
                        cn = min(CB, nblk - c0)
                        src = tbuf[b][:, 128 * c0:128 * (c0 + cn)].rearrange(
                            "p (c l) -> p c l", l=128)
                        dst = bass.AP(qdst, 128 * i + 128 * c0 * qdst.shape[1],
                                      [[qdst.shape[1], 128], [128 * qdst.shape[1], cn],
                                       [1, 128]])
                        lastq[b] = dma("sync", dst, src, sem="sWQ",
                                       waits=[("sS", slot_ev[0]), ("sS", slot_ev[1])])
                    yield

    def phase_load_x0(self):
        P = self.P
        op, dma = self.op, self.dma
        with ExitStack() as ph:
            xbuf = [self.sb(ph, f"x0b{i}", [128, H], F32) for i in range(2)]
            slots = [self.p0, self.p1]
            slot_ev = [0, 0]
            lastuse = [0, 0]
            for tt in range(16):
                yield
                b = tt % 2
                dL = dma("sync" if b == 0 else "scalar",
                         xbuf[b][:], P["x0"].ap()[128 * tt:128 * (tt + 1), :],
                         waits=[("sS", lastuse[b])], sem="sLD")
                for hp in range(HP):
                    sl = (tt * HP + hp) % 2
                    tT = op("tensor", lambda e, b=b, hp=hp, sl=sl: e.transpose(
                        slots[sl][:, 0:128], xbuf[b][:, 128 * hp:128 * (hp + 1)],
                        self.idf_sb[:]), waits=[("sLD", dL), ("sS", slot_ev[sl])], inc="sT")
                    slot_ev[sl] = op("scalar", lambda e, hp=hp, tt=tt, sl=sl: e.activation(
                        self.XR[hp][:, 128 * tt:128 * (tt + 1)], slots[sl][:, 0:128],
                        AF.Identity), waits=[("sT", tT)], inc="sS")
                lastuse[b] = slot_ev[(tt * HP + HP - 1) % 2]
            self._xr_ready = self.cnt["sS"]

    _stats_n = 0

    def stats_rows(self, X):
        op = self.op
        C = self.cnt
        B._stats_n += 1
        sn = B._stats_n
        with ExitStack() as ph:
            xb = [self.sb(ph, f"stb{sn}_{i}", [128, TME], BF16) for i in range(2)]
            lastmm = [0, 0]
            xrdy = getattr(self, "_xr_ready", 0)
            for hp in range(HP):
                b = hp % 2
                vCp = op("vector", lambda e, hp=hp, b=b: e.tensor_copy(xb[b][:], X[hp][:]),
                         waits=[("sT", lastmm[b]), ("sS", xrdy)], inc="sV")
                for ts in range(4):
                    pt = [self.p0, self.p1, self.p2, self.p3][ts]
                    lastmm[b] = op("tensor", lambda e, b=b, ts=ts, hp=hp, pt=pt: e.matmul(
                        pt[0:1, 0:512], self.onescol[:, 0:1],
                        xb[b][:, 512 * ts:512 * (ts + 1)],
                        start=(hp == 0), stop=(hp == HP - 1), skip_group_check=True),
                        waits=[("sV", vCp)], inc="sT")
            tSum = C["sT"]
            for ts in range(4):
                pt = [self.p0, self.p1, self.p2, self.p3][ts]
                op("scalar", lambda e, ts=ts, pt=pt: e.activation(
                    self.rowA[0:1, 512 * ts:512 * (ts + 1)], pt[0:1, 0:512],
                    AF.Identity, scale=1.0 / H), waits=[("sT", tSum)], inc="sS")
            sM = C["sS"]
            lastmm = [0, 0]
            for hp in range(HP):
                b = hp % 2
                sQp = op("scalar", lambda e, hp=hp, b=b: e.activation(
                    xb[b][:], X[hp][:], AF.Square),
                    waits=[("sM_dummy", 0), ("sS", sM if hp < 2 else 0), ("sT", lastmm[b])],
                    inc="sS")
                for ts in range(4):
                    pt = [self.p0, self.p1, self.p2, self.p3][ts]
                    lastmm[b] = op("tensor", lambda e, b=b, ts=ts, hp=hp, pt=pt: e.matmul(
                        pt[0:1, 0:512], self.onescol[:, 0:1],
                        xb[b][:, 512 * ts:512 * (ts + 1)],
                        start=(hp == 0), stop=(hp == HP - 1), skip_group_check=True),
                        waits=[("sS", sQp)], inc="sT")
            tSq = C["sT"]
            for ts in range(4):
                pt = [self.p0, self.p1, self.p2, self.p3][ts]
                op("scalar", lambda e, ts=ts, pt=pt: e.activation(
                    self.rowB[0:1, 512 * ts:512 * (ts + 1)], pt[0:1, 0:512],
                    AF.Identity, scale=1.0 / H), waits=[("sT", tSq)], inc="sS")
            sQ = C["sS"]
            v1 = op("vector", lambda e: e.tensor_tensor(
                self.rowC[0:1, :], self.rowA[0:1, :], self.rowA[0:1, :], ALU.mult),
                waits=[("sS", sQ)], inc="sV")
            v2 = op("vector", lambda e: e.tensor_tensor(
                self.rowB[0:1, :], self.rowB[0:1, :], self.rowC[0:1, :], ALU.subtract),
                inc="sV")
            op("scalar", lambda e: e.activation(self.rowB[0:1, :], self.rowB[0:1, :],
                                                AF.Ln, bias=self.epsc[0:1, 0:1]), waits=[("sV", v2)], inc="sS")
            sR = op("scalar", lambda e: e.activation(self.rowC[0:1, :], self.rowB[0:1, :],
                                                     AF.Exp, scale=-0.5), inc="sS")
            return sR

    def bcast_ts(self, ts, waits):
        op = self.op
        op("tensor", lambda e, ts=ts: e.matmul(
            self.p2[:, 0:512], self.onesf[0:1, :], self.rowA[0:1, 512 * ts:512 * (ts + 1)],
            start=True, stop=True), waits=waits, inc="sT")
        return op("tensor", lambda e, ts=ts: e.matmul(
            self.p3[:, 0:512], self.onesf[0:1, :], self.rowC[0:1, 512 * ts:512 * (ts + 1)],
            start=True, stop=True), inc="sT")

    def global_max_col(self, vReady):
        op, dma = self.op, self.dma
        tT = op("tensor", lambda e: e.transpose(
            self.p2[0:1, 0:128], self.rmax[0:128, 0:1], self.idf_sb[:]),
            waits=[("sV", vReady)], inc="sT")
        vG = op("vector", lambda e: e.tensor_reduce(
            self.gmax[0:1, 0:1], self.p2[0:1, 0:128], AX.X, ALU.max),
            waits=[("sT", tT)], inc="sV")
        dW = dma("sync", self.D["mx_b"].ap()[0:1, 0:1], self.gmax[0:1, 0:1],
                 waits=[("sV", vG)])
        cc = op("gpsimd", lambda e: e.collective_compute(
            "AllReduce", ALU.max, replica_groups=G8,
            ins=[self.D["mx_b"].ap()], outs=[self.D["mx_o"].ap()]),
            waits=[("sLD", dW)], inc="sCC", by=1)
        dR = dma("sync", self.gmax[0:1, 0:1], self.D["mx_o"].ap()[0:1, 0:1],
                 waits=[("sCC", cc)])
        return dR

    def ffn(self, app):
        fstack = ExitStack()
        XQ = [self.sb(fstack, f"XQ{app}_{p}", [128, TME], BF16) for p in range(HP)]
        self.begin()
        self.drive(self.ffn_stageA_gen(app, XQ))
        self.flush()
        self.ffn_rest(app, XQ, fstack)

    def ffn_stageA_gen(self, app, XQ):
        nc, P, D = self.nc, self.P, self.D
        op, dma = self.op, self.dma
        C = self.cnt
        if True:
            with ExitStack() as ph:
                xt = [self.sb(ph, f"fxt{app}_{i}", [128, 512], F32) for i in range(2)]
                qt = [self.sb(ph, f"fqt{app}_{i}", [128, 512], F32) for i in range(2)]
                yield
                sR = self.stats_rows(self.XR)
                yield
                vPrev = 0
                for ts in range(4):
                    yield
                    tB = self.bcast_ts(ts, [("sS", sR), ("sV", vPrev)])
                    for hp in range(HP):
                        b = hp % 2
                        op("vector", lambda e, ts=ts, hp=hp, b=b: e.tensor_tensor(
                            xt[b][:], self.XR[hp][:, 512 * ts:512 * (ts + 1)],
                            self.p2[:, 0:512], ALU.subtract),
                            waits=[("sT", tB)] if hp == 0 else [], inc="sV")
                        op("vector", lambda e, b=b: e.tensor_tensor(
                            xt[b][:], xt[b][:], self.p3[:, 0:512], ALU.mult), inc="sV")
                        op("vector", lambda e, b=b, ts=ts, hp=hp: e.tensor_reduce(
                            self.mxw[:, ts * 8 + hp:ts * 8 + hp + 1], xt[b][:], AX.X,
                            ALU.max, apply_absolute_value=True), inc="sV")
                    vPrev = C["sV"]
                vRm = op("vector", lambda e: e.tensor_reduce(
                    self.rmax[:], self.mxw[:, 0:32], AX.X, ALU.max),
                    waits=[("sV", vPrev)], inc="sV")
                dR = self.global_max_col(vRm)
                v1 = op("vector", lambda e: e.tensor_tensor(
                    self.gmax[0:1, 0:1], self.gmax[0:1, 0:1], self.sig2[0:1, 0:1], ALU.mult),
                    waits=[("sLD", dR)], inc="sV")
                tBc = op("tensor", lambda e: e.matmul(
                    self.p2[0:128, 0:1], self.onesf[0:1, :], self.gmax[0:1, 0:1],
                    start=True, stop=True), waits=[("sV", v1)], inc="sT")
                s1 = op("scalar", lambda e: e.activation(
                    self.sxc[:], self.p2[0:128, 0:1], AF.Identity, scale=1.0 / QMAX),
                    waits=[("sT", tBc)], inc="sS")
                v2 = op("vector", lambda e: e.reciprocal(self.ivc[:], self.sxc[:]),
                        waits=[("sS", s1)], inc="sV")
                lastq = [0, 0]
                for ts in range(4):
                    yield
                    tB = self.bcast_ts(ts, [("sV", C["sV"])])
                    for hp in range(HP):
                        b = hp % 2
                        op("vector", lambda e, ts=ts, hp=hp, b=b: e.tensor_tensor(
                            xt[b][:], self.XR[hp][:, 512 * ts:512 * (ts + 1)],
                            self.p2[:, 0:512], ALU.subtract),
                            waits=([("sT", tB)] if hp == 0 else []) +
                                  [("sS", lastq[b])], inc="sV")
                        vN = op("vector", lambda e, b=b: e.tensor_tensor(
                            xt[b][:], xt[b][:], self.p3[:, 0:512], ALU.mult), inc="sV")
                        sQ = op("scalar", lambda e, b=b: e.activation(
                            qt[b][:], xt[b][:], AF.Identity, scale=self.ivc[:, 0:1],
                            bias=self.mcol[:, 0:1]), waits=[("sV", vN), ("sV", v2)], inc="sS")
                        lastq[b] = sQ
                        op("vector", lambda e, b=b: e.tensor_scalar(
                            qt[b][:], qt[b][:], MAGIC, QMAX, ALU.subtract, ALU.min),
                            waits=[("sS", sQ)], inc="sV")
                        op("vector", lambda e, ts=ts, hp=hp, b=b: e.tensor_scalar(
                            XQ[hp][:, 512 * ts:512 * (ts + 1)], qt[b][:], -QMAX - 1.0, None,
                            ALU.max), inc="sV")
                if self.DEBUG_TAPS and app == 0:
                    P["tap_rows"] = nc.declare_dram_parameter("tap_rows", [2, TME], F32,
                                                              isOutput=True)
                    P["tap_mx"] = nc.declare_dram_parameter("tap_mx", [1, 8], F32,
                                                            isOutput=True)
                    P["tap_xq"] = nc.declare_dram_parameter("tap_xq", [H, TME], BF16,
                                                            isOutput=True)
                    fin = self.dma("sync", P["tap_rows"].ap()[0:1, :], self.rowA[0:1, :])
                    fin = self.dma("sync", P["tap_rows"].ap()[1:2, :], self.rowC[0:1, :])
                    fin = self.dma("sync", P["tap_mx"].ap(), D["mx_o"].ap())
                    for p in range(HP):
                        fin = self.dma("sync", P["tap_xq"].ap()[128 * p:128 * (p + 1), :],
                                       XQ[p][:], waits=[("sV", C["sV"])])
                    self.op("sync", lambda e: None, waits=[("sLD", fin)])

    def ffn_rest(self, app, XQ, fstack):
        nc, P, D = self.nc, self.P, self.D
        op, dma = self.op, self.dma
        C = self.cnt
        if True:
            with ExitStack() as ph:
                gst = [self.sb(ph, f"gst{app}_{i}", [128, 1024], BF16) for i in range(2)]
                ust = [self.sb(ph, f"ust{app}_{i}", [128, 1024], BF16) for i in range(2)]
                gt = [self.sb(ph, f"fgt{app}_{i}", [128, 512], F32) for i in range(2)]
                ut = [self.sb(ph, f"fut{app}_{i}", [128, 512], F32) for i in range(2)]
                hbuf = [self.sb(ph, f"fhb{app}_{i}", [128, TME], F32) for i in range(2)]
                self.begin()
                self.fence(("sWQ", "sLD"))
                op("vector", lambda e: e.tensor_scalar(
                    self.sgx[:, 0:IT], self.sg_c[:, 0:IT], self.sxc[:, 0:1], None,
                    ALU.mult), inc="sV")
                vScl = op("vector", lambda e: e.tensor_scalar(
                    self.sux[:, 0:IT], self.su_c[:, 0:IT], self.sxc[:, 0:1], None,
                    ALU.mult), inc="sV")
                lastmmG = [0, 0]
                lasthd = [0, 0]
                lastTT = [0, 0]
                prevG = 0
                prevU = 0
                for it in range(IT):
                    b = it % 2
                    for k0 in range(0, HP, 2):
                        srcg = bass.AP(D["wgt_q"], 128 * it + 128 * k0 * 2816,
                                       [[2816, 128], [128 * 2816, 2], [1, 128]])
                        dstg = gst[b][:, 128 * k0:128 * (k0 + 2)].rearrange(
                            "p (k l) -> p k l", l=128)
                        dG = dma("sync", dstg, srcg,
                                 waits=([("sT", lastmmG[b]), ("sWQ", 1)] if k0 == 0 else []))
                        srcu = bass.AP(D["wut_q"], 128 * it + 128 * k0 * 2816,
                                       [[2816, 128], [128 * 2816, 2], [1, 128]])
                        dstu = ust[b][:, 128 * k0:128 * (k0 + 2)].rearrange(
                            "p (k l) -> p k l", l=128)
                        dU = dma("sync", dstu, srcu)
                    for ts in range(4):
                        for k in range(HP):
                            op("tensor", lambda e, b=b, ts=ts, k=k: e.matmul(
                                self.p0[:, 0:512], gst[b][:, 128 * k:128 * (k + 1)],
                                XQ[k][:, 512 * ts:512 * (ts + 1)],
                                start=(k == 0), stop=(k == HP - 1), skip_group_check=True),
                                waits=([("sLD", dU), ("sS", prevG)] if k == 0 else []),
                                inc="sT")
                        tG = C["sT"]
                        for k in range(HP):
                            op("tensor", lambda e, b=b, ts=ts, k=k: e.matmul(
                                self.p1[:, 0:512], ust[b][:, 128 * k:128 * (k + 1)],
                                XQ[k][:, 512 * ts:512 * (ts + 1)],
                                start=(k == 0), stop=(k == HP - 1), skip_group_check=True),
                                waits=([("sS", prevU)] if k == 0 else []),
                                inc="sT")
                        tU = C["sT"]
                        lastmmG[b] = tU
                        bb = ts % 2
                        sG = op("scalar", lambda e, it=it, bb=bb: e.activation(
                            gt[bb][:], self.p0[:, 0:512], AF.Silu,
                            scale=self.sgx[:, it:it + 1]),
                            waits=[("sT", tG), ("sV", lastTT[bb]), ("sV", vScl)],
                            inc="sS")
                        prevG = sG
                        sU = op("scalar", lambda e, it=it, bb=bb: e.activation(
                            ut[bb][:], self.p1[:, 0:512], AF.Identity,
                            scale=self.sux[:, it:it + 1]), waits=[("sT", tU)], inc="sS")
                        prevU = sU
                        vH = op("vector", lambda e, b=b, ts=ts, bb=bb: e.tensor_tensor(
                            hbuf[b][:, 512 * ts:512 * (ts + 1)], gt[bb][:], ut[bb][:],
                            ALU.mult), waits=[("sS", sU), ("sLD", lasthd[b])], inc="sV")
                        lastTT[bb] = vH
                        op("vector", lambda e, b=b, ts=ts, it=it: e.tensor_reduce(
                            self.mxw[:, (it * 4 + ts) % 96:(it * 4 + ts) % 96 + 1],
                            hbuf[b][:, 512 * ts:512 * (ts + 1)], AX.X,
                            ALU.max, apply_absolute_value=True), inc="sV")
                    vDone = C["sV"]
                    lasthd[b] = dma("sync", D["hpre_sp"].ap()[128 * it:128 * (it + 1), :],
                                    hbuf[b][:], waits=[("sV", vDone)])
                vRm = op("vector", lambda e: e.tensor_reduce(
                    self.rmax[:], self.mxw[:, 0:88], AX.X, ALU.max),
                    waits=[("sV", C["sV"])], inc="sV")
                dR = self.global_max_col(vRm)
                v1 = op("vector", lambda e: e.tensor_tensor(
                    self.gmax[0:1, 0:1], self.gmax[0:1, 0:1], self.sig2[0:1, 1:2], ALU.mult),
                    waits=[("sLD", dR)], inc="sV")
                tBc = op("tensor", lambda e: e.matmul(
                    self.p2[0:128, 0:1], self.onesf[0:1, :], self.gmax[0:1, 0:1],
                    start=True, stop=True), waits=[("sV", v1)], inc="sT")
                s1 = op("scalar", lambda e: e.activation(
                    self.col1[:], self.p2[0:128, 0:1], AF.Identity, scale=1.0 / QMAX),
                    waits=[("sT", tBc)], inc="sS")
                op("vector", lambda e: e.reciprocal(self.col0[:], self.col1[:]),
                   waits=[("sS", s1)], inc="sV")
                op("vector", lambda e: e.tensor_scalar(
                    self.sdx[:, 0:HP], self.sd_c[:, 0:HP], self.col1[:, 0:1], 0.5,
                    ALU.mult, ALU.mult), inc="sV")
                if self.DEBUG_TAPS and app == 0:
                    P["tap_hpre0"] = nc.declare_dram_parameter(
                        "tap_hpre0", [2816, TME], F32, isOutput=True)
                    P["tap_mxh"] = nc.declare_dram_parameter(
                        "tap_mxh", [1, 8], F32, isOutput=True)
                    fin = dma("sync", P["tap_hpre0"].ap(), D["hpre_sp"].ap())
                    fin = dma("sync", P["tap_mxh"].ap(), D["mx_o"].ap())
                    op("sync", lambda e: None, waits=[("sLD", fin)])
                self.flush()

            fstack.close()

            with ExitStack() as ph:
                hbuf = [self.sb(ph, f"fh2{app}_{i}", [128, TME], F32) for i in range(2)]
                hqb = [self.sb(ph, f"fhq{app}_{i}", [128, TME], BF16) for i in range(2)]
                self.begin()
                self.fence(("sLD",))
                lastV = [0, 0]
                lastW = [0, 0]
                for it in range(IT):
                    b = it % 2
                    dL = dma("sync", hbuf[b][:], D["hpre_sp"].ap()[128 * it:128 * (it + 1), :],
                             waits=[("sV", lastV[b])])
                    sQ = op("scalar", lambda e, b=b: e.activation(
                        hbuf[b][:], hbuf[b][:], AF.Identity, scale=self.col0[:, 0:1],
                        bias=self.mcol[:, 0:1]), waits=[("sLD", dL)], inc="sS")
                    op("vector", lambda e, b=b: e.tensor_scalar(
                        hbuf[b][:], hbuf[b][:], MAGIC, QMAX, ALU.subtract, ALU.min),
                        waits=[("sS", sQ), ("sLD", lastW[b])], inc="sV")
                    vQ = op("vector", lambda e, b=b: e.tensor_scalar(
                        hqb[b][:], hbuf[b][:], -QMAX - 1.0, None, ALU.max), inc="sV")
                    lastV[b] = vQ
                    lastW[b] = dma("sync", D["hq_sp"].ap()[128 * it:128 * (it + 1), :],
                                   hqb[b][:], waits=[("sV", vQ)])
                if self.DEBUG_TAPS and app == 0:
                    P["tap_hq0"] = nc.declare_dram_parameter(
                        "tap_hq0", [2816, TME], BF16, isOutput=True)
                    fin = dma("sync", P["tap_hq0"].ap(), D["hq_sp"].ap())
                    op("sync", lambda e: None, waits=[("sLD", fin)])
                self.flush()

            with ExitStack() as ph:
                wdt = [self.sb(ph, f"wdt{app}_{k}", [128, 1024], BF16) for k in range(IT)]
                hqc = [self.sb(ph, f"hqc{app}_{i}", [128, 22 * 512], BF16) for i in range(2)]
                dt_ = [self.sb(ph, f"fdn{app}_{i}", [128, 512], F32) for i in range(2)]
                self.begin()
                self.fence(("sLD",))
                dW = 0
                for k in range(IT):
                    srcd = bass.AP(D["wdt_q"], 128 * k * H, [[H, 128], [1, H]])
                    dW = dma("sync", wdt[k][:, 0:H], srcd)
                lastmm = [0, 0]
                lastrs = [0, 0]
                for ts in range(4):
                    b = ts % 2
                    for k0 in range(0, IT, 2):
                        kn = min(2, IT - k0)
                        srch = bass.AP(D["hq_sp"], 512 * ts + 128 * k0 * TME,
                                       [[TME, 128], [128 * TME, kn], [1, 512]])
                        dsth = hqc[b][:, 512 * k0:512 * (k0 + kn)].rearrange(
                            "p (k l) -> p k l", l=512)
                        dH = dma("sync", dsth, srch,
                                 waits=[("sT", lastmm[b])] if k0 == 0 else [])
                    for hb in range(HP):
                        pk = [self.p0, self.p1, self.p2, self.p3][hb % 4]
                        for k in range(IT):
                            op("tensor", lambda e, b=b, hb=hb, k=k, pk=pk: e.matmul(
                                pk[:, 0:512], wdt[k][:, 128 * hb:128 * (hb + 1)],
                                hqc[b][:, 512 * k:512 * (k + 1)],
                                start=(k == 0), stop=(k == IT - 1), skip_group_check=True),
                                waits=([("sLD", max(dH, dW)), ("sV", lastrs[hb % 2])]
                                       if k == 0 else []), inc="sT")
                        lastmm[b] = C["sT"]
                        bb = hb % 2
                        sE = op("scalar", lambda e, hb=hb, bb=bb, pk=pk: e.activation(
                            dt_[bb][:], pk[:, 0:512], AF.Identity,
                            scale=self.sdx[:, hb:hb + 1]),
                            waits=[("sT", lastmm[b]), ("sV", lastrs[bb])], inc="sS")
                        lastrs[bb] = op("vector", lambda e, ts=ts, hb=hb, bb=bb: e.tensor_tensor(
                            self.XR[hb][:, 512 * ts:512 * (ts + 1)],
                            self.XR[hb][:, 512 * ts:512 * (ts + 1)], dt_[bb][:], ALU.add),
                            waits=[("sS", sE)], inc="sV")
                self.flush()

    def phase2a(self):
        P, D = self.P, self.D
        op, dma = self.op, self.dma
        C = self.cnt
        with ExitStack() as ph:
            xnb = [self.sb(ph, f"xnb{p}", [128, TME], BF16) for p in range(HP)]
            xt = [self.sb(ph, f"x2t{i}", [128, 512], F32) for i in range(2)]
            self.begin()
            sR = self.stats_rows(self.XR)
            vPrev = 0
            for ts in range(4):
                tB = self.bcast_ts(ts, [("sS", sR), ("sV", vPrev)])
                for hp in range(HP):
                    b = hp % 2
                    op("vector", lambda e, ts=ts, hp=hp, b=b: e.tensor_tensor(
                        xt[b][:], self.XR[hp][:, 512 * ts:512 * (ts + 1)],
                        self.p2[:, 0:512], ALU.subtract),
                        waits=[("sT", tB)] if hp == 0 else [], inc="sV")
                    op("vector", lambda e, ts=ts, hp=hp, b=b: e.tensor_tensor(
                        xnb[hp][:, 512 * ts:512 * (ts + 1)], xt[b][:], self.p3[:, 0:512],
                        ALU.mult), inc="sV")
                vPrev = C["sV"]
                if ts in (1, 3):
                    half = "A" if ts == 1 else "B"
                    col = 0 if ts == 1 else TME // 2
                    dh = 0
                    for hp in range(HP):
                        dh = dma("sync", D[f"xn2_b{half}"].ap()[128 * hp:128 * (hp + 1), :],
                                 xnb[hp][:, col:col + TME // 2], waits=[("sV", vPrev)])
                    op("gpsimd", lambda e, half=half: e.collective_compute(
                        "AllGather", ALU.bypass, replica_groups=G4,
                        ins=[D[f"xn2_b{half}"].ap()], outs=[D[f"xn2_o{half}"].ap()]),
                        waits=[("sLD", dh)], inc="sCC", by=1)
            for hp in range(HP):
                dma("sync", D["x1_sp"].ap()[128 * hp:128 * (hp + 1), :], self.XR[hp][:])
            self.flush()

    def phase2b(self):
        nc, P, D = self.nc, self.P, self.D
        op, dma = self.op, self.dma
        C = self.cnt
        ccAGB = C["sCC"]
        ccAGA = ccAGB - 1
        with ExitStack() as ph:
            sb = self.sb
            ipw = [sb(ph, f"ipw{k}", [128, 1024], BF16) for k in range(HP)]
            xin = [sb(ph, f"xin{k}", [128, CK], BF16) for k in range(HP)]
            xc = [sb(ph, f"xc{p}", [128, CK + 3], BF16) for p in range(DP)]
            zb = [sb(ph, f"zb{i}", [128, CK], BF16) for i in range(2)]
            uc = [sb(ph, f"uc{i}", [128, CK], BF16) for i in range(2)]
            xdr = sb(ph, "xdr", [80, CK], F32)
            self.begin()
            dI = 0
            for k in range(HP):
                dI = dma("gpsimd", ipw[k][:], P["ipwT"].ap()[128 * k:128 * (k + 1), :])
            for p in range(DP):
                op("vector", lambda e, p=p: e.memset(xc[p][:, 0:3], 0.0), inc="sV")
            vHead = C["sV"]
            lastx = 0
            lastxc = [0] * DP
            lastz = [0, 0]
            lastu = [0, 0]
            lastuT = [0, 0]
            lastxd = 0
            lastxdS = 0
            prevheads = vHead
            for ck in range(NCK):
                r = (CK * ck) // TME
                off = (CK * ck) % TME
                dX = 0
                srcT = D["xn2_oA"] if ck % 2 == 0 else D["xn2_oB"]
                ccW = ccAGA if ck % 2 == 0 else ccAGB
                for k in range(HP):
                    dX = dma("sync", xin[k][:],
                             srcT.ap()[H * r + 128 * k: H * r + 128 * (k + 1), :],
                             waits=[("sT", lastx), ("sCC", ccW), ("sGD", dI)])
                if ck > 0:
                    for p in range(DP):
                        op("vector", lambda e, p=p: e.tensor_copy(
                            xc[p][:, 0:3], xc[p][:, CK:CK + 3]),
                            waits=[("sT", lastxc[p])], inc="sV")
                    prevheads = C["sV"]
                for po in range(8):
                    for ts in range(CK // 512):
                        pt = [self.p0, self.p1][ts]
                        for k in range(HP):
                            op("tensor", lambda e, po=po, ts=ts, k=k, pt=pt: e.matmul(
                                pt[:, 0:512], ipw[k][:, 128 * po:128 * (po + 1)],
                                xin[k][:, 512 * ts:512 * (ts + 1)],
                                start=(k == 0), stop=(k == HP - 1), skip_group_check=True),
                                waits=([("sLD", dX), ("sS", C["sS"])] if k == 0 else []),
                                inc="sT")
                        tIP = C["sT"]
                        if po < DP:
                            op("scalar", lambda e, po=po, ts=ts, pt=pt: e.activation(
                                xc[po][:, 3 + 512 * ts:3 + 512 * (ts + 1)], pt[:, 0:512],
                                AF.Identity),
                                waits=[("sT", tIP), ("sV", prevheads)], inc="sS")
                        else:
                            p = po - DP
                            b = ck % 2
                            op("scalar", lambda e, ts=ts, pt=pt, b=b: e.activation(
                                zb[b][:, 512 * ts:512 * (ts + 1)], pt[:, 0:512], AF.Silu),
                                waits=[("sT", tIP), ("sLD", lastz[b])], inc="sS")
                            if ts == CK // 512 - 1:
                                sZ = C["sS"]
                                lastz[b] = dma(
                                    "sync", D["z_sp"].ap()[128 * p:128 * (p + 1),
                                                           CK * ck:CK * (ck + 1)],
                                    zb[b][:], waits=[("sS", sZ)])
                    if po == HP - 1:
                        lastx = C["sT"]
                sIP = C["sS"]
                for p in range(DP):
                    b = (ck + p) % 2
                    for ts in range(CK // 512):
                        for kk in range(4):
                            op("tensor", lambda e, p=p, ts=ts, kk=kk: e.matmul(
                                self.p2[:, 0:512],
                                self.DGT[:, (p * 4 + kk) * 128:(p * 4 + kk + 1) * 128],
                                xc[p][:, 512 * ts + kk: 512 * ts + kk + 512],
                                start=(kk == 0), stop=(kk == 3), skip_group_check=True),
                                waits=([("sS", sIP), ("sS", C["sS"])] if kk == 0 else []),
                                inc="sT")
                        tCV = C["sT"]
                        op("scalar", lambda e, p=p, ts=ts, b=b: e.activation(
                            uc[b][:, 512 * ts:512 * (ts + 1)], self.p2[:, 0:512], AF.Silu,
                            bias=self.cbD_sb[:, p:p + 1]),
                            waits=[("sT", tCV), ("sLD", lastu[b]), ("sT", lastuT[b])],
                            inc="sS")
                    lastxc[p] = C["sT"]
                    sCV = C["sS"]
                    lastu[b] = dma("sync",
                                   D["u_sp"].ap()[128 * p:128 * (p + 1), CK * ck:CK * (ck + 1)],
                                   uc[b][:], waits=[("sS", sCV)])
                    for ts in range(CK // 512):
                        op("tensor", lambda e, p=p, ts=ts, b=b: e.matmul(
                            self.pw[0:80, 512 * ts:512 * (ts + 1)], self.xpw_sb[p][:, 0:80],
                            uc[b][:, 512 * ts:512 * (ts + 1)],
                            start=(p == 0), stop=(p == DP - 1), skip_group_check=True),
                            waits=([("sS", sCV), ("sS", lastxdS)] if p == 0 else
                                   [("sS", sCV)] if ts == 0 else []),
                            inc="sT")
                    lastuT[b] = C["sT"]
                tXD = C["sT"]
                sXD = op("scalar", lambda e: e.activation(
                    xdr[0:80, 0:512], self.pw[0:80, 0:512], AF.Identity),
                    waits=[("sT", tXD)], inc="sS")
                sXD2 = op("scalar", lambda e: e.activation(
                    xdr[0:80, 512:1024], self.pw[0:80, 512:1024], AF.Identity),
                    waits=[("sT", tXD)], inc="sS")
                lastxdS = sXD2
                lastxd = dma("sync", D["xdb_b"].ap()[0:80, CK * ck:CK * (ck + 1)],
                             xdr[0:80, :], waits=[("sS", sXD2), ("sLD", lastxd)])
            op("gpsimd", lambda e: e.collective_compute(
                "AllReduce", ALU.add, replica_groups=G4,
                ins=[D["xdb_b"].ap()], outs=[D["xdb_o"].ap()]),
                waits=[("sLD", lastxd)], inc="sCC", by=1)
            self.flush()

    def phase2c(self):
        nc, P, D = self.nc, self.P, self.D
        op, dma = self.op, self.dma
        C = self.cnt
        ccAR = C["sCC"]
        with ExitStack() as ph:
            sb = self.sb
            dtlr = sb(ph, "dtlr", [64, CK], BF16)
            bcn = [sb(ph, f"bcn{i}", [1, 2 * CK], BF16) for i in range(2)]
            uch = [sb(ph, f"uch{p}", [128, CK], BF16) for p in range(DP)]
            zch = [sb(ph, f"zch{i}", [128, CK], BF16) for i in range(2)]
            delta = [sb(ph, f"dlt{p}", [128, CK], BF16) for p in range(DP)]
            du = [sb(ph, f"du{p}", [128, CK], BF16) for p in range(DP)]
            sa = sb(ph, "sa", [128, CK], F32)
            sr = sb(ph, "sr", [128, CK], F32)
            Bb = [sb(ph, f"Bb{i}", [128, CK], BF16) for i in range(2)]
            Cb = [sb(ph, f"Cb{i}", [128, CK], F32) for i in range(2)]
            dab = [sb(ph, f"dab{i}", [128, CK], F32) for i in range(2)]
            Xb = [sb(ph, f"Xsc{i}", [128, CK], BF16) for i in range(2)]
            hb = [sb(ph, f"hsc{i}", [128, CK], F32) for i in range(2)]
            tmp = sb(ph, "ytmp", [128, CK], F32)
            yac = [sb(ph, f"yac{p}", [128, CK], F32) for p in range(DP)]
            self.begin()
            self.fence(("sLD", "sGD"))
            lastdt = 0
            lastBC = [0, 0]
            lastCC = [0, 0]
            lastDA = [0, 0]
            lastX = [0, 0]
            lastH = [0, 0]
            lastld = 0
            lastvch = 0
            lastz2 = [0, 0]
            lastbcn = [0, 0]
            carrycnt = {}
            for ck in range(NCK):
                dD = dma("gpsimd", dtlr[:], D["xdb_o"].ap()[0:64, CK * ck:CK * (ck + 1)],
                         waits=[("sCC", ccAR), ("sT", lastld)])

                dU = 0
                for p in range(DP):
                    dU = dma("sync", uch[p][:],
                             D["u_sp"].ap()[128 * p:128 * (p + 1), CK * ck:CK * (ck + 1)],
                             waits=[("sV", lastvch)])
                vDel = []
                prevVD = 0
                for p in range(DP):
                    for ts in range(CK // 512):
                        op("tensor", lambda e, p=p, ts=ts: e.matmul(
                            self.pw[:, 512 * ts:512 * (ts + 1)], self.dtw_sb[0:64,
                            128 * p:128 * (p + 1)], dtlr[0:64, 512 * ts:512 * (ts + 1)],
                            start=True, stop=True),
                            waits=[("sGD", dD), ("sS", lastdt)] if ts == 0 else [], inc="sT")
                    tDT = C["sT"]
                    s1 = op("scalar", lambda e, p=p: e.activation(
                        sa[:], self.pw[:, 0:CK], AF.Abs, bias=self.dtb_sb[:, p:p + 1]),
                        waits=[("sT", tDT), ("sV", prevVD)], inc="sS")
                    s2 = op("scalar", lambda e, p=p: e.activation(
                        sr[:], self.pw[:, 0:CK], AF.Relu, bias=self.dtb_sb[:, p:p + 1]),
                        inc="sS")
                    lastdt = s2
                    s3 = op("scalar", lambda e: e.activation(sa[:], sa[:], AF.Exp, scale=-1.0),
                            inc="sS")
                    s4 = op("scalar", lambda e: e.activation(sa[:], sa[:], AF.Ln, bias=1.0),
                            inc="sS")
                    vD = op("vector", lambda e, p=p: e.tensor_tensor(
                        delta[p][:], sr[:], sa[:], ALU.add), waits=[("sS", s4)], inc="sV")
                    vDu = op("vector", lambda e, p=p: e.tensor_tensor(
                        du[p][:], delta[p][:], uch[p][:], ALU.mult),
                        waits=[("sLD", dU)], inc="sV")
                    prevVD = vDu
                    vDel.append((vD, vDu))
                prevBCevac = 0
                for n in range(NST):
                    nb = n % 2
                    dB = dma("gpsimd", bcn[nb][0:1, :].rearrange("p (r l) -> p r l", l=CK),
                             bass.AP(D["xdb_o"].ap().tensor, (64 + n) * TB + CK * ck,
                                     [[0, 1], [8 * TB, 2], [1, CK]]),
                             waits=[("sT", lastbcn[nb]), ("sCC", ccAR)])
                    op("tensor", lambda e, nb=nb: e.matmul(
                        self.p2[:, 0:512], self.onesb[0:1, :],
                        bcn[nb][0:1, 0:512], start=True, stop=True),
                        waits=[("sGD", dB), ("sV", lastBC[nb]), ("sV", lastCC[nb]),
                               ("sS", prevBCevac)], inc="sT")
                    op("tensor", lambda e, nb=nb: e.matmul(
                        self.p3[:, 0:512], self.onesb[0:1, :],
                        bcn[nb][0:1, 512:CK], start=True, stop=True),
                        inc="sT")
                    tBC = C["sT"]
                    op("scalar", lambda e, nb=nb: e.activation(
                        Bb[nb][:, 0:512], self.p2[:, 0:512], AF.Identity),
                        waits=[("sT", tBC)], inc="sS")
                    sB = op("scalar", lambda e, nb=nb: e.activation(
                        Bb[nb][:, 512:CK], self.p3[:, 0:512], AF.Identity), inc="sS")
                    op("tensor", lambda e, nb=nb: e.matmul(
                        self.p2[:, 0:512], self.onesb[0:1, :],
                        bcn[nb][0:1, CK:CK + 512], start=True, stop=True),
                        waits=[("sS", sB)], inc="sT")
                    lastbcn[nb] = op("tensor", lambda e, nb=nb: e.matmul(
                        self.p3[:, 0:512], self.onesb[0:1, :],
                        bcn[nb][0:1, CK + 512:2 * CK], start=True, stop=True),
                        inc="sT")
                    tCC = C["sT"]
                    op("scalar", lambda e, nb=nb: e.activation(
                        Cb[nb][:, 0:512], self.p2[:, 0:512], AF.Identity),
                        waits=[("sT", tCC)], inc="sS")
                    sCr2 = op("scalar", lambda e, nb=nb: e.activation(
                        Cb[nb][:, 512:CK], self.p3[:, 0:512], AF.Identity), inc="sS")
                    prevBCevac = sCr2
                    for p in range(DP):
                        pb = (n * DP + p) % 2
                        vD, vDu = vDel[p]
                        sDA = op("scalar", lambda e, n=n, p=p, pb=pb: e.activation(
                            dab[pb][:], delta[p][:], AF.Exp,
                            scale=self.A_sb[:, (p * 8 + n):(p * 8 + n) + 1]),
                            waits=[("sV", vD), ("sV", lastDA[pb])], inc="sS")
                        vX = op("vector", lambda e, p=p, pb=pb, nb=nb: e.tensor_tensor(
                            Xb[pb][:], du[p][:], Bb[nb][:], ALU.mult),
                            waits=[("sV", vDu), ("sS", sB), ("sV", lastX[pb])], inc="sV")
                        lastBC[nb] = vX
                        idx = n * DP + p
                        wv = [("sS", sDA), ("sV", lastH[pb])]
                        if idx in carrycnt:
                            wv.append(("sV", carrycnt[idx]))
                        vSc = op("vector", lambda e, pb=pb, idx=idx: e.tensor_tensor_scan(
                            hb[pb][:], dab[pb][:], Xb[pb][:],
                            self.carry[:, idx:idx + 1], ALU.mult, ALU.add),
                            waits=wv, inc="sV")
                        lastDA[pb] = vSc
                        lastX[pb] = vSc
                        carrycnt[idx] = op("vector", lambda e, pb=pb, idx=idx: e.tensor_copy(
                            self.carry[:, idx:idx + 1], hb[pb][:, CK - 1:CK]),
                            waits=[("sV", vSc)], inc="sV")
                        if n == 0:
                            vY = op("vector", lambda e, p=p, pb=pb, nb=nb:
                                    e.tensor_tensor(
                                        yac[p][:], hb[pb][:], Cb[nb][:], ALU.mult),
                                    waits=[("sS", sCr2)], inc="sV")
                        else:
                            op("vector", lambda e, pb=pb, nb=nb: e.tensor_tensor(
                                tmp[:], hb[pb][:], Cb[nb][:], ALU.mult),
                                waits=[("sS", sCr2)], inc="sV")
                            vY = op("vector", lambda e, p=p: e.tensor_tensor(
                                yac[p][:], yac[p][:], tmp[:], ALU.add),
                                inc="sV")
                        lastH[pb] = vY
                        lastCC[nb] = vY
                for p in range(DP):
                    dZ = dma("sync", zch[p % 2][:],
                             D["z_sp"].ap()[128 * p:128 * (p + 1), CK * ck:CK * (ck + 1)],
                             waits=[("sV", lastz2[p % 2])])
                    op("vector", lambda e, p=p: e.tensor_scalar(
                        tmp[:], uch[p][:], self.cbD_sb[:, 4 + p:5 + p], None, ALU.mult),
                        inc="sV")
                    op("vector", lambda e, p=p: e.tensor_tensor(
                        yac[p][:], yac[p][:], tmp[:], ALU.add), inc="sV")
                    lastz2[p % 2] = op("vector", lambda e, p=p, ck=ck: e.tensor_tensor(
                        self.yall[p][:, CK * ck:CK * (ck + 1)],
                        yac[p][:], zch[p % 2][:], ALU.mult),
                        waits=[("sLD", dZ)], inc="sV")
                lastvch = C["sV"]
                lastld = C["sT"]
            self.flush()

    def phase2d(self):
        nc, P, D = self.nc, self.P, self.D
        op, dma = self.op, self.dma
        C = self.cnt
        with ExitStack() as ph:
            opw = [self.sb(ph, f"opw{p}", [128, 1024], BF16) for p in range(DP)]
            ob = [self.sb(ph, f"opb{i}", [128, TME], BF16) for i in range(2)]
            self.begin()
            self.fence(("sLD", "sGD"))
            if self.DEBUG_TAPS:
                if "y_tap" not in D:
                    D["y_tap"] = nc.dram_tensor("y_tap", [DIN, TB], BF16)
                    P["tap_y"] = nc.declare_dram_parameter("tap_y", [DIN, TB], BF16,
                                                           isOutput=True)
                for p in range(DP):
                    dma("sync", D["y_tap"].ap()[128 * p:128 * (p + 1), :], self.yall[p][:])
            dO = 0
            for p in range(DP):
                dO = dma("gpsimd", opw[p][:], P["opwT"].ap()[128 * p:128 * (p + 1), :])
            lastst = [0, 0]
            lastev = 0
            idx = 0
            for hh in range(2):
                for tb in range(4):
                    for hq in range(4):
                        hb = 4 * hh + hq
                        b = idx % 2
                        idx += 1
                        for ts in range(4):
                            pk = [self.p0, self.p1, self.p2, self.p3][ts]
                            for p in range(DP):
                                op("tensor", lambda e, tb=tb, hb=hb, ts=ts, p=p, pk=pk:
                                   e.matmul(
                                    pk[:, 0:512], opw[p][:, 128 * hb:128 * (hb + 1)],
                                    self.yall[p][:, TME * tb + 512 * ts:TME * tb + 512 * (ts + 1)],
                                    start=(p == 0), stop=(p == DP - 1), skip_group_check=True),
                                    waits=([("sGD", dO), ("sS", lastev)]
                                           if (ts == 0 and p == 0) else []), inc="sT")
                        tMM = C["sT"]
                        for ts in range(4):
                            pk = [self.p0, self.p1, self.p2, self.p3][ts]
                            op("scalar", lambda e, b=b, ts=ts, pk=pk: e.activation(
                                ob[b][:, 512 * ts:512 * (ts + 1)], pk[:, 0:512], AF.Identity),
                                waits=[("sT", tMM), ("sLD", lastst[b])], inc="sS")
                        sEv = C["sS"]
                        lastev = sEv
                        lastst[b] = dma(
                            "sync",
                            D[f"prtH{hh}"].ap()[512 * tb + 128 * hq: 512 * tb + 128 * (hq + 1), :],
                            ob[b][:], waits=[("sS", sEv)])
                op("gpsimd", lambda e, hh=hh: e.collective_compute(
                    "ReduceScatter", ALU.add, replica_groups=G4,
                    ins=[D[f"prtH{hh}"].ap()], outs=[D[f"prt_o{hh}"].ap()]),
                    waits=[("sLD", max(lastst))], inc="sCC", by=1)
            self.flush()

    def phase3_resid(self):
        P, D = self.P, self.D
        op, dma = self.op, self.dma
        C = self.cnt
        ccRS1 = C["sCC"]
        ccRS0 = ccRS1 - 1
        with ExitStack() as ph:
            rb = [self.sb(ph, f"rb{i}", [128, TME], BF16) for i in range(2)]
            self.begin()
            self.fence(("sLD", "sGD"))
            lastadd = [0, 0]
            for hp in range(HP):
                b = hp % 2
                dX = dma("sync", self.XR[hp][:], D["x1_sp"].ap()[128 * hp:128 * (hp + 1), :])
                dR = dma("sync", rb[b][:],
                         D[f"prt_o{hp // 4}"].ap()[128 * (hp % 4):128 * (hp % 4 + 1), :],
                         waits=[("sCC", ccRS1), ("sV", lastadd[b])])
                lastadd[b] = op("vector", lambda e, hp=hp, b=b: e.tensor_tensor(
                    self.XR[hp][:], self.XR[hp][:], rb[b][:], ALU.add),
                    waits=[("sLD", dR)], inc="sV")
            self.flush()

    def phase4_out(self):
        P = self.P
        op, dma = self.op, self.dma
        with ExitStack() as ph:
            obuf = [self.sb(ph, f"ob4{i}", [128, H], F32) for i in range(2)]
            self.begin()
            slots = [self.p0, self.p1]
            slot_ev = [0, 0]
            lastdma = [0, 0]
            for tt in range(16):
                b = tt % 2
                for hp in range(HP):
                    sl = (tt * HP + hp) % 2
                    tT = op("tensor", lambda e, tt=tt, hp=hp, sl=sl: e.transpose(
                        slots[sl][:, 0:128], self.XR[hp][:, 128 * tt:128 * (tt + 1)],
                        self.idf_sb[:]), waits=[("sS", slot_ev[sl])], inc="sT")
                    slot_ev[sl] = op("scalar", lambda e, hp=hp, b=b, sl=sl: e.activation(
                        obuf[b][:, 128 * hp:128 * (hp + 1)], slots[sl][:, 0:128],
                        AF.Identity), waits=[("sT", tT), ("sLD", lastdma[b])], inc="sS")
                lastdma[b] = dma("sync", P["out"].ap()[128 * tt:128 * (tt + 1), :], obuf[b][:],
                                 waits=[("sS", slot_ev[(tt * HP + HP - 1) % 2])])
            fin = self.cnt["sLD"]
            op("sync", lambda e: None, waits=[("sLD", fin)])
            self.flush()


_NC_CACHE = None


def _get_nc():
    global _NC_CACHE
    if _NC_CACHE is None:
        _NC_CACHE = B().build()
    return _NC_CACHE


def kernel(**inp):
    nc = _get_nc()
    xs = np.ascontiguousarray(inp["L0_attn_hidden_states"].reshape(16384, 1024))
    ipw = inp["L0_attn_in_proj_w"]
    cw = inp["L0_attn_conv_w"]
    cb = inp["L0_attn_conv_b"]
    xpw = inp["L0_attn_x_proj_w"]
    dtw = inp["L0_attn_dt_proj_w"]
    dtb = inp["L0_attn_dt_proj_b"]
    alog = inp["L0_attn_A_log"]
    dD = inp["L0_attn_D"]
    opw = inp["L0_attn_out_proj_w"]
    U = inp["L0_attn_U_param"]
    V = inp["L0_attn_V_param"]
    lrs = inp["L0_attn_low_rank_scale"]
    f32 = np.float32
    in_maps = []
    for c in range(8):
        q = c % 4
        dsl = slice(512 * q, 512 * (q + 1))
        zsl = slice(2048 + 512 * q, 2048 + 512 * (q + 1))
        m = {
            "x0": np.ascontiguousarray(xs[2048 * c:2048 * (c + 1)], f32),
            "wg": np.ascontiguousarray(inp["L0_ffn_w_gate"], f32),
            "wu": np.ascontiguousarray(inp["L0_ffn_w_up"], f32),
            "wd": np.ascontiguousarray(inp["L0_ffn_w_down"], f32),
            "clip2": np.ascontiguousarray(
                np.array([[inp["L0_ffn_clip_x"][0], inp["L0_ffn_clip_down"][0]]], f32)),
            "ipwT": np.ascontiguousarray(
                np.concatenate([ipw[dsl], ipw[zsl]], axis=0).T, f32),
            "cw4": np.ascontiguousarray(
                cw[dsl].reshape(4, 128, 4).transpose(1, 0, 2).reshape(128, 16), f32),
            "cbD": np.ascontiguousarray(
                np.concatenate([cb[dsl].reshape(4, 128).T, dD[dsl].reshape(4, 128).T],
                               axis=1), f32),
            "dtb4": np.ascontiguousarray(dtb[dsl].reshape(4, 128).T, f32),
            "alog": np.ascontiguousarray(
                alog[dsl].reshape(4, 128, 8).transpose(1, 0, 2).reshape(128, 32), f32),
            "uT": np.ascontiguousarray(U[dsl].T, f32),
            "vprm": np.ascontiguousarray(V, f32),
            "lrs1": np.ascontiguousarray(np.asarray(lrs, f32).reshape(1, 1)),
            "xpwT": np.ascontiguousarray(xpw[:, dsl].T, f32),
            "dtwT": np.ascontiguousarray(dtw[dsl].T, f32),
            "opwT": np.ascontiguousarray(opw[:, dsl].T, f32),
            "idf": np.eye(128, dtype=f32),
        }
        in_maps.append(m)
    res = run_bass_kernel_spmd(nc, in_maps, core_ids=list(range(8)))
    out = np.concatenate([np.asarray(res.results[c]["out"]) for c in range(8)], axis=0)
    return out.reshape(2, 8192, 1024).astype(np.float32)


# version 40
# speedup vs baseline: 1.0141x; 1.0130x over previous
import math
import numpy as np
import ml_dtypes
from contextlib import ExitStack
import concourse.bass as bass
import concourse.mybir as mybir
from concourse.bass_utils import run_bass_kernel_spmd

F32 = mybir.dt.float32
BF16 = mybir.dt.bfloat16
AF = mybir.ActivationFunctionType
ALU = mybir.AluOpType
AX = mybir.AxisListType

MAGIC = float(np.float32(1.5 * 2 ** 23))
EPS = 1e-5
QMAX = 127.0
G8 = [list(range(8))]
G4 = [[0, 1, 2, 3], [4, 5, 6, 7]]

H = 1024
HP = 8
TME = 2048
IT = 22
DIN = 512
DP = 4
NST = 8
TB = 8192
CK = 1024
NCK = TB // CK


class B:
    DEBUG_TAPS = False
    TRUNCATE = False
    P0_QUANT = True
    lists = None

    def sem(self, name):
        h = self.outer.enter_context(self.nc.semaphore(name))
        self.sems[name] = h
        self.cnt[name] = 0

    def begin(self):
        if self.lists is None:
            self.lists = {k: [] for k in ("sync", "gpsimd", "vector", "scalar", "tensor")}

    def drive(self, *gens):
        gens = list(gens)
        while gens:
            for g in gens[:]:
                try:
                    next(g)
                except StopIteration:
                    gens.remove(g)

    def fence(self, sems):
        for eng in ("sync", "gpsimd", "vector", "scalar", "tensor"):
            self.op(eng, lambda e: None, waits=[(x, 1) for x in sems])

    def op(self, eng, emit, waits=(), inc=None, by=1):
        waits = tuple((s, self.cnt[s] if s in ("sLD", "sGD", "sSD", "sWQ") else v)
                      for (s, v) in waits if v)

        def closure(e, emit=emit, waits=waits, inc=inc, by=by):
            for s, v in waits:
                e.wait_ge(self.sems[s], v)
            inst = emit(e)
            if inc is not None and inst is not None:
                inst.then_inc(self.sems[inc], by)
        self.lists[eng].append(closure)
        if inc is not None:
            self.cnt[inc] += by
            return self.cnt[inc]
        return None

    def dma(self, eng, out, in_, waits=(), sem=None, **kw):
        if sem is None:
            sem = {"sync": "sLD", "gpsimd": "sGD", "scalar": "sSD"}[eng]
        return self.op(eng, lambda e: e.dma_start(out=out, in_=in_, **kw), waits, sem, 16)

    def flush(self):
        with self.nc.Block() as blk:
            for eng in ("sync", "gpsimd", "vector", "scalar", "tensor"):
                ops = self.lists[eng]
                if ops:
                    getattr(blk, eng)(lambda e, ops=tuple(ops): [f(e) for f in ops])
        self.lists = None
        self.nc.all_engine_barrier()

    def sb(self, stack, name, shape, dt):
        return stack.enter_context(self.nc.sbuf_tensor(name, shape, dt))

    def build(self):
        self.nc = bass.Bass("TRN2", target_bir_lowering=False, debug=False)
        nc = self.nc
        self.cnt = {}
        self.sems = {}
        P = {}

        def par(name, shape, dt=F32, out=False):
            P[name] = nc.declare_dram_parameter(name, shape, dt, isOutput=out)

        par("x0", [TME, H])
        par("wg", [2816, H]); par("wu", [2816, H]); par("wd", [H, 2816])
        par("clip2", [1, 2])
        par("ipwT", [H, 1024])
        par("cw4", [128, 16]); par("cbD", [128, 8]); par("dtb4", [128, 4])
        par("alog", [128, 32]); par("uT", [8, 512]); par("vprm", [8, 8])
        par("lrs1", [1, 1])
        par("xpwT", [512, 80]); par("dtwT", [64, 512]); par("opwT", [512, 1024])
        par("idf", [128, 128])
        par("out", [TME, H], out=True)

        D = {}

        def dram(name, shape, dt):
            D[name] = nc.dram_tensor(name, shape, dt)

        dram("wgt_q", [H, 2816], BF16)
        dram("wut_q", [H, 2816], BF16)
        dram("wdt_q", [2816, H], BF16)
        dram("hpre_sp", [2816, TME], F32)
        dram("hq_sp", [2816, TME], BF16)
        dram("x1_sp", [H, TME], F32)
        dram("z_sp", [DIN, TB], BF16)
        dram("u_sp", [DIN, TB], BF16)
        dram("mx_b", [1, 8], F32); dram("mx_o", [1, 8], F32)
        dram("xn2_bA", [H, TME // 2], BF16); dram("xn2_oA", [4 * H, TME // 2], BF16)
        dram("xn2_bB", [H, TME // 2], BF16); dram("xn2_oB", [4 * H, TME // 2], BF16)
        dram("xdb_b", [80, TB], F32); dram("xdb_o", [80, TB], F32)
        dram("prtH0", [2 * H, TME], BF16); dram("prtH1", [2 * H, TME], BF16)
        dram("prt_o0", [H // 2, TME], BF16); dram("prt_o1", [H // 2, TME], BF16)
        self.P, self.D = P, D

        self.outer = ExitStack()
        with self.outer:
            o = self.outer
            for s in ("sLD", "sGD", "sSD", "sCC", "sV", "sS", "sT", "sWQ"):
                self.sem(s)
            sb = self.sb
            self.A_sb = sb(o, "A_sb", [128, 32], F32)
            self.cbD_sb = sb(o, "cbD_sb", [128, 8], F32)
            self.dtb_sb = sb(o, "dtb_sb", [128, 4], F32)
            self.carry = sb(o, "carry", [128, 32], F32)
            self.onesf = sb(o, "onesf", [1, 128], F32)
            self.onesb = sb(o, "onesb", [1, 128], BF16)
            self.onescol = sb(o, "onescol", [128, 1], BF16)
            self.idf_sb = sb(o, "idf_sb", [128, 128], F32)
            self.idb_sb = sb(o, "idb_sb", [128, 128], BF16)
            self.DGT = sb(o, "DGT", [128, 16 * 128], BF16)
            self.sg_c = sb(o, "sg_c", [128, IT], F32)
            self.su_c = sb(o, "su_c", [128, IT], F32)
            self.sd_c = sb(o, "sd_c", [128, HP], F32)
            self.sgx = sb(o, "sgx", [128, IT], F32)
            self.sux = sb(o, "sux", [128, IT], F32)
            self.sdx = sb(o, "sdx", [128, HP], F32)
            self.sig2 = sb(o, "sig2", [1, 2], F32)
            self.gmax = sb(o, "gmax", [1, 1], F32)
            self.rowA = sb(o, "rowA", [1, TME], F32)
            self.rowB = sb(o, "rowB", [1, TME], F32)
            self.rowC = sb(o, "rowC", [1, TME], F32)
            self.col0 = sb(o, "col0", [128, 1], F32)
            self.col1 = sb(o, "col1", [128, 1], F32)
            self.rmax = sb(o, "rmax", [128, 1], F32)
            self.mcol = sb(o, "mcol", [128, 1], F32)
            self.sxc = sb(o, "sxc", [128, 1], F32)
            self.ivc = sb(o, "ivc", [128, 1], F32)
            self.mxw = sb(o, "mxw", [128, 96], F32)
            self.epsc = sb(o, "epsc", [1, 1], F32)
            self.xpw_sb = [sb(o, f"xpw{p}", [128, 80], BF16) for p in range(DP)]
            self.dtw_sb = sb(o, "dtw_sb", [64, 512], BF16)
            self.clip_sb = sb(o, "clip_sb", [1, 2], F32)
            self.p0 = o.enter_context(nc.psum_tensor("p0", [128, 512], F32))
            self.p1 = o.enter_context(nc.psum_tensor("p1", [128, 512], F32))
            self.p2 = o.enter_context(nc.psum_tensor("p2", [128, 512], F32))
            self.p3 = o.enter_context(nc.psum_tensor("p3", [128, 512], F32))
            self.pw = o.enter_context(nc.psum_tensor("pw", [128, 1024], F32))
            self.ptb0 = o.enter_context(nc.psum_tensor("ptb0", [128, 512], BF16))
            self.ptb1 = o.enter_context(nc.psum_tensor("ptb1", [128, 512], BF16))

            ph0 = ExitStack()
            self.begin()
            qg = self.phase0(ph0)
            with ExitStack() as ph:
                self.XR = [sb(ph, f"XR{p}", [128, TME], F32) for p in range(HP)]
                import itertools
                fstack0 = ExitStack()
                XQ0 = [self.sb(fstack0, f"XQ0_{p}", [128, TME], BF16)
                       for p in range(HP)]
                ag = itertools.chain(self.phase_load_x0(),
                                     self.ffn_stageA_gen(0, XQ0))
                self.drive(qg, ag)
                self.flush()
                ph0.close()
                self.ffn_rest(0, XQ0, fstack0)
                self.phase2a()
            if False:
                pass
                if self.DEBUG_TAPS:
                    P["tap_xr0"] = nc.declare_dram_parameter("tap_xr0", [H, TME], F32,
                                                             isOutput=True)
                    P["tap_idf"] = nc.declare_dram_parameter("tap_idf", [128, 128], F32,
                                                             isOutput=True)
                    P["tap_idb"] = nc.declare_dram_parameter("tap_idb", [128, 128], BF16,
                                                             isOutput=True)
                    self.begin()
                    fin = 0
                    for p in range(HP):
                        fin = self.dma("sync", P["tap_xr0"].ap()[128 * p:128 * (p + 1), :],
                                       self.XR[p][:])
                    fin = self.dma("sync", P["tap_idf"].ap(), self.idf_sb[:])
                    fin = self.dma("sync", P["tap_idb"].ap(), self.idb_sb[:])
                    self.op("sync", lambda e: None, waits=[("sLD", fin)])
                    self.flush()
                if self.TRUNCATE:
                    return nc
                self.ffn(app=0)
                self.phase2a()
            with ExitStack() as ph:
                self.yall = [sb(ph, f"yall{p}", [128, TB], BF16) for p in range(DP)]
                self.phase2b()
                self.phase2c()
                self.phase2d()
            with ExitStack() as ph:
                self.XR = [sb(ph, f"XS{p}", [128, TME], F32) for p in range(HP)]
                self.phase3_resid()
                self.ffn(app=1)
                self.phase4_out()
            if self.DEBUG_TAPS:
                taps = [("wgt_q", [H, 2816], BF16), ("wdt_q", [2816, H], BF16),
                        ("x1_sp", [H, TME], F32), ("xn2_o", [4 * H, TME], BF16),
                        ("xdb_o", [80, TB], F32), ("u_sp", [DIN, TB], BF16),
                        ("z_sp", [DIN, TB], BF16), ("prt_o", [H, TME], BF16),
                        ("hpre_sp", [2816, TME], F32), ("hq_sp", [2816, TME], BF16),
                        ("xdb_b", [80, TB], F32), ("prt_b", [4 * H, TME], BF16)]
                for nm, shape, dt in taps:
                    P["tap_" + nm] = nc.declare_dram_parameter(
                        "tap_" + nm, shape, dt, isOutput=True)
                self.begin()
                fin = 0
                for nm, shape, dt in taps:
                    fin = self.dma("sync", P["tap_" + nm].ap(), D[nm].ap())
                if "y_tap" in D:
                    fin = self.dma("sync", P["tap_y"].ap(), D["y_tap"].ap())
                self.op("sync", lambda e: None, waits=[("sLD", fin)])
                self.flush()
        return nc

    def phase0(self, ph):
        nc, P, D = self.nc, self.P, self.D
        op, dma = self.op, self.dma
        C = self.cnt
        if True:
            sb = self.sb
            wbuf = [sb(ph, f"w0b{i}", [128, 2816], F32) for i in range(2)]
            qbf = [sb(ph, f"q0b{i}", [128, 2816], BF16) for i in range(2)]
            qc0p = [sb(ph, f"qc0{i}", [128, 1], F32) for i in range(2)]
            qc1p = [sb(ph, f"qc1{i}", [128, 1], F32) for i in range(2)]
            tbuf = [sb(ph, f"t0b{i}", [128, 2816], BF16) for i in range(2)]
            alog_sb = sb(ph, "alog_sb", [128, 32], F32)
            expa = sb(ph, "expa", [128, 32], F32)
            uT_sb = sb(ph, "uT_sb", [8, 512], F32)
            v_sb = sb(ph, "v_sb", [8, 8], F32)
            lrs_sb = sb(ph, "lrs_sb", [1, 1], F32)
            ccol = sb(ph, "ccol", [128, 1], F32)
            uvev = sb(ph, "uvev", [128, 32], F32)
            iop = sb(ph, "iop", [128, 128], F32)
            iof = sb(ph, "iof", [128, 128], F32)
            mask = sb(ph, "mask", [128, 128], F32)
            cw_sb = sb(ph, "cw_sb", [128, 16], F32)

            for name, dst in [("alog", alog_sb), ("uT", uT_sb), ("vprm", v_sb),
                              ("lrs1", lrs_sb), ("cw4", cw_sb), ("cbD", self.cbD_sb),
                              ("dtb4", self.dtb_sb), ("idf", self.idf_sb),
                              ("clip2", self.clip_sb)]:
                dma("sync", dst[:], P[name].ap())
            ld0 = C["sLD"]
            dma("gpsimd", self.dtw_sb[:], P["dtwT"].ap())
            for p in range(DP):
                dma("gpsimd", self.xpw_sb[p][:], P["xpwT"].ap()[128 * p:128 * (p + 1), :])
            op("vector", lambda e: e.memset(self.onesf[:], 1.0), inc="sV")
            op("vector", lambda e: e.memset(self.onesb[:], 1.0), inc="sV")
            op("vector", lambda e: e.memset(self.onescol[:], 1.0), inc="sV")
            op("vector", lambda e: e.memset(self.carry[:], 0.0), inc="sV")
            op("vector", lambda e: e.memset(self.mcol[:], MAGIC), inc="sV")
            op("vector", lambda e: e.memset(self.epsc[:], EPS), inc="sV")
            vOnes = C["sV"]
            op("vector", lambda e: e.tensor_copy(self.idb_sb[:], self.idf_sb[:]),
               waits=[("sLD", ld0)], inc="sV")
            vIdb = C["sV"]
            op("gpsimd", lambda e: e.iota(iop[:], [[0, 128]], channel_multiplier=1,
                                          allow_small_or_imprecise_dtypes=True), inc="sGD", by=1)
            op("gpsimd", lambda e: e.iota(iof[:], [[1, 128]], channel_multiplier=0,
                                          allow_small_or_imprecise_dtypes=True), inc="sGD", by=1)
            gIota = C["sGD"]
            op("vector", lambda e: e.tensor_tensor(mask[:], iop[:], iof[:], ALU.is_equal),
               waits=[("sGD", gIota)], inc="sV")
            vMask = C["sV"]
            for j in range(16):
                op("vector", lambda e, j=j: e.tensor_scalar(
                    self.DGT[:, j * 128:(j + 1) * 128], mask[:], cw_sb[:, j:j + 1], None,
                    ALU.mult), waits=[("sV", vMask), ("sLD", ld0)], inc="sV")
            op("scalar", lambda e: e.activation(self.sig2[:], self.clip_sb[:], AF.Sigmoid),
               waits=[("sLD", ld0)], inc="sS")
            op("scalar", lambda e: e.activation(expa[:], alog_sb[:], AF.Exp),
               waits=[("sLD", ld0)], inc="sS")
            tC = op("tensor", lambda e: e.matmul(self.p2[0:128, 0:1], self.onesf[0:1, :],
                                                 lrs_sb[0:1, 0:1], start=True, stop=True),
                    waits=[("sLD", ld0), ("sV", vOnes)], inc="sT")
            sC = op("scalar", lambda e: e.activation(ccol[:], self.p2[0:128, 0:1], AF.Identity,
                                                     scale=1.0 / math.sqrt(8.0)),
                    waits=[("sT", tC)], inc="sS")
            for p in range(DP):
                tUV = op("tensor", lambda e, p=p: e.matmul(
                    self.p3[0:128, 0:8], uT_sb[0:8, 128 * p:128 * (p + 1)], v_sb[0:8, 0:8],
                    start=True, stop=True), waits=[("sLD", ld0), ("sS", sC)], inc="sT")
                op("scalar", lambda e, p=p: e.activation(
                    uvev[:, 8 * p:8 * p + 8], self.p3[0:128, 0:8], AF.Identity),
                    waits=[("sT", tUV)], inc="sS")
            sUV = C["sS"]
            op("vector", lambda e: e.scalar_tensor_tensor(
                self.A_sb[:], uvev[:], ccol[:, 0:1], expa[:], ALU.mult, ALU.subtract),
               waits=[("sS", sUV)], inc="sV")
            yield

            specs = [("wg", self.sg_c, D["wgt_q"], 22, H),
                     ("wu", self.su_c, D["wut_q"], 22, H),
                     ("wd", self.sd_c, D["wdt_q"], HP, 2816)]
            if not self.P0_QUANT:
                specs = []
            ptbs = [self.ptb0, self.ptb1]
            lastq = [0, 0]
            slot_ev = [0, 0]
            lastrd = [0, 0]
            prevrdP = [0, 0]
            for wname, scol, qdst, ntile, CW in specs:
                for i in range(ntile):
                    b = i % 2
                    qc0, qc1 = qc0p[b], qc1p[b]
                    dL = dma("sync" if b == 0 else "scalar", wbuf[b][:, 0:CW],
                             P[wname].ap()[128 * i:128 * (i + 1), :],
                             waits=[("sV", lastrd[b])], sem="sWQ")
                    vr1 = op("vector", lambda e, b=b, CW=CW, qc1=qc1: e.tensor_reduce(
                        qc1[:], wbuf[b][:, 0:CW], AX.X, ALU.max,
                        apply_absolute_value=True),
                        waits=[("sWQ", dL), ("sS", prevrdP[b])], inc="sV")
                    vr2 = op("vector", lambda e, qc0=qc0, qc1=qc1: e.reciprocal(
                        qc0[:], qc1[:]), waits=[("sV", vr1)], inc="sV")
                    vr3 = op("vector", lambda e, qc0=qc0: e.tensor_scalar(
                        qc0[:], qc0[:], QMAX, None, ALU.mult),
                        waits=[("sV", vr2)], inc="sV")
                    vI = op("vector", lambda e, scol=scol, i=i, qc1=qc1: e.tensor_scalar(
                        scol[:, i:i + 1], qc1[:], 1.0 / QMAX, None, ALU.mult),
                        waits=[("sV", vr3)], inc="sV")
                    sR = op("scalar", lambda e, b=b, CW=CW, qc0=qc0: e.activation(
                        wbuf[b][:, 0:CW], wbuf[b][:, 0:CW], AF.Identity,
                        scale=qc0[:, 0:1], bias=self.mcol[:, 0:1]), waits=[("sV", vI)], inc="sS")
                    prevrdP[b] = sR
                    vQ = op("vector", lambda e, b=b, CW=CW: e.tensor_scalar(
                        qbf[b][:, 0:CW], wbuf[b][:, 0:CW], MAGIC, None, ALU.subtract),
                        waits=[("sS", sR), ("sWQ", lastq[b])], inc="sV")
                    lastrd[b] = vQ
                    nblk = CW // 128
                    for cb in range(nblk):
                        sl = cb % 2
                        tT = op("tensor", lambda e, b=b, cb=cb, sl=sl: e.transpose(
                            ptbs[sl][:, 0:128], qbf[b][:, 128 * cb:128 * (cb + 1)],
                            self.idb_sb[:]),
                            waits=[("sV", vQ), ("sV", vIdb), ("sS", slot_ev[sl])], inc="sT")
                        slot_ev[sl] = op("scalar", lambda e, b=b, cb=cb, sl=sl: e.activation(
                            tbuf[b][:, 128 * cb:128 * (cb + 1)], ptbs[sl][:, 0:128],
                            AF.Identity),
                            waits=[("sT", tT)] + ([("sWQ", lastq[b])] if cb == 0 else []),
                            inc="sS")
                    CB = 2
                    for c0 in range(0, nblk, CB):
                        cn = min(CB, nblk - c0)
                        src = tbuf[b][:, 128 * c0:128 * (c0 + cn)].rearrange(
                            "p (c l) -> p c l", l=128)
                        dst = bass.AP(qdst, 128 * i + 128 * c0 * qdst.shape[1],
                                      [[qdst.shape[1], 128], [128 * qdst.shape[1], cn],
                                       [1, 128]])
                        lastq[b] = dma("sync", dst, src, sem="sWQ",
                                       waits=[("sS", slot_ev[0]), ("sS", slot_ev[1])])
                    yield

    def phase_load_x0(self):
        P = self.P
        op, dma = self.op, self.dma
        with ExitStack() as ph:
            xbuf = [self.sb(ph, f"x0b{i}", [128, H], F32) for i in range(2)]
            slots = [self.p0, self.p1]
            slot_ev = [0, 0]
            lastuse = [0, 0]
            for tt in range(16):
                yield
                b = tt % 2
                dL = dma("sync" if b == 0 else "scalar",
                         xbuf[b][:], P["x0"].ap()[128 * tt:128 * (tt + 1), :],
                         waits=[("sS", lastuse[b])], sem="sLD")
                for hp in range(HP):
                    sl = (tt * HP + hp) % 2
                    tT = op("tensor", lambda e, b=b, hp=hp, sl=sl: e.transpose(
                        slots[sl][:, 0:128], xbuf[b][:, 128 * hp:128 * (hp + 1)],
                        self.idf_sb[:]), waits=[("sLD", dL), ("sS", slot_ev[sl])], inc="sT")
                    slot_ev[sl] = op("scalar", lambda e, hp=hp, tt=tt, sl=sl: e.activation(
                        self.XR[hp][:, 128 * tt:128 * (tt + 1)], slots[sl][:, 0:128],
                        AF.Identity), waits=[("sT", tT)], inc="sS")
                lastuse[b] = slot_ev[(tt * HP + HP - 1) % 2]
            self._xr_ready = self.cnt["sS"]

    _stats_n = 0

    def stats_rows(self, X):
        op = self.op
        C = self.cnt
        B._stats_n += 1
        sn = B._stats_n
        with ExitStack() as ph:
            xb = [self.sb(ph, f"stb{sn}_{i}", [128, TME], BF16) for i in range(2)]
            lastmm = [0, 0]
            xrdy = getattr(self, "_xr_ready", 0)
            for hp in range(HP):
                b = hp % 2
                vCp = op("vector", lambda e, hp=hp, b=b: e.tensor_copy(xb[b][:], X[hp][:]),
                         waits=[("sT", lastmm[b]), ("sS", xrdy)], inc="sV")
                for ts in range(4):
                    pt = [self.p0, self.p1, self.p2, self.p3][ts]
                    lastmm[b] = op("tensor", lambda e, b=b, ts=ts, hp=hp, pt=pt: e.matmul(
                        pt[0:1, 0:512], self.onescol[:, 0:1],
                        xb[b][:, 512 * ts:512 * (ts + 1)],
                        start=(hp == 0), stop=(hp == HP - 1), skip_group_check=True),
                        waits=[("sV", vCp)], inc="sT")
            tSum = C["sT"]
            for ts in range(4):
                pt = [self.p0, self.p1, self.p2, self.p3][ts]
                op("scalar", lambda e, ts=ts, pt=pt: e.activation(
                    self.rowA[0:1, 512 * ts:512 * (ts + 1)], pt[0:1, 0:512],
                    AF.Identity, scale=1.0 / H), waits=[("sT", tSum)], inc="sS")
            sM = C["sS"]
            lastmm = [0, 0]
            for hp in range(HP):
                b = hp % 2
                sQp = op("scalar", lambda e, hp=hp, b=b: e.activation(
                    xb[b][:], X[hp][:], AF.Square),
                    waits=[("sM_dummy", 0), ("sS", sM if hp < 2 else 0), ("sT", lastmm[b])],
                    inc="sS")
                for ts in range(4):
                    pt = [self.p0, self.p1, self.p2, self.p3][ts]
                    lastmm[b] = op("tensor", lambda e, b=b, ts=ts, hp=hp, pt=pt: e.matmul(
                        pt[0:1, 0:512], self.onescol[:, 0:1],
                        xb[b][:, 512 * ts:512 * (ts + 1)],
                        start=(hp == 0), stop=(hp == HP - 1), skip_group_check=True),
                        waits=[("sS", sQp)], inc="sT")
            tSq = C["sT"]
            for ts in range(4):
                pt = [self.p0, self.p1, self.p2, self.p3][ts]
                op("scalar", lambda e, ts=ts, pt=pt: e.activation(
                    self.rowB[0:1, 512 * ts:512 * (ts + 1)], pt[0:1, 0:512],
                    AF.Identity, scale=1.0 / H), waits=[("sT", tSq)], inc="sS")
            sQ = C["sS"]
            v1 = op("vector", lambda e: e.tensor_tensor(
                self.rowC[0:1, :], self.rowA[0:1, :], self.rowA[0:1, :], ALU.mult),
                waits=[("sS", sQ)], inc="sV")
            v2 = op("vector", lambda e: e.tensor_tensor(
                self.rowB[0:1, :], self.rowB[0:1, :], self.rowC[0:1, :], ALU.subtract),
                inc="sV")
            op("scalar", lambda e: e.activation(self.rowB[0:1, :], self.rowB[0:1, :],
                                                AF.Ln, bias=self.epsc[0:1, 0:1]), waits=[("sV", v2)], inc="sS")
            sR = op("scalar", lambda e: e.activation(self.rowC[0:1, :], self.rowB[0:1, :],
                                                     AF.Exp, scale=-0.5), inc="sS")
            return sR

    def bcast_ts(self, ts, waits):
        op = self.op
        op("tensor", lambda e, ts=ts: e.matmul(
            self.p2[:, 0:512], self.onesf[0:1, :], self.rowA[0:1, 512 * ts:512 * (ts + 1)],
            start=True, stop=True), waits=waits, inc="sT")
        return op("tensor", lambda e, ts=ts: e.matmul(
            self.p3[:, 0:512], self.onesf[0:1, :], self.rowC[0:1, 512 * ts:512 * (ts + 1)],
            start=True, stop=True), inc="sT")

    def global_max_col(self, vReady):
        op, dma = self.op, self.dma
        tT = op("tensor", lambda e: e.transpose(
            self.p2[0:1, 0:128], self.rmax[0:128, 0:1], self.idf_sb[:]),
            waits=[("sV", vReady)], inc="sT")
        vG = op("vector", lambda e: e.tensor_reduce(
            self.gmax[0:1, 0:1], self.p2[0:1, 0:128], AX.X, ALU.max),
            waits=[("sT", tT)], inc="sV")
        dW = dma("sync", self.D["mx_b"].ap()[0:1, 0:1], self.gmax[0:1, 0:1],
                 waits=[("sV", vG)])
        cc = op("gpsimd", lambda e: e.collective_compute(
            "AllReduce", ALU.max, replica_groups=G8,
            ins=[self.D["mx_b"].ap()], outs=[self.D["mx_o"].ap()]),
            waits=[("sLD", dW)], inc="sCC", by=1)
        dR = dma("sync", self.gmax[0:1, 0:1], self.D["mx_o"].ap()[0:1, 0:1],
                 waits=[("sCC", cc)])
        return dR

    def ffn(self, app):
        fstack = ExitStack()
        XQ = [self.sb(fstack, f"XQ{app}_{p}", [128, TME], BF16) for p in range(HP)]
        self.begin()
        self.drive(self.ffn_stageA_gen(app, XQ))
        self.flush()
        self.ffn_rest(app, XQ, fstack)

    def ffn_stageA_gen(self, app, XQ):
        nc, P, D = self.nc, self.P, self.D
        op, dma = self.op, self.dma
        C = self.cnt
        if True:
            with ExitStack() as ph:
                xt = [self.sb(ph, f"fxt{app}_{i}", [128, 512], F32) for i in range(2)]
                qt = [self.sb(ph, f"fqt{app}_{i}", [128, 512], F32) for i in range(2)]
                yield
                sR = self.stats_rows(self.XR)
                yield
                vPrev = 0
                for ts in range(4):
                    yield
                    tB = self.bcast_ts(ts, [("sS", sR), ("sV", vPrev)])
                    for hp in range(HP):
                        b = hp % 2
                        op("vector", lambda e, ts=ts, hp=hp, b=b: e.tensor_tensor(
                            xt[b][:], self.XR[hp][:, 512 * ts:512 * (ts + 1)],
                            self.p2[:, 0:512], ALU.subtract),
                            waits=[("sT", tB)] if hp == 0 else [], inc="sV")
                        op("vector", lambda e, b=b: e.tensor_tensor(
                            xt[b][:], xt[b][:], self.p3[:, 0:512], ALU.mult), inc="sV")
                        op("vector", lambda e, b=b, ts=ts, hp=hp: e.tensor_reduce(
                            self.mxw[:, ts * 8 + hp:ts * 8 + hp + 1], xt[b][:], AX.X,
                            ALU.max, apply_absolute_value=True), inc="sV")
                    vPrev = C["sV"]
                vRm = op("vector", lambda e: e.tensor_reduce(
                    self.rmax[:], self.mxw[:, 0:32], AX.X, ALU.max),
                    waits=[("sV", vPrev)], inc="sV")
                dR = self.global_max_col(vRm)
                v1 = op("vector", lambda e: e.tensor_tensor(
                    self.gmax[0:1, 0:1], self.gmax[0:1, 0:1], self.sig2[0:1, 0:1], ALU.mult),
                    waits=[("sLD", dR)], inc="sV")
                tBc = op("tensor", lambda e: e.matmul(
                    self.p2[0:128, 0:1], self.onesf[0:1, :], self.gmax[0:1, 0:1],
                    start=True, stop=True), waits=[("sV", v1)], inc="sT")
                s1 = op("scalar", lambda e: e.activation(
                    self.sxc[:], self.p2[0:128, 0:1], AF.Identity, scale=1.0 / QMAX),
                    waits=[("sT", tBc)], inc="sS")
                v2 = op("vector", lambda e: e.reciprocal(self.ivc[:], self.sxc[:]),
                        waits=[("sS", s1)], inc="sV")
                lastq = [0, 0]
                for ts in range(4):
                    yield
                    tB = self.bcast_ts(ts, [("sV", C["sV"])])
                    for hp in range(HP):
                        b = hp % 2
                        op("vector", lambda e, ts=ts, hp=hp, b=b: e.tensor_tensor(
                            xt[b][:], self.XR[hp][:, 512 * ts:512 * (ts + 1)],
                            self.p2[:, 0:512], ALU.subtract),
                            waits=([("sT", tB)] if hp == 0 else []) +
                                  [("sS", lastq[b])], inc="sV")
                        vN = op("vector", lambda e, b=b: e.tensor_tensor(
                            xt[b][:], xt[b][:], self.p3[:, 0:512], ALU.mult), inc="sV")
                        sQ = op("scalar", lambda e, b=b: e.activation(
                            qt[b][:], xt[b][:], AF.Identity, scale=self.ivc[:, 0:1],
                            bias=self.mcol[:, 0:1]), waits=[("sV", vN), ("sV", v2)], inc="sS")
                        lastq[b] = sQ
                        op("vector", lambda e, b=b: e.tensor_scalar(
                            qt[b][:], qt[b][:], MAGIC, QMAX, ALU.subtract, ALU.min),
                            waits=[("sS", sQ)], inc="sV")
                        op("vector", lambda e, ts=ts, hp=hp, b=b: e.tensor_scalar(
                            XQ[hp][:, 512 * ts:512 * (ts + 1)], qt[b][:], -QMAX - 1.0, None,
                            ALU.max), inc="sV")
                if self.DEBUG_TAPS and app == 0:
                    P["tap_rows"] = nc.declare_dram_parameter("tap_rows", [2, TME], F32,
                                                              isOutput=True)
                    P["tap_mx"] = nc.declare_dram_parameter("tap_mx", [1, 8], F32,
                                                            isOutput=True)
                    P["tap_xq"] = nc.declare_dram_parameter("tap_xq", [H, TME], BF16,
                                                            isOutput=True)
                    fin = self.dma("sync", P["tap_rows"].ap()[0:1, :], self.rowA[0:1, :])
                    fin = self.dma("sync", P["tap_rows"].ap()[1:2, :], self.rowC[0:1, :])
                    fin = self.dma("sync", P["tap_mx"].ap(), D["mx_o"].ap())
                    for p in range(HP):
                        fin = self.dma("sync", P["tap_xq"].ap()[128 * p:128 * (p + 1), :],
                                       XQ[p][:], waits=[("sV", C["sV"])])
                    self.op("sync", lambda e: None, waits=[("sLD", fin)])

    def ffn_rest(self, app, XQ, fstack):
        nc, P, D = self.nc, self.P, self.D
        op, dma = self.op, self.dma
        C = self.cnt
        if True:
            with ExitStack() as ph:
                gst = [self.sb(ph, f"gst{app}_{i}", [128, 1024], BF16) for i in range(2)]
                ust = [self.sb(ph, f"ust{app}_{i}", [128, 1024], BF16) for i in range(2)]
                gt = [self.sb(ph, f"fgt{app}_{i}", [128, 512], F32) for i in range(2)]
                ut = [self.sb(ph, f"fut{app}_{i}", [128, 512], F32) for i in range(2)]
                hbuf = [self.sb(ph, f"fhb{app}_{i}", [128, TME], F32) for i in range(2)]
                self.begin()
                self.fence(("sWQ", "sLD"))
                op("vector", lambda e: e.tensor_scalar(
                    self.sgx[:, 0:IT], self.sg_c[:, 0:IT], self.sxc[:, 0:1], None,
                    ALU.mult), inc="sV")
                vScl = op("vector", lambda e: e.tensor_scalar(
                    self.sux[:, 0:IT], self.su_c[:, 0:IT], self.sxc[:, 0:1], None,
                    ALU.mult), inc="sV")
                lastmmG = [0, 0]
                lasthd = [0, 0]
                lastTT = [0, 0]
                prevG = 0
                prevU = 0
                for it in range(IT):
                    b = it % 2
                    for k0 in range(0, HP, 2):
                        srcg = bass.AP(D["wgt_q"], 128 * it + 128 * k0 * 2816,
                                       [[2816, 128], [128 * 2816, 2], [1, 128]])
                        dstg = gst[b][:, 128 * k0:128 * (k0 + 2)].rearrange(
                            "p (k l) -> p k l", l=128)
                        dG = dma("sync", dstg, srcg,
                                 waits=([("sT", lastmmG[b]), ("sWQ", 1)] if k0 == 0 else []))
                        srcu = bass.AP(D["wut_q"], 128 * it + 128 * k0 * 2816,
                                       [[2816, 128], [128 * 2816, 2], [1, 128]])
                        dstu = ust[b][:, 128 * k0:128 * (k0 + 2)].rearrange(
                            "p (k l) -> p k l", l=128)
                        dU = dma("sync", dstu, srcu)
                    for ts in range(4):
                        for k in range(HP):
                            op("tensor", lambda e, b=b, ts=ts, k=k: e.matmul(
                                self.p0[:, 0:512], gst[b][:, 128 * k:128 * (k + 1)],
                                XQ[k][:, 512 * ts:512 * (ts + 1)],
                                start=(k == 0), stop=(k == HP - 1), skip_group_check=True),
                                waits=([("sLD", dU), ("sS", prevG)] if k == 0 else []),
                                inc="sT")
                        tG = C["sT"]
                        for k in range(HP):
                            op("tensor", lambda e, b=b, ts=ts, k=k: e.matmul(
                                self.p1[:, 0:512], ust[b][:, 128 * k:128 * (k + 1)],
                                XQ[k][:, 512 * ts:512 * (ts + 1)],
                                start=(k == 0), stop=(k == HP - 1), skip_group_check=True),
                                waits=([("sS", prevU)] if k == 0 else []),
                                inc="sT")
                        tU = C["sT"]
                        lastmmG[b] = tU
                        bb = ts % 2
                        sG = op("scalar", lambda e, it=it, bb=bb: e.activation(
                            gt[bb][:], self.p0[:, 0:512], AF.Silu,
                            scale=self.sgx[:, it:it + 1]),
                            waits=[("sT", tG), ("sV", lastTT[bb]), ("sV", vScl)],
                            inc="sS")
                        prevG = sG
                        sU = op("scalar", lambda e, it=it, bb=bb: e.activation(
                            ut[bb][:], self.p1[:, 0:512], AF.Identity,
                            scale=self.sux[:, it:it + 1]), waits=[("sT", tU)], inc="sS")
                        prevU = sU
                        vH = op("vector", lambda e, b=b, ts=ts, bb=bb: e.tensor_tensor(
                            hbuf[b][:, 512 * ts:512 * (ts + 1)], gt[bb][:], ut[bb][:],
                            ALU.mult), waits=[("sS", sU), ("sLD", lasthd[b])], inc="sV")
                        lastTT[bb] = vH
                        op("vector", lambda e, b=b, ts=ts, it=it: e.tensor_reduce(
                            self.mxw[:, (it * 4 + ts) % 96:(it * 4 + ts) % 96 + 1],
                            hbuf[b][:, 512 * ts:512 * (ts + 1)], AX.X,
                            ALU.max, apply_absolute_value=True), inc="sV")
                    vDone = C["sV"]
                    lasthd[b] = dma("sync", D["hpre_sp"].ap()[128 * it:128 * (it + 1), :],
                                    hbuf[b][:], waits=[("sV", vDone)])
                vRm = op("vector", lambda e: e.tensor_reduce(
                    self.rmax[:], self.mxw[:, 0:88], AX.X, ALU.max),
                    waits=[("sV", C["sV"])], inc="sV")
                dR = self.global_max_col(vRm)
                v1 = op("vector", lambda e: e.tensor_tensor(
                    self.gmax[0:1, 0:1], self.gmax[0:1, 0:1], self.sig2[0:1, 1:2], ALU.mult),
                    waits=[("sLD", dR)], inc="sV")
                tBc = op("tensor", lambda e: e.matmul(
                    self.p2[0:128, 0:1], self.onesf[0:1, :], self.gmax[0:1, 0:1],
                    start=True, stop=True), waits=[("sV", v1)], inc="sT")
                s1 = op("scalar", lambda e: e.activation(
                    self.col1[:], self.p2[0:128, 0:1], AF.Identity, scale=1.0 / QMAX),
                    waits=[("sT", tBc)], inc="sS")
                op("vector", lambda e: e.reciprocal(self.col0[:], self.col1[:]),
                   waits=[("sS", s1)], inc="sV")
                op("vector", lambda e: e.tensor_scalar(
                    self.sdx[:, 0:HP], self.sd_c[:, 0:HP], self.col1[:, 0:1], 0.5,
                    ALU.mult, ALU.mult), inc="sV")
                if self.DEBUG_TAPS and app == 0:
                    P["tap_hpre0"] = nc.declare_dram_parameter(
                        "tap_hpre0", [2816, TME], F32, isOutput=True)
                    P["tap_mxh"] = nc.declare_dram_parameter(
                        "tap_mxh", [1, 8], F32, isOutput=True)
                    fin = dma("sync", P["tap_hpre0"].ap(), D["hpre_sp"].ap())
                    fin = dma("sync", P["tap_mxh"].ap(), D["mx_o"].ap())
                    op("sync", lambda e: None, waits=[("sLD", fin)])
                self.flush()

            fstack.close()

            with ExitStack() as ph:
                hbuf = [self.sb(ph, f"fh2{app}_{i}", [128, TME], F32) for i in range(2)]
                hqb = [self.sb(ph, f"fhq{app}_{i}", [128, TME], BF16) for i in range(2)]
                self.begin()
                self.fence(("sLD",))
                lastV = [0, 0]
                lastW = [0, 0]
                for it in range(IT):
                    b = it % 2
                    dL = dma("sync", hbuf[b][:], D["hpre_sp"].ap()[128 * it:128 * (it + 1), :],
                             waits=[("sV", lastV[b])])
                    sQ = op("scalar", lambda e, b=b: e.activation(
                        hbuf[b][:], hbuf[b][:], AF.Identity, scale=self.col0[:, 0:1],
                        bias=self.mcol[:, 0:1]), waits=[("sLD", dL)], inc="sS")
                    op("vector", lambda e, b=b: e.tensor_scalar(
                        hbuf[b][:], hbuf[b][:], MAGIC, QMAX, ALU.subtract, ALU.min),
                        waits=[("sS", sQ), ("sLD", lastW[b])], inc="sV")
                    vQ = op("vector", lambda e, b=b: e.tensor_scalar(
                        hqb[b][:], hbuf[b][:], -QMAX - 1.0, None, ALU.max), inc="sV")
                    lastV[b] = vQ
                    lastW[b] = dma("sync", D["hq_sp"].ap()[128 * it:128 * (it + 1), :],
                                   hqb[b][:], waits=[("sV", vQ)])
                if self.DEBUG_TAPS and app == 0:
                    P["tap_hq0"] = nc.declare_dram_parameter(
                        "tap_hq0", [2816, TME], BF16, isOutput=True)
                    fin = dma("sync", P["tap_hq0"].ap(), D["hq_sp"].ap())
                    op("sync", lambda e: None, waits=[("sLD", fin)])
                self.flush()

            with ExitStack() as ph:
                wdt = [self.sb(ph, f"wdt{app}_{k}", [128, 1024], BF16) for k in range(IT)]
                hqc = [self.sb(ph, f"hqc{app}_{i}", [128, 22 * 512], BF16) for i in range(2)]
                dt_ = [self.sb(ph, f"fdn{app}_{i}", [128, 512], F32) for i in range(2)]
                self.begin()
                self.fence(("sLD",))
                dW = 0
                for k in range(IT):
                    srcd = bass.AP(D["wdt_q"], 128 * k * H, [[H, 128], [1, H]])
                    dW = dma("sync", wdt[k][:, 0:H], srcd)
                lastmm = [0, 0]
                lastrs = [0, 0]
                for ts in range(4):
                    b = ts % 2
                    for k0 in range(0, IT, 2):
                        kn = min(2, IT - k0)
                        srch = bass.AP(D["hq_sp"], 512 * ts + 128 * k0 * TME,
                                       [[TME, 128], [128 * TME, kn], [1, 512]])
                        dsth = hqc[b][:, 512 * k0:512 * (k0 + kn)].rearrange(
                            "p (k l) -> p k l", l=512)
                        dH = dma("sync", dsth, srch,
                                 waits=[("sT", lastmm[b])] if k0 == 0 else [])
                    for hb in range(HP):
                        pk = [self.p0, self.p1, self.p2, self.p3][hb % 4]
                        for k in range(IT):
                            op("tensor", lambda e, b=b, hb=hb, k=k, pk=pk: e.matmul(
                                pk[:, 0:512], wdt[k][:, 128 * hb:128 * (hb + 1)],
                                hqc[b][:, 512 * k:512 * (k + 1)],
                                start=(k == 0), stop=(k == IT - 1), skip_group_check=True),
                                waits=([("sLD", max(dH, dW)), ("sV", lastrs[hb % 2])]
                                       if k == 0 else []), inc="sT")
                        lastmm[b] = C["sT"]
                        bb = hb % 2
                        sE = op("scalar", lambda e, hb=hb, bb=bb, pk=pk: e.activation(
                            dt_[bb][:], pk[:, 0:512], AF.Identity,
                            scale=self.sdx[:, hb:hb + 1]),
                            waits=[("sT", lastmm[b]), ("sV", lastrs[bb])], inc="sS")
                        lastrs[bb] = op("vector", lambda e, ts=ts, hb=hb, bb=bb: e.tensor_tensor(
                            self.XR[hb][:, 512 * ts:512 * (ts + 1)],
                            self.XR[hb][:, 512 * ts:512 * (ts + 1)], dt_[bb][:], ALU.add),
                            waits=[("sS", sE)], inc="sV")
                self.flush()

    def phase2a(self):
        P, D = self.P, self.D
        op, dma = self.op, self.dma
        C = self.cnt
        with ExitStack() as ph:
            xnb = [self.sb(ph, f"xnb{p}", [128, TME], BF16) for p in range(HP)]
            xt = [self.sb(ph, f"x2t{i}", [128, 512], F32) for i in range(2)]
            self.begin()
            sR = self.stats_rows(self.XR)
            vPrev = 0
            for ts in range(4):
                tB = self.bcast_ts(ts, [("sS", sR), ("sV", vPrev)])
                for hp in range(HP):
                    b = hp % 2
                    op("vector", lambda e, ts=ts, hp=hp, b=b: e.tensor_tensor(
                        xt[b][:], self.XR[hp][:, 512 * ts:512 * (ts + 1)],
                        self.p2[:, 0:512], ALU.subtract),
                        waits=[("sT", tB)] if hp == 0 else [], inc="sV")
                    op("vector", lambda e, ts=ts, hp=hp, b=b: e.tensor_tensor(
                        xnb[hp][:, 512 * ts:512 * (ts + 1)], xt[b][:], self.p3[:, 0:512],
                        ALU.mult), inc="sV")
                vPrev = C["sV"]
                if ts in (1, 3):
                    half = "A" if ts == 1 else "B"
                    col = 0 if ts == 1 else TME // 2
                    dh = 0
                    for hp in range(HP):
                        dh = dma("sync", D[f"xn2_b{half}"].ap()[128 * hp:128 * (hp + 1), :],
                                 xnb[hp][:, col:col + TME // 2], waits=[("sV", vPrev)])
                    op("gpsimd", lambda e, half=half: e.collective_compute(
                        "AllGather", ALU.bypass, replica_groups=G4,
                        ins=[D[f"xn2_b{half}"].ap()], outs=[D[f"xn2_o{half}"].ap()]),
                        waits=[("sLD", dh)], inc="sCC", by=1)
            for hp in range(HP):
                dma("sync", D["x1_sp"].ap()[128 * hp:128 * (hp + 1), :], self.XR[hp][:])
            self.flush()

    def phase2b(self):
        nc, P, D = self.nc, self.P, self.D
        op, dma = self.op, self.dma
        C = self.cnt
        ccAGB = C["sCC"]
        ccAGA = ccAGB - 1
        with ExitStack() as ph:
            sb = self.sb
            ipw = [sb(ph, f"ipw{k}", [128, 1024], BF16) for k in range(HP)]
            xin = [sb(ph, f"xin{k}", [128, CK], BF16) for k in range(HP)]
            xc = [sb(ph, f"xc{p}", [128, CK + 3], BF16) for p in range(DP)]
            zb = [sb(ph, f"zb{i}", [128, CK], BF16) for i in range(2)]
            uc = [sb(ph, f"uc{i}", [128, CK], BF16) for i in range(2)]
            xdr = sb(ph, "xdr", [80, CK], F32)
            self.begin()
            dI = 0
            for k in range(HP):
                dI = dma("gpsimd", ipw[k][:], P["ipwT"].ap()[128 * k:128 * (k + 1), :])
            for p in range(DP):
                op("vector", lambda e, p=p: e.memset(xc[p][:, 0:3], 0.0), inc="sV")
            vHead = C["sV"]
            lastx = 0
            lastxc = [0] * DP
            lastz = [0, 0]
            lastu = [0, 0]
            lastuT = [0, 0]
            lastxd = 0
            lastxdS = 0
            prevheads = vHead
            for ck in range(NCK):
                r = (CK * ck) // TME
                off = (CK * ck) % TME
                dX = 0
                srcT = D["xn2_oA"] if ck % 2 == 0 else D["xn2_oB"]
                ccW = ccAGA if ck % 2 == 0 else ccAGB
                for k in range(HP):
                    dX = dma("sync", xin[k][:],
                             srcT.ap()[H * r + 128 * k: H * r + 128 * (k + 1), :],
                             waits=[("sT", lastx), ("sCC", ccW), ("sGD", dI)])
                if ck > 0:
                    for p in range(DP):
                        op("vector", lambda e, p=p: e.tensor_copy(
                            xc[p][:, 0:3], xc[p][:, CK:CK + 3]),
                            waits=[("sT", lastxc[p])], inc="sV")
                    prevheads = C["sV"]
                for po in range(8):
                    for ts in range(CK // 512):
                        pt = [self.p0, self.p1][ts]
                        for k in range(HP):
                            op("tensor", lambda e, po=po, ts=ts, k=k, pt=pt: e.matmul(
                                pt[:, 0:512], ipw[k][:, 128 * po:128 * (po + 1)],
                                xin[k][:, 512 * ts:512 * (ts + 1)],
                                start=(k == 0), stop=(k == HP - 1), skip_group_check=True),
                                waits=([("sLD", dX), ("sS", C["sS"])] if k == 0 else []),
                                inc="sT")
                        tIP = C["sT"]
                        if po < DP:
                            op("scalar", lambda e, po=po, ts=ts, pt=pt: e.activation(
                                xc[po][:, 3 + 512 * ts:3 + 512 * (ts + 1)], pt[:, 0:512],
                                AF.Identity),
                                waits=[("sT", tIP), ("sV", prevheads)], inc="sS")
                        else:
                            p = po - DP
                            b = ck % 2
                            op("scalar", lambda e, ts=ts, pt=pt, b=b: e.activation(
                                zb[b][:, 512 * ts:512 * (ts + 1)], pt[:, 0:512], AF.Silu),
                                waits=[("sT", tIP), ("sLD", lastz[b])], inc="sS")
                            if ts == CK // 512 - 1:
                                sZ = C["sS"]
                                lastz[b] = dma(
                                    "sync", D["z_sp"].ap()[128 * p:128 * (p + 1),
                                                           CK * ck:CK * (ck + 1)],
                                    zb[b][:], waits=[("sS", sZ)])
                    if po == HP - 1:
                        lastx = C["sT"]
                sIP = C["sS"]
                for p in range(DP):
                    b = (ck + p) % 2
                    for ts in range(CK // 512):
                        for kk in range(4):
                            op("tensor", lambda e, p=p, ts=ts, kk=kk: e.matmul(
                                self.p2[:, 0:512],
                                self.DGT[:, (p * 4 + kk) * 128:(p * 4 + kk + 1) * 128],
                                xc[p][:, 512 * ts + kk: 512 * ts + kk + 512],
                                start=(kk == 0), stop=(kk == 3), skip_group_check=True),
                                waits=([("sS", sIP), ("sS", C["sS"])] if kk == 0 else []),
                                inc="sT")
                        tCV = C["sT"]
                        op("scalar", lambda e, p=p, ts=ts, b=b: e.activation(
                            uc[b][:, 512 * ts:512 * (ts + 1)], self.p2[:, 0:512], AF.Silu,
                            bias=self.cbD_sb[:, p:p + 1]),
                            waits=[("sT", tCV), ("sLD", lastu[b]), ("sT", lastuT[b])],
                            inc="sS")
                    lastxc[p] = C["sT"]
                    sCV = C["sS"]
                    lastu[b] = dma("sync",
                                   D["u_sp"].ap()[128 * p:128 * (p + 1), CK * ck:CK * (ck + 1)],
                                   uc[b][:], waits=[("sS", sCV)])
                    for ts in range(CK // 512):
                        op("tensor", lambda e, p=p, ts=ts, b=b: e.matmul(
                            self.pw[0:80, 512 * ts:512 * (ts + 1)], self.xpw_sb[p][:, 0:80],
                            uc[b][:, 512 * ts:512 * (ts + 1)],
                            start=(p == 0), stop=(p == DP - 1), skip_group_check=True),
                            waits=([("sS", sCV), ("sS", lastxdS)] if p == 0 else
                                   [("sS", sCV)] if ts == 0 else []),
                            inc="sT")
                    lastuT[b] = C["sT"]
                tXD = C["sT"]
                sXD = op("scalar", lambda e: e.activation(
                    xdr[0:80, 0:512], self.pw[0:80, 0:512], AF.Identity),
                    waits=[("sT", tXD)], inc="sS")
                sXD2 = op("scalar", lambda e: e.activation(
                    xdr[0:80, 512:1024], self.pw[0:80, 512:1024], AF.Identity),
                    waits=[("sT", tXD)], inc="sS")
                lastxdS = sXD2
                lastxd = dma("sync", D["xdb_b"].ap()[0:80, CK * ck:CK * (ck + 1)],
                             xdr[0:80, :], waits=[("sS", sXD2), ("sLD", lastxd)])
            op("gpsimd", lambda e: e.collective_compute(
                "AllReduce", ALU.add, replica_groups=G4,
                ins=[D["xdb_b"].ap()], outs=[D["xdb_o"].ap()]),
                waits=[("sLD", lastxd)], inc="sCC", by=1)
            self.flush()

    def phase2c(self):
        nc, P, D = self.nc, self.P, self.D
        op, dma = self.op, self.dma
        C = self.cnt
        ccAR = C["sCC"]
        with ExitStack() as ph:
            sb = self.sb
            dtlr = sb(ph, "dtlr", [64, CK], BF16)
            bcn = [sb(ph, f"bcn{i}", [1, 2 * CK], BF16) for i in range(2)]
            uch = [sb(ph, f"uch{p}", [128, CK], BF16) for p in range(DP)]
            zch = [sb(ph, f"zch{i}", [128, CK], BF16) for i in range(2)]
            delta = [sb(ph, f"dlt{p}", [128, CK], BF16) for p in range(DP)]
            du = [sb(ph, f"du{p}", [128, CK], BF16) for p in range(DP)]
            sa = sb(ph, "sa", [128, CK], F32)
            sr = sb(ph, "sr", [128, CK], F32)
            Bb = [sb(ph, f"Bb{i}", [128, CK], BF16) for i in range(2)]
            Cb = [sb(ph, f"Cb{i}", [128, CK], F32) for i in range(2)]
            dab = [sb(ph, f"dab{i}", [128, CK], F32) for i in range(2)]
            Xb = [sb(ph, f"Xsc{i}", [128, CK], BF16) for i in range(2)]
            hb = [sb(ph, f"hsc{i}", [128, CK], F32) for i in range(2)]
            tmp = sb(ph, "ytmp", [128, CK], F32)
            yac = [sb(ph, f"yac{p}", [128, CK], F32) for p in range(DP)]
            self.begin()
            self.fence(("sLD", "sGD"))
            lastdt = 0
            lastBC = [0, 0]
            lastCC = [0, 0]
            lastDA = [0, 0]
            lastX = [0, 0]
            lastH = [0, 0]
            lastld = 0
            lastvch = 0
            lastsch = 0
            lastz2 = [0, 0]
            lastbcn = [0, 0]
            carrycnt = {}
            for ck in range(NCK):
                dD = dma("gpsimd", dtlr[:], D["xdb_o"].ap()[0:64, CK * ck:CK * (ck + 1)],
                         waits=[("sCC", ccAR), ("sT", lastld)])

                dU = 0
                for p in range(DP):
                    dU = dma("sync", uch[p][:],
                             D["u_sp"].ap()[128 * p:128 * (p + 1), CK * ck:CK * (ck + 1)],
                             waits=[("sV", lastvch), ("sS", lastsch)])
                vDel = []
                prevVD = 0
                for p in range(DP):
                    for ts in range(CK // 512):
                        op("tensor", lambda e, p=p, ts=ts: e.matmul(
                            self.pw[:, 512 * ts:512 * (ts + 1)], self.dtw_sb[0:64,
                            128 * p:128 * (p + 1)], dtlr[0:64, 512 * ts:512 * (ts + 1)],
                            start=True, stop=True),
                            waits=[("sGD", dD), ("sS", lastdt)] if ts == 0 else [], inc="sT")
                    tDT = C["sT"]
                    s1 = op("scalar", lambda e, p=p: e.activation(
                        sa[:], self.pw[:, 0:CK], AF.Abs, bias=self.dtb_sb[:, p:p + 1]),
                        waits=[("sT", tDT), ("sV", prevVD)], inc="sS")
                    s2 = op("scalar", lambda e, p=p: e.activation(
                        sr[:], self.pw[:, 0:CK], AF.Relu, bias=self.dtb_sb[:, p:p + 1]),
                        inc="sS")
                    lastdt = s2
                    s3 = op("scalar", lambda e: e.activation(sa[:], sa[:], AF.Exp, scale=-1.0),
                            inc="sS")
                    s4 = op("scalar", lambda e: e.activation(sa[:], sa[:], AF.Ln, bias=1.0),
                            inc="sS")
                    vD = op("vector", lambda e, p=p: e.tensor_tensor(
                        delta[p][:], sr[:], sa[:], ALU.add), waits=[("sS", s4)], inc="sV")
                    vDu = op("vector", lambda e, p=p: e.tensor_tensor(
                        du[p][:], delta[p][:], uch[p][:], ALU.mult),
                        waits=[("sLD", dU)], inc="sV")
                    prevVD = vDu
                    vDel.append((vD, vDu))
                prevBCevac = 0
                for n in range(NST):
                    nb = n % 2
                    dB = dma("gpsimd", bcn[nb][0:1, :].rearrange("p (r l) -> p r l", l=CK),
                             bass.AP(D["xdb_o"].ap().tensor, (64 + n) * TB + CK * ck,
                                     [[0, 1], [8 * TB, 2], [1, CK]]),
                             waits=[("sT", lastbcn[nb]), ("sCC", ccAR)])
                    op("tensor", lambda e, nb=nb: e.matmul(
                        self.p2[:, 0:512], self.onesb[0:1, :],
                        bcn[nb][0:1, 0:512], start=True, stop=True),
                        waits=[("sGD", dB), ("sV", lastBC[nb]), ("sV", lastCC[nb]),
                               ("sS", prevBCevac)], inc="sT")
                    op("tensor", lambda e, nb=nb: e.matmul(
                        self.p3[:, 0:512], self.onesb[0:1, :],
                        bcn[nb][0:1, 512:CK], start=True, stop=True),
                        inc="sT")
                    tBC = C["sT"]
                    op("scalar", lambda e, nb=nb: e.activation(
                        Bb[nb][:, 0:512], self.p2[:, 0:512], AF.Identity),
                        waits=[("sT", tBC)], inc="sS")
                    sB = op("scalar", lambda e, nb=nb: e.activation(
                        Bb[nb][:, 512:CK], self.p3[:, 0:512], AF.Identity), inc="sS")
                    op("tensor", lambda e, nb=nb: e.matmul(
                        self.p2[:, 0:512], self.onesb[0:1, :],
                        bcn[nb][0:1, CK:CK + 512], start=True, stop=True),
                        waits=[("sS", sB)], inc="sT")
                    lastbcn[nb] = op("tensor", lambda e, nb=nb: e.matmul(
                        self.p3[:, 0:512], self.onesb[0:1, :],
                        bcn[nb][0:1, CK + 512:2 * CK], start=True, stop=True),
                        inc="sT")
                    tCC = C["sT"]
                    op("scalar", lambda e, nb=nb: e.activation(
                        Cb[nb][:, 0:512], self.p2[:, 0:512], AF.Identity),
                        waits=[("sT", tCC)], inc="sS")
                    sCr2 = op("scalar", lambda e, nb=nb: e.activation(
                        Cb[nb][:, 512:CK], self.p3[:, 0:512], AF.Identity), inc="sS")
                    prevBCevac = sCr2
                    for p in range(DP):
                        pb = (n * DP + p) % 2
                        vD, vDu = vDel[p]
                        sDA = op("scalar", lambda e, n=n, p=p, pb=pb: e.activation(
                            dab[pb][:], delta[p][:], AF.Exp,
                            scale=self.A_sb[:, (p * 8 + n):(p * 8 + n) + 1]),
                            waits=[("sV", vD), ("sV", lastDA[pb])], inc="sS")
                        vX = op("vector", lambda e, p=p, pb=pb, nb=nb: e.tensor_tensor(
                            Xb[pb][:], du[p][:], Bb[nb][:], ALU.mult),
                            waits=[("sV", vDu), ("sS", sB), ("sV", lastX[pb])], inc="sV")
                        lastBC[nb] = vX
                        idx = n * DP + p
                        wv = [("sS", sDA), ("sV", lastH[pb])]
                        if idx in carrycnt:
                            wv.append(("sS", carrycnt[idx]))
                        vSc = op("vector", lambda e, pb=pb, idx=idx: e.tensor_tensor_scan(
                            hb[pb][:], dab[pb][:], Xb[pb][:],
                            self.carry[:, idx:idx + 1], ALU.mult, ALU.add),
                            waits=wv, inc="sV")
                        lastDA[pb] = vSc
                        lastX[pb] = vSc
                        carrycnt[idx] = op("scalar", lambda e, pb=pb, idx=idx: e.activation(
                            self.carry[:, idx:idx + 1], hb[pb][:, CK - 1:CK],
                            AF.Identity), waits=[("sV", vSc)], inc="sS")
                        if n == 0:
                            vY = op("vector", lambda e, p=p, pb=pb, nb=nb:
                                    e.tensor_tensor(
                                        yac[p][:], hb[pb][:], Cb[nb][:], ALU.mult),
                                    waits=[("sS", sCr2)], inc="sV")
                        else:
                            op("vector", lambda e, pb=pb, nb=nb: e.tensor_tensor(
                                tmp[:], hb[pb][:], Cb[nb][:], ALU.mult),
                                waits=[("sS", sCr2)], inc="sV")
                            vY = op("vector", lambda e, p=p: e.tensor_tensor(
                                yac[p][:], yac[p][:], tmp[:], ALU.add),
                                inc="sV")
                        lastH[pb] = vY
                        lastCC[nb] = vY
                tmpfree = C["sV"]
                for p in range(DP):
                    dZ = dma("sync", zch[p % 2][:],
                             D["z_sp"].ap()[128 * p:128 * (p + 1), CK * ck:CK * (ck + 1)],
                             waits=[("sV", lastz2[p % 2])])
                    sUD = op("scalar", lambda e, p=p: e.activation(
                        tmp[:], uch[p][:], AF.Identity,
                        scale=self.cbD_sb[:, 4 + p:5 + p]),
                        waits=[("sV", tmpfree)], inc="sS")
                    tmpfree = op("vector", lambda e, p=p: e.tensor_tensor(
                        yac[p][:], yac[p][:], tmp[:], ALU.add),
                        waits=[("sS", sUD)], inc="sV")
                    lastz2[p % 2] = op("vector", lambda e, p=p, ck=ck: e.tensor_tensor(
                        self.yall[p][:, CK * ck:CK * (ck + 1)],
                        yac[p][:], zch[p % 2][:], ALU.mult),
                        waits=[("sLD", dZ)], inc="sV")
                lastvch = C["sV"]
                lastsch = C["sS"]
                lastld = C["sT"]
            self.flush()

    def phase2d(self):
        nc, P, D = self.nc, self.P, self.D
        op, dma = self.op, self.dma
        C = self.cnt
        with ExitStack() as ph:
            opw = [self.sb(ph, f"opw{p}", [128, 1024], BF16) for p in range(DP)]
            ob = [self.sb(ph, f"opb{i}", [128, TME], BF16) for i in range(2)]
            self.begin()
            self.fence(("sLD", "sGD"))
            if self.DEBUG_TAPS:
                if "y_tap" not in D:
                    D["y_tap"] = nc.dram_tensor("y_tap", [DIN, TB], BF16)
                    P["tap_y"] = nc.declare_dram_parameter("tap_y", [DIN, TB], BF16,
                                                           isOutput=True)
                for p in range(DP):
                    dma("sync", D["y_tap"].ap()[128 * p:128 * (p + 1), :], self.yall[p][:])
            dO = 0
            for p in range(DP):
                dO = dma("gpsimd", opw[p][:], P["opwT"].ap()[128 * p:128 * (p + 1), :])
            lastst = [0, 0]
            lastev = 0
            idx = 0
            for hh in range(2):
                for tb in range(4):
                    for hq in range(4):
                        hb = 4 * hh + hq
                        b = idx % 2
                        idx += 1
                        for ts in range(4):
                            pk = [self.p0, self.p1, self.p2, self.p3][ts]
                            for p in range(DP):
                                op("tensor", lambda e, tb=tb, hb=hb, ts=ts, p=p, pk=pk:
                                   e.matmul(
                                    pk[:, 0:512], opw[p][:, 128 * hb:128 * (hb + 1)],
                                    self.yall[p][:, TME * tb + 512 * ts:TME * tb + 512 * (ts + 1)],
                                    start=(p == 0), stop=(p == DP - 1), skip_group_check=True),
                                    waits=([("sGD", dO), ("sS", lastev)]
                                           if (ts == 0 and p == 0) else []), inc="sT")
                        tMM = C["sT"]
                        for ts in range(4):
                            pk = [self.p0, self.p1, self.p2, self.p3][ts]
                            op("scalar", lambda e, b=b, ts=ts, pk=pk: e.activation(
                                ob[b][:, 512 * ts:512 * (ts + 1)], pk[:, 0:512], AF.Identity),
                                waits=[("sT", tMM), ("sLD", lastst[b])], inc="sS")
                        sEv = C["sS"]
                        lastev = sEv
                        lastst[b] = dma(
                            "sync",
                            D[f"prtH{hh}"].ap()[512 * tb + 128 * hq: 512 * tb + 128 * (hq + 1), :],
                            ob[b][:], waits=[("sS", sEv)])
                op("gpsimd", lambda e, hh=hh: e.collective_compute(
                    "ReduceScatter", ALU.add, replica_groups=G4,
                    ins=[D[f"prtH{hh}"].ap()], outs=[D[f"prt_o{hh}"].ap()]),
                    waits=[("sLD", max(lastst))], inc="sCC", by=1)
            self.flush()

    def phase3_resid(self):
        P, D = self.P, self.D
        op, dma = self.op, self.dma
        C = self.cnt
        ccRS1 = C["sCC"]
        ccRS0 = ccRS1 - 1
        with ExitStack() as ph:
            rb = [self.sb(ph, f"rb{i}", [128, TME], BF16) for i in range(2)]
            self.begin()
            self.fence(("sLD", "sGD"))
            lastadd = [0, 0]
            for hp in range(HP):
                b = hp % 2
                dX = dma("sync", self.XR[hp][:], D["x1_sp"].ap()[128 * hp:128 * (hp + 1), :])
                dR = dma("sync", rb[b][:],
                         D[f"prt_o{hp // 4}"].ap()[128 * (hp % 4):128 * (hp % 4 + 1), :],
                         waits=[("sCC", ccRS1), ("sV", lastadd[b])])
                lastadd[b] = op("vector", lambda e, hp=hp, b=b: e.tensor_tensor(
                    self.XR[hp][:], self.XR[hp][:], rb[b][:], ALU.add),
                    waits=[("sLD", dR)], inc="sV")
            self.flush()

    def phase4_out(self):
        P = self.P
        op, dma = self.op, self.dma
        with ExitStack() as ph:
            obuf = [self.sb(ph, f"ob4{i}", [128, H], F32) for i in range(2)]
            self.begin()
            slots = [self.p0, self.p1]
            slot_ev = [0, 0]
            lastdma = [0, 0]
            for tt in range(16):
                b = tt % 2
                for hp in range(HP):
                    sl = (tt * HP + hp) % 2
                    tT = op("tensor", lambda e, tt=tt, hp=hp, sl=sl: e.transpose(
                        slots[sl][:, 0:128], self.XR[hp][:, 128 * tt:128 * (tt + 1)],
                        self.idf_sb[:]), waits=[("sS", slot_ev[sl])], inc="sT")
                    slot_ev[sl] = op("scalar", lambda e, hp=hp, b=b, sl=sl: e.activation(
                        obuf[b][:, 128 * hp:128 * (hp + 1)], slots[sl][:, 0:128],
                        AF.Identity), waits=[("sT", tT), ("sLD", lastdma[b])], inc="sS")
                lastdma[b] = dma("sync", P["out"].ap()[128 * tt:128 * (tt + 1), :], obuf[b][:],
                                 waits=[("sS", slot_ev[(tt * HP + HP - 1) % 2])])
            fin = self.cnt["sLD"]
            op("sync", lambda e: None, waits=[("sLD", fin)])
            self.flush()


_NC_CACHE = None


def _get_nc():
    global _NC_CACHE
    if _NC_CACHE is None:
        _NC_CACHE = B().build()
    return _NC_CACHE


def kernel(**inp):
    nc = _get_nc()
    xs = np.ascontiguousarray(inp["L0_attn_hidden_states"].reshape(16384, 1024))
    ipw = inp["L0_attn_in_proj_w"]
    cw = inp["L0_attn_conv_w"]
    cb = inp["L0_attn_conv_b"]
    xpw = inp["L0_attn_x_proj_w"]
    dtw = inp["L0_attn_dt_proj_w"]
    dtb = inp["L0_attn_dt_proj_b"]
    alog = inp["L0_attn_A_log"]
    dD = inp["L0_attn_D"]
    opw = inp["L0_attn_out_proj_w"]
    U = inp["L0_attn_U_param"]
    V = inp["L0_attn_V_param"]
    lrs = inp["L0_attn_low_rank_scale"]
    f32 = np.float32
    in_maps = []
    for c in range(8):
        q = c % 4
        dsl = slice(512 * q, 512 * (q + 1))
        zsl = slice(2048 + 512 * q, 2048 + 512 * (q + 1))
        m = {
            "x0": np.ascontiguousarray(xs[2048 * c:2048 * (c + 1)], f32),
            "wg": np.ascontiguousarray(inp["L0_ffn_w_gate"], f32),
            "wu": np.ascontiguousarray(inp["L0_ffn_w_up"], f32),
            "wd": np.ascontiguousarray(inp["L0_ffn_w_down"], f32),
            "clip2": np.ascontiguousarray(
                np.array([[inp["L0_ffn_clip_x"][0], inp["L0_ffn_clip_down"][0]]], f32)),
            "ipwT": np.ascontiguousarray(
                np.concatenate([ipw[dsl], ipw[zsl]], axis=0).T, f32),
            "cw4": np.ascontiguousarray(
                cw[dsl].reshape(4, 128, 4).transpose(1, 0, 2).reshape(128, 16), f32),
            "cbD": np.ascontiguousarray(
                np.concatenate([cb[dsl].reshape(4, 128).T, dD[dsl].reshape(4, 128).T],
                               axis=1), f32),
            "dtb4": np.ascontiguousarray(dtb[dsl].reshape(4, 128).T, f32),
            "alog": np.ascontiguousarray(
                alog[dsl].reshape(4, 128, 8).transpose(1, 0, 2).reshape(128, 32), f32),
            "uT": np.ascontiguousarray(U[dsl].T, f32),
            "vprm": np.ascontiguousarray(V, f32),
            "lrs1": np.ascontiguousarray(np.asarray(lrs, f32).reshape(1, 1)),
            "xpwT": np.ascontiguousarray(xpw[:, dsl].T, f32),
            "dtwT": np.ascontiguousarray(dtw[dsl].T, f32),
            "opwT": np.ascontiguousarray(opw[:, dsl].T, f32),
            "idf": np.eye(128, dtype=f32),
        }
        in_maps.append(m)
    res = run_bass_kernel_spmd(nc, in_maps, core_ids=list(range(8)))
    out = np.concatenate([np.asarray(res.results[c]["out"]) for c in range(8)], axis=0)
    return out.reshape(2, 8192, 1024).astype(np.float32)
